# version 01
# baseline (speedup 1.0000x reference)
import numpy as np
import concourse.bass as bass
import concourse.mybir as mybir
from concourse.bass_utils import run_bass_kernel_spmd

F32 = mybir.dt.float32
BF16 = mybir.dt.bfloat16
AF = mybir.ActivationFunctionType
ALU = mybir.AluOpType
AX = mybir.AxisListType

P = 128
D = 1024
DT = D // P
TOK = 8192
SEQ = 2048
NSEQ = 4
TC = 512
NCH = TOK // TC
NEG = -1.0e30

N_CORES = 8
MOE_MAX_ROUNDS = None
SQKV_LEVEL = 0
SQKV_EPI_TILES = 10**9
SQKV_NO_V = False
SQKV_NO_STORE = False


def _host_consts():
    c = {}
    c["identity"] = np.eye(P, dtype=np.float32)
    R = np.zeros((P, P), dtype=np.float32)
    for m in range(P):
        r = m % 64
        base = m - r
        if r < 32:
            R[base + r + 32, m] = -1.0
        else:
            R[base + r - 32, m] = 1.0
    c["rothalf"] = R
    HS = np.zeros((P, 2), dtype=np.float32)
    HS[0:64, 0] = 1.0
    HS[64:128, 1] = 1.0
    c["headsum2"] = HS
    c["ones_col"] = np.ones((P, 1), dtype=np.float32)
    c["zeros_col"] = np.zeros((D, 1), dtype=np.float32)
    HSEL = np.zeros((P, P), dtype=np.float32)
    HSEL[0:64, 0] = 1.0
    HSEL[64:128, 64] = 1.0
    c["hsel"] = HSEL
    BSEL = np.zeros((P, P), dtype=np.float32)
    BSEL[0, 0:64] = 1.0
    BSEL[64, 64:128] = 1.0
    c["bsel"] = BSEL
    c["ones_row"] = np.ones((1, P), dtype=np.float32)
    i = np.arange(P)[:, None]
    j = np.arange(P)[None, :]
    c["mask_causal"] = np.where(j <= i, 0.0, NEG).astype(np.float32)
    ii = np.arange(P)[:, None]
    jj = np.arange(640)[None, :]
    c["mask_band"] = np.where((jj >= ii) & (jj <= ii + 512), 0.0, NEG).astype(np.float32)
    DIM_HEAD = 64
    WINDOW = 512
    QK_SCALE = 8.0
    n = SEQ
    half = np.arange(0, DIM_HEAD, 2, dtype=np.float64)
    inv_freq = 1.0 / 10000.0 ** (half / DIM_HEAD)
    t = np.arange(n, dtype=np.float64)
    freqs = t[:, None] * inv_freq[None, :]
    freqs = np.concatenate([freqs, freqs], axis=1)
    base = (half + 0.4 * DIM_HEAD) / (1.4 * DIM_HEAD)
    power = (t - n // 2) / float(WINDOW // 2)
    scale = base[None, :] ** power[:, None]
    scale = np.concatenate([scale, scale], axis=1)
    cos = np.cos(freqs)
    sin = np.sin(freqs)
    def fm(a):
        a = np.ascontiguousarray(a.T.astype(np.float32))
        return np.concatenate([a, a], axis=0)
    c["cs_q"] = fm(cos * scale * QK_SCALE)
    c["ss_q"] = fm(sin * scale * QK_SCALE)
    c["cs_k"] = fm(cos / scale)
    c["ss_k"] = fm(sin / scale)
    return c


def _prep_shared(inp):
    A = {}
    ct = lambda a: np.ascontiguousarray(a)
    A["qkvT"] = ct(inp["L0_attn_qkv_w"].T.astype(np.float32))
    A["outT"] = ct(inp["L0_attn_out_w"].T.astype(np.float32))
    A["qkvb"] = ct(inp["L0_attn_qkv_b"].reshape(24, P).T.astype(np.float32))
    A["qkvb_row"] = ct(inp["L0_attn_qkv_b"].reshape(1, 3072).astype(np.float32))
    A["outb"] = ct(inp["L0_attn_out_b"].reshape(8, P).T.astype(np.float32))
    A["wrT"] = ct(inp["L0_ffn_Wr"].T.astype(np.float32))
    A["w1T"] = ct(np.transpose(inp["L0_ffn_W1"], (0, 2, 1)).astype(np.float32))
    A["w3T"] = ct(np.transpose(inp["L0_ffn_W3"], (0, 2, 1)).astype(np.float32))
    A["w2T"] = ct(np.transpose(inp["L0_ffn_W2"], (0, 2, 1)).astype(np.float32))
    for nm, key in [("wql", "L1_attn_Wq_l"), ("wkl", "L1_attn_Wk_l"),
                    ("wvl", "L1_attn_Wv_l"), ("wol", "L1_attn_Wo_l"),
                    ("wqg", "L1_attn_Wq_g"), ("wkg", "L1_attn_Wk_g"),
                    ("wvg", "L1_attn_Wv_g"), ("wog", "L1_attn_Wo_g")]:
        A[nm] = ct(inp[key].astype(np.float32))
    A["wqkv1"] = ct(inp["L1_ffn_wqkv"].astype(np.float32))
    A["wo1"] = ct(inp["L1_ffn_wo"].astype(np.float32))
    w1 = inp["L1_ffn_w1"].astype(np.float32)
    HID = 2730
    HIDP = 2816
    w1p = np.zeros((2, D, 2 * HIDP), dtype=np.float32)
    w1p[:, :, :HID] = w1[:, :, :HID]
    w1p[:, :, HIDP:HIDP + HID] = w1[:, :, HID:]
    A["w1p"] = ct(w1p)
    b1 = inp["L1_ffn_b1"].astype(np.float32)
    b1p = np.zeros((2, 2 * HIDP), dtype=np.float32)
    b1p[:, :HID] = b1[:, :HID]
    b1p[:, HIDP:HIDP + HID] = b1[:, HID:]
    A["b1p"] = ct(np.transpose(b1p.reshape(2, 44, P), (0, 2, 1)))
    w2 = inp["L1_ffn_w2"].astype(np.float32)
    w2p = np.zeros((2, HIDP, D), dtype=np.float32)
    w2p[:, :HID, :] = w2
    A["w2p"] = ct(w2p)
    A["b2p"] = ct(np.transpose(inp["L1_ffn_b2"].astype(np.float32).reshape(2, 8, P), (0, 2, 1)))
    A["qs"] = ct(np.concatenate([inp["L1_ffn_q_scale"]] * 2, axis=1).astype(np.float32)[:, :, None])
    A["ks"] = ct(np.concatenate([inp["L1_ffn_k_scale"]] * 2, axis=1).astype(np.float32)[:, :, None])
    A["lng"] = ct(np.transpose(inp["L1_ffn_ln_g"].astype(np.float32).reshape(2, 8, P), (0, 2, 1)))
    A["lnb"] = ct(np.transpose(inp["L1_ffn_ln_b"].astype(np.float32).reshape(2, 8, P), (0, 2, 1)))
    A.update(_host_consts())
    return A


class Ctx:

    def __init__(self, nc):
        self.nc = nc
        self.din = {}
        self.scratch = {}
        self.dout = {}

    def inp(self, name, shape):
        if name not in self.din:
            self.din[name] = self.nc.dram_tensor(name, list(shape), F32,
                                                 kind="ExternalInput").ap()
        return self.din[name]

    def scr(self, name, shape, dtype=F32):
        if name not in self.scratch:
            self.scratch[name] = self.nc.dram_tensor(name, list(shape), dtype,
                                                     kind="Internal").ap()
        return self.scratch[name]

    def out(self, name, shape, dtype=F32):
        if name not in self.dout:
            self.dout[name] = self.nc.dram_tensor(name, list(shape), dtype,
                                                  kind="ExternalOutput").ap()
        return self.dout[name]


class Sems:

    def __init__(self, nc, tag):
        self.nc = nc
        self.tag = tag
        self.h = {}
        self.c = {}

    def inc(self, instr, key, n=1):
        if key not in self.h:
            self.h[key] = self.nc.alloc_semaphore(f"{self.tag}_{key}")
            self.c[key] = 0
        instr.then_inc(self.h[key], n)
        self.c[key] += n
        return self.c[key]

    def wait(self, eng, key, val=None):
        if key not in self.h:
            self.h[key] = self.nc.alloc_semaphore(f"{self.tag}_{key}")
            self.c[key] = 0
        eng.wait_ge(self.h[key], self.c[key] if val is None else val)


def _fm3(dram2d, kt, tsl):
    K = dram2d.shape[0]
    v = dram2d.rearrange("(kt p) t -> p kt t", p=P)
    return v[:, kt, tsl]


def phase_ln(ctx, tag, xsrc, xn_bf, xn_f32=None, eps=1e-5):
    nc = ctx.nc
    nstores = 1 if xn_f32 is None else 2
    with nc.cleanup_on_exit():
        s = Sems(nc, tag)
        xf = nc.alloc_sbuf_tensor(f"{tag}_xf", [P, 2 * DT * TC], F32)
        xb = nc.alloc_sbuf_tensor(f"{tag}_xb", [P, 2 * DT * TC], BF16)
        x2 = nc.alloc_sbuf_tensor(f"{tag}_x2", [P, DT * TC], BF16)
        tmp = nc.alloc_sbuf_tensor(f"{tag}_tmp", [P, 2 * DT * TC], F32)
        onesb = nc.alloc_sbuf_tensor(f"{tag}_ones", [P, 1], BF16)
        onesf_row = nc.alloc_sbuf_tensor(f"{tag}_onesr", [1, P], F32)
        ssum = nc.alloc_sbuf_tensor(f"{tag}_ssum", [1, TOK], F32)
        ssq = nc.alloc_sbuf_tensor(f"{tag}_ssq", [1, TOK], F32)
        xnout = nc.alloc_sbuf_tensor(f"{tag}_xn", [P, 2 * DT * TC], BF16)
        ps_sum = nc.alloc_psum_tensor(f"{tag}_pss", [1, TC], F32)
        ps_sq = nc.alloc_psum_tensor(f"{tag}_psq", [1, TC], F32)
        ps_a = nc.alloc_psum_tensor(f"{tag}_psa", [P, TC], F32)
        ps_b = nc.alloc_psum_tensor(f"{tag}_psb", [P, TC], F32)

        g, v, sc, t, sy = nc.gpsimd, nc.vector, nc.scalar, nc.tensor, nc.sync

        s.inc(g.memset(onesb[:, :], 1.0), "g")
        s.inc(g.memset(onesf_row[:, :], 1.0), "g")

        for c in range(NCH):
            b = c % 2
            tsl = slice(c * TC, (c + 1) * TC)
            xf_b = xf[:, b * DT * TC:(b + 1) * DT * TC]
            xb_b = xb[:, b * DT * TC:(b + 1) * DT * TC]
            if c >= 2:
                s.wait(g, "t_stats", 2 * (c - 1))
                s.wait(sy, "s_sq", c - 1)
            s.inc(g.dma_start(out=xb_b.rearrange("p (kt t) -> p kt t", kt=DT),
                              in_=_fm3(xsrc, slice(None), tsl)), f"ld{b}", 16)
            s.inc(sy.dma_start(out=xf_b.rearrange("p (kt t) -> p kt t", kt=DT),
                               in_=_fm3(xsrc, slice(None), tsl)), f"ldf{b}", 16)
            s.wait(sc, f"ldf{b}", 16 * (c // 2 + 1))
            if c >= 1:
                s.wait(sc, "t_stats", 2 * c)
            s.inc(sc.activation(x2[:, :], xf_b[:, :], AF.Square), "s_sq")
            s.wait(t, f"ld{b}", 16 * (c // 2 + 1))
            s.wait(t, "g", 2)
            s.wait(t, "s_sq", c + 1)
            if c >= 1:
                s.wait(t, "v_stats", c)
            for kt in range(DT):
                t.matmul(ps_sum[:, :], onesb[:, :], xb_b[:, kt * TC:(kt + 1) * TC],
                         start=(kt == 0), stop=(kt == DT - 1))
            for kt in range(DT):
                mm = t.matmul(ps_sq[:, :], onesb[:, :], x2[:, kt * TC:(kt + 1) * TC],
                              start=(kt == 0), stop=(kt == DT - 1))
            s.inc(mm, "t_stats", 2)
            s.wait(v, "t_stats", 2 * (c + 1))
            v.tensor_copy(ssum[:, tsl], ps_sum[:, :])
            s.inc(v.tensor_copy(ssq[:, tsl], ps_sq[:, :]), "v_stats")

        trow = tmp[0:1, 0:TOK]
        s.wait(v, "v_stats", NCH)
        v.tensor_scalar_mul(ssum[:, :], ssum[:, :], 1.0 / D)
        v.drain()
        v.tensor_scalar_mul(ssq[:, :], ssq[:, :], 1.0 / D)
        v.drain()
        v.tensor_mul(trow, ssum[:, :], ssum[:, :])
        v.drain()
        v.tensor_sub(ssq[:, :], ssq[:, :], trow)
        v.drain()
        s.inc(v.tensor_scalar_add(ssq[:, :], ssq[:, :], eps), "v1")
        s.wait(sc, "v1", 1)
        s.inc(sc.activation(ssq[:, :], ssq[:, :], AF.Sqrt), "s1")
        s.wait(v, "s1", 1)
        v.reciprocal(ssq[:, :], ssq[:, :])
        v.drain()
        s.inc(v.scalar_tensor_tensor(ssum[:, :], ssum[:, :], -1.0, ssq[:, :],
                                     ALU.mult, ALU.mult), "v2")
        arow, brow = ssq, ssum

        for c in range(NCH):
            b = c % 2
            tsl = slice(c * TC, (c + 1) * TC)
            xf_b = xf[:, b * DT * TC:(b + 1) * DT * TC]
            tmp_b = tmp[:, b * DT * TC:(b + 1) * DT * TC]
            xn_b = xnout[:, b * DT * TC:(b + 1) * DT * TC]
            s.wait(sy, "s_sq", NCH)
            if c >= 2:
                s.wait(sy, "v_app", c - 1)
            s.inc(sy.dma_start(out=xf_b.rearrange("p (kt t) -> p kt t", kt=DT),
                               in_=_fm3(xsrc, slice(None), tsl)), f"ldf{b}", 16)
            s.wait(t, "v2", 1)
            if c >= 1:
                s.wait(t, "v_app", c)
            t.matmul(ps_a[:, :], onesf_row[:, :], arow[:, tsl], start=True, stop=True)
            s.inc(t.matmul(ps_b[:, :], onesf_row[:, :], brow[:, tsl], start=True, stop=True), "t_ab")
            s.wait(v, f"ldf{b}", 16 * (NCH // 2 + c // 2 + 1))
            s.wait(v, "t_ab", c + 1)
            if c >= 2:
                s.wait(v, f"st{b}", 16 * nstores * (c // 2))
            for kt in range(DT):
                v.tensor_mul(tmp_b[:, kt * TC:(kt + 1) * TC],
                             xf_b[:, kt * TC:(kt + 1) * TC], ps_a[:, :])
            v.drain()
            for kt in range(DT):
                v.tensor_add(tmp_b[:, kt * TC:(kt + 1) * TC],
                             tmp_b[:, kt * TC:(kt + 1) * TC], ps_b[:, :])
            v.drain()
            for kt in range(DT):
                v.tensor_copy(xn_b[:, kt * TC:(kt + 1) * TC],
                              tmp_b[:, kt * TC:(kt + 1) * TC])
            s.inc(v.drain(), "v_app")
            s.wait(g, "v_app", c + 1)
            s.inc(g.dma_start(out=_fm3(xn_bf, slice(None), tsl),
                              in_=xn_b.rearrange("p (kt t) -> p kt t", kt=DT)), f"st{b}", 16)
            if xn_f32 is not None:
                s.inc(g.dma_start(out=_fm3(xn_f32, slice(None), tsl),
                                  in_=tmp_b.rearrange("p (kt t) -> p kt t", kt=DT)), f"st{b}", 16)
        s.wait(g, "st0", 16 * nstores * (NCH // 2))
        s.wait(g, "st1", 16 * nstores * (NCH // 2))
        nc.all_engine_barrier()


class PsumPool:

    def __init__(self, nc, tag, n, shape, dtype=F32):
        self.tiles = [nc.alloc_psum_tensor(f"{tag}_ps{i}", list(shape), dtype)
                      for i in range(n)]
        self.gate = [None] * n
        self.i = 0

    def get(self, s, eng):
        i = self.i
        self.i = (self.i + 1) % len(self.tiles)
        if self.gate[i] is not None:
            for kv in self.gate[i]:
                s.wait(eng, *kv)
        tile = self.tiles[i]

        def setgate(*pairs):
            self.gate[i] = [pairs] if pairs and not isinstance(pairs[0], tuple)                 else list(pairs)
        return tile, setgate


def phase_gemm(ctx, tag, act_src, Kdim, specs):
    nc = ctx.nc
    KT = Kdim // P
    g, v, sc, t, sy = nc.gpsimd, nc.vector, nc.scalar, nc.tensor, nc.sync
    with nc.cleanup_on_exit():
        s = Sems(nc, tag)
        nw = 0
        for sp in specs:
            N = sp["N"]
            sp["wsb"] = nc.alloc_sbuf_tensor(f"{tag}_w{nw}", [P, KT * N], BF16)
            s.inc(g.dma_start(out=sp["wsb"][:, :].rearrange("p (kt n) -> p kt n", kt=KT),
                              in_=sp["w"].rearrange("(kt p) n -> p kt n", p=P)), "ldw", 16)
            if sp["kind"] == "fm" and sp.get("bias") is not None:
                NT = N // P
                sp["bsb"] = nc.alloc_sbuf_tensor(f"{tag}_b{nw}", [P, NT], F32)
                s.inc(g.dma_start(out=sp["bsb"][:, :], in_=sp["bias"][:, :]), "ldw", 16)
            if sp["kind"] == "tok" and sp.get("bias_row") is not None:
                sp["brsb"] = nc.alloc_sbuf_tensor(f"{tag}_br{nw}", [1, N], F32)
                s.inc(g.dma_start(out=sp["brsb"][:, :], in_=sp["bias_row"]), "ldw", 16)
            nw += 1
        LDW = s.c["ldw"]
        ab = nc.alloc_sbuf_tensor(f"{tag}_act", [P, 2 * KT * TC], BF16)
        onesf_row = nc.alloc_sbuf_tensor(f"{tag}_onesr", [1, P], F32)
        s.inc(g.memset(onesf_row[:, :], 1.0), "g")
        for i, sp in enumerate(specs):
            N = sp["N"]
            if sp["kind"] == "fm":
                NT = N // P
                dt_o = F32 if any(d == "f32" for _, d in sp["outs"]) else BF16
                sp["osb"] = nc.alloc_sbuf_tensor(f"{tag}_o{i}", [P, 2 * NT * TC], dt_o)
                if any(d == "bf16" for _, d in sp["outs"]) and dt_o == F32:
                    sp["osb_bf"] = nc.alloc_sbuf_tensor(f"{tag}_ob{i}", [P, 2 * NT * TC], BF16)
                if sp.get("res") is not None:
                    sp["rsb"] = nc.alloc_sbuf_tensor(f"{tag}_r{i}", [P, 2 * NT * TC], F32)
            else:
                sp["osb"] = nc.alloc_sbuf_tensor(f"{tag}_o{i}", [P, 2 * (TC // P) * N], BF16)
                if sp.get("bias_row") is not None:
                    sp["vbias"] = nc.alloc_sbuf_tensor(f"{tag}_vb{i}", [P, N], F32)
        pool_fm = PsumPool(nc, tag + "f", 4, [P, TC])
        pool_tok = PsumPool(nc, tag + "t", 2, [P, 1024]) if any(
            sp["kind"] == "tok" for sp in specs) else None

        for sp in specs:
            if sp["kind"] == "tok" and sp.get("bias_row") is not None:
                N = sp["N"]
                ps, setg = pool_tok.get(s, t)
                s.wait(t, "ldw", LDW)
                s.wait(t, "g", 1)
                for h in range(N // 512):
                    mm = t.matmul(ps[:, h * 512:(h + 1) * 512], onesf_row[:, :],
                                  sp["brsb"][0:1, h * 512:(h + 1) * 512],
                                  start=True, stop=True)
                s.inc(mm, "tb")
                s.wait(v, "tb", s.c["tb"])
                s.inc(v.tensor_copy(sp["vbias"][:, :], ps[:, :]), "vb")
                setg("vb", s.c["vb"])

        per_chunk_t = sum((sp["N"] // P if sp["kind"] == "fm" else TC // P)
                          for sp in specs)
        nst_chunk = sum((len(sp["outs"]) if sp["kind"] == "fm" else 1)
                        for sp in specs)
        for c in range(NCH):
            b = c % 2
            tsl = slice(c * TC, (c + 1) * TC)
            ab_b = ab[:, b * KT * TC:(b + 1) * KT * TC]
            if c >= 2:
                s.wait(g, "t_c", (c - 1) * per_chunk_t)
            s.inc(g.dma_start(out=ab_b.rearrange("p (kt t) -> p kt t", kt=KT),
                              in_=_fm3(act_src, slice(None), tsl)), f"lda{b}", 16)
            nlda = 16 * (c // 2 + 1)
            st_gate = 16 * nst_chunk * (c // 2) if c >= 2 else 0

            for i, sp in enumerate(specs):
                N = sp["N"]
                if sp["kind"] == "fm":
                    NT = N // P
                    osb_b = sp["osb"][:, b * NT * TC:(b + 1) * NT * TC]
                    if sp.get("res") is not None:
                        rsb_b = sp["rsb"][:, b * NT * TC:(b + 1) * NT * TC]
                        if c >= 2:
                            s.wait(sy, f"v_res{i}", c - 1)
                        s.inc(sy.dma_start(
                            out=rsb_b.rearrange("p (nt t) -> p nt t", nt=NT),
                            in_=_fm3(sp["res"], slice(None), tsl)), f"ldr{i}{b}", 16)
                    for j in range(NT):
                        ps, setg = pool_fm.get(s, t)
                        s.wait(t, f"lda{b}", nlda)
                        s.wait(t, "ldw", LDW)
                        for kt in range(KT):
                            mm = t.matmul(ps[:, :],
                                          sp["wsb"][:, kt * N + j * P: kt * N + j * P + P],
                                          ab_b[:, kt * TC:(kt + 1) * TC],
                                          start=(kt == 0), stop=(kt == KT - 1))
                        s.inc(mm, "t_c")
                        s.wait(sc, "t_c", s.c["t_c"])
                        if c >= 2 and j == 0:
                            s.wait(sc, f"st{b}", st_gate)
                        dst = osb_b[:, j * TC:(j + 1) * TC]
                        if sp.get("bias") is not None:
                            ev = sc.add(dst, ps[:, :], sp["bsb"][:, j:j + 1])
                        elif sp.get("act_fn") is not None:
                            ev = sc.activation(dst, ps[:, :], sp["act_fn"])
                        else:
                            ev = sc.copy(dst, ps[:, :])
                        s.inc(ev, f"s_ev{i}")
                        setg(f"s_ev{i}", s.c[f"s_ev{i}"])
                    if sp.get("res") is not None:
                        s.wait(v, f"s_ev{i}", s.c[f"s_ev{i}"])
                        s.wait(v, f"ldr{i}{b}", 16 * (c // 2 + 1))
                        for j in range(NT):
                            vv = v.tensor_add(osb_b[:, j * TC:(j + 1) * TC],
                                              osb_b[:, j * TC:(j + 1) * TC],
                                              rsb_b[:, j * TC:(j + 1) * TC])
                        s.inc(v.drain(), f"v_res{i}")
                    if "osb_bf" in sp:
                        src_sem = f"v_res{i}" if sp.get("res") is not None else f"s_ev{i}"
                        s.wait(v, src_sem, s.c[src_sem])
                        for j in range(NT):
                            v.tensor_copy(
                                sp["osb_bf"][:, b * NT * TC + j * TC:
                                             b * NT * TC + (j + 1) * TC],
                                osb_b[:, j * TC:(j + 1) * TC])
                        s.inc(v.drain(), f"v_bf{i}")
                else:
                    osb_b = sp["osb"][:, b * (TC // P) * N:(b + 1) * (TC // P) * N]
                    for tt in range(TC // P):
                        ps, setg = pool_tok.get(s, t)
                        s.wait(t, f"lda{b}", nlda)
                        s.wait(t, "ldw", LDW)
                        for h in range(N // 512):
                            for kt in range(KT):
                                mm = t.matmul(
                                    ps[:, h * 512:(h + 1) * 512],
                                    ab_b[:, kt * TC + tt * P: kt * TC + (tt + 1) * P],
                                    sp["wsb"][:, kt * N + h * 512: kt * N + (h + 1) * 512],
                                    start=(kt == 0), stop=(kt == KT - 1))
                        s.inc(mm, "t_c")
                        s.wait(v, "t_c", s.c["t_c"])
                        if c == 0 and tt == 0 and "vb" in s.c:
                            s.wait(v, "vb", s.c["vb"])
                        if c >= 2 and tt == 0:
                            s.wait(v, f"st{b}", st_gate)
                        dst = osb_b[:, tt * N:(tt + 1) * N]
                        if sp.get("bias_row") is not None:
                            ev = v.tensor_add(dst, ps[:, :], sp["vbias"][:, :])
                        else:
                            ev = v.tensor_copy(dst, ps[:, :])
                        s.inc(ev, f"v_ev{i}")
                        setg(f"v_ev{i}", s.c[f"v_ev{i}"])

            for i, sp in enumerate(specs):
                N = sp["N"]
                if sp["kind"] == "fm":
                    NT = N // P
                    osb_b = sp["osb"][:, b * NT * TC:(b + 1) * NT * TC]
                    done_key = (f"v_bf{i}" if "osb_bf" in sp
                                else (f"v_res{i}" if sp.get("res") is not None
                                      else f"s_ev{i}"))
                    s.wait(g, done_key, s.c[done_key])
                    for od, odt in sp["outs"]:
                        src = (sp["osb_bf"][:, b * NT * TC:(b + 1) * NT * TC]
                               if (odt == "bf16" and "osb_bf" in sp) else osb_b)
                        s.inc(g.dma_start(
                            out=_fm3(od, slice(None), tsl),
                            in_=src.rearrange("p (nt t) -> p nt t", nt=NT)),
                            f"st{b}", 16)
                else:
                    osb_b = sp["osb"][:, b * (TC // P) * N:(b + 1) * (TC // P) * N]
                    s.wait(g, f"v_ev{i}", s.c[f"v_ev{i}"])
                    s.inc(g.dma_start(
                        out=sp["out"][tsl, :].rearrange("(tt p) n -> p tt n", p=P),
                        in_=osb_b.rearrange("p (tt n) -> p tt n", tt=TC // P)),
                        f"st{b}", 16)
        for b in range(2):
            if f"st{b}" in s.c:
                s.wait(g, f"st{b}", s.c[f"st{b}"])
        nc.all_engine_barrier()


def phase_attn0(ctx, tag, qk0, v0t, outT_d, outb_d, s0):
    nc = ctx.nc
    g, v, sc, t, sy = nc.gpsimd, nc.vector, nc.scalar, nc.tensor, nc.sync
    NH = 16
    QT = SEQ // P
    with nc.cleanup_on_exit():
        s = Sems(nc, tag)
        qsb = nc.alloc_sbuf_tensor(f"{tag}_q", [P, DT * SEQ], BF16)
        ksb = nc.alloc_sbuf_tensor(f"{tag}_k", [P, DT * SEQ], BF16)
        vsb = nc.alloc_sbuf_tensor(f"{tag}_v", [P, (SEQ // P) * D], BF16)
        ctx_sb = nc.alloc_sbuf_tensor(f"{tag}_ctx", [P, DT * SEQ], BF16)
        psb = nc.alloc_sbuf_tensor(f"{tag}_p", [P, SEQ], BF16)
        ptsb = nc.alloc_sbuf_tensor(f"{tag}_pt", [P, SEQ], BF16)
        maskc = nc.alloc_sbuf_tensor(f"{tag}_mask", [P, P], F32)
        identb = nc.alloc_sbuf_tensor(f"{tag}_id", [P, P], BF16)
        negmax = nc.alloc_sbuf_tensor(f"{tag}_nm", [P, 1], F32)
        denom = nc.alloc_sbuf_tensor(f"{tag}_dn", [P, 1], F32)
        recip = nc.alloc_sbuf_tensor(f"{tag}_rc", [P, 1], F32)
        wsb = nc.alloc_sbuf_tensor(f"{tag}_w", [P, DT * D], BF16)
        bsb = nc.alloc_sbuf_tensor(f"{tag}_b", [P, DT], F32)
        osb = nc.alloc_sbuf_tensor(f"{tag}_o", [P, 2 * DT * TC], F32)
        S_ps = nc.alloc_psum_tensor(f"{tag}_S", [P, SEQ], F32)
        PT_ps = nc.alloc_psum_tensor(f"{tag}_PT", [P, SEQ], BF16)
        cx_ps = nc.alloc_psum_tensor(f"{tag}_cx", [P, P], F32)

        s.inc(g.dma_start(out=maskc[:, :], in_=ctx.inp("mask_causal", [P, P])), "ldc", 16)
        s.inc(g.dma_start(out=identb[:, :], in_=ctx.inp("identity", [P, P])), "ldc", 16)
        s.inc(g.dma_start(out=wsb[:, :].rearrange("p (kt n) -> p kt n", kt=DT),
                          in_=outT_d.rearrange("(kt p) n -> p kt n", p=P)), "ldc", 16)
        s.inc(g.dma_start(out=bsb[:, :], in_=outb_d[:, :]), "ldc", 16)
        LDC = s.c["ldc"]

        it = 0
        nst = 0
        for sq in range(NSEQ):
            ssl = slice(sq * SEQ, (sq + 1) * SEQ)
            if sq >= 1:
                s.wait(g, "st0", 16 * ((nst + 1) // 2))
                s.wait(g, "st1", 16 * (nst // 2))
            s.inc(g.dma_start(out=qsb[:, :].rearrange("p (kt t) -> p kt t", kt=DT),
                              in_=qk0.rearrange("(kt p) t -> p kt t", p=P)[:, 0:DT, ssl]),
                  f"ldq{sq % 2}", 16)
            s.inc(g.dma_start(out=ksb[:, :].rearrange("p (kt t) -> p kt t", kt=DT),
                              in_=qk0.rearrange("(kt p) t -> p kt t", p=P)[:, DT:2 * DT, ssl]),
                  f"ldq{sq % 2}", 16)
            s.inc(g.dma_start(out=vsb[:, :].rearrange("p (tt n) -> p tt n", tt=SEQ // P),
                              in_=v0t[ssl, :].rearrange("(tt p) n -> p tt n", p=P)),
                  f"ldq{sq % 2}", 16)
            nldq = 48 * (sq // 2 + 1)
            s.wait(v, f"ldq{sq % 2}", nldq)
            s.inc(v.tensor_scalar_mul(qsb[:, :], qsb[:, :], 0.125), "v_qs")

            s.wait(t, "s_op", s.c.get("s_op", 0))
            for h in range(NH):
                hb = (h % 2) * 64
                kt_h = h // 2
                qoff = kt_h * SEQ
                for qt in range(QT):
                    kr = (qt + 1) * P
                    q_t = qsb[hb:hb + 64, qoff + qt * P: qoff + (qt + 1) * P]
                    k_r = ksb[hb:hb + 64, qoff: qoff + kr]
                    s.wait(t, f"ldq{sq % 2}", nldq)
                    s.wait(t, "v_qs", sq + 1)
                    s.wait(t, "ldc", LDC)
                    if it >= 1:
                        s.wait(t, "s_e", it)
                    for kk in range(qt + 1):
                        mm = t.matmul(S_ps[:, kk * P:(kk + 1) * P], q_t,
                                      k_r[:, kk * P:(kk + 1) * P],
                                      start=True, stop=True)
                    s.inc(mm, "t_s")
                    s.wait(v, "t_s", it + 1)
                    v.tensor_add(S_ps[:, qt * P: qt * P + P],
                                 S_ps[:, qt * P: qt * P + P], maskc[:, :])
                    v.drain()
                    s.inc(v.reduce_max(negmax[:, 0:1], S_ps[:, 0:kr], axis=AX.X,
                                       negate=True), "v_m")
                    s.wait(sc, "v_m", it + 1)
                    if it >= 1:
                        s.wait(sc, "t_pt", it)
                    s.inc(sc.activation(psb[:, 0:kr], S_ps[:, 0:kr], AF.Exp,
                                        bias=negmax[:, 0:1],
                                        accum_out=denom[:, 0:1]), "s_e")
                    s.wait(v, "s_e", it + 1)
                    v.reciprocal(recip[:, 0:1], denom[:, 0:1])
                    v.drain()
                    s.inc(v.tensor_scalar_mul(psb[:, 0:kr], psb[:, 0:kr],
                                              recip[:, 0:1]), "v_r")
                    s.wait(t, "v_r", it + 1)
                    if it >= 1:
                        s.wait(t, "s_pt", it)
                    for kk in range(qt + 1):
                        tp = t.transpose(PT_ps[:, kk * P:(kk + 1) * P],
                                         psb[:, kk * P:(kk + 1) * P], identb[:, :])
                    s.inc(tp, "t_pt")
                    s.wait(sc, "t_pt", it + 1)
                    if it >= 1:
                        s.wait(sc, "t_av", it)
                    s.inc(sc.copy(ptsb[:, 0:kr], PT_ps[:, 0:kr]), "s_pt")
                    s.wait(t, "s_pt", it + 1)
                    if it >= 1:
                        s.wait(t, "s_cx", it)
                    for kk in range(qt + 1):
                        mm = t.matmul(cx_ps[hb:hb + 64, :],
                                      vsb[:, kk * D + h * 64: kk * D + h * 64 + 64],
                                      ptsb[:, kk * P:(kk + 1) * P],
                                      start=(kk == 0), stop=(kk == qt),
                                      tile_position=(0, hb))
                    s.inc(mm, "t_av")
                    s.wait(sc, "t_av", it + 1)
                    s.inc(sc.copy(ctx_sb[hb:hb + 64, qoff + qt * P: qoff + (qt + 1) * P],
                                  cx_ps[hb:hb + 64, :]), "s_cx")
                    it += 1

            for c4 in range(SEQ // TC):
                b = c4 % 2
                osb_b = osb[:, b * DT * TC:(b + 1) * DT * TC]
                csl = slice(c4 * TC, (c4 + 1) * TC)
                for j in range(DT):
                    op_n = s.c.get("s_op", 0)
                    ps_o = S_ps[:, (op_n % 2) * TC:(op_n % 2 + 1) * TC]
                    s.wait(t, "s_cx", it)
                    if op_n >= 2:
                        s.wait(t, "s_op", op_n - 1)
                    for kt in range(DT):
                        mm = t.matmul(ps_o, wsb[:, kt * D + j * P: kt * D + j * P + P],
                                      ctx_sb[:, kt * SEQ + c4 * TC: kt * SEQ + (c4 + 1) * TC],
                                      start=(kt == 0), stop=(kt == DT - 1))
                    s.inc(mm, "t_op")
                    s.wait(sc, "t_op", s.c["t_op"])
                    if nst >= 2 and j == 0:
                        s.wait(sc, f"st{nst % 2}", 16 * (nst // 2))
                    s.inc(sc.add(osb_b[:, j * TC:(j + 1) * TC], ps_o, bsb[:, j:j + 1]),
                          "s_op")
                s.wait(sy, "s_op", s.c["s_op"])
                s.inc(sy.dma_start(out=_fm3(s0, slice(None), slice(sq * SEQ + c4 * TC,
                                                                   sq * SEQ + (c4 + 1) * TC)),
                                   in_=osb_b.rearrange("p (nt t) -> p nt t", nt=DT)),
                      f"st{nst % 2}", 16)
                nst += 1
        s.wait(sy, "st0", 16 * ((nst + 1) // 2))
        s.wait(sy, "st1", 16 * (nst // 2))
        nc.all_engine_barrier()


def phase_router(ctx, tag, xn0f, wrT_d, gates_d):
    nc = ctx.nc
    g, v, sc, t, sy = nc.gpsimd, nc.vector, nc.scalar, nc.tensor, nc.sync
    NT_TOK = TOK // P
    with nc.cleanup_on_exit():
        s = Sems(nc, tag)
        xf = nc.alloc_sbuf_tensor(f"{tag}_xf", [P, 2 * DT * TC], F32)
        wr = nc.alloc_sbuf_tensor(f"{tag}_wr", [P, DT * 8], F32)
        idf = nc.alloc_sbuf_tensor(f"{tag}_id", [P, P], F32)
        probs = nc.alloc_sbuf_tensor(f"{tag}_pr", [P, 8], F32)
        gates_t = nc.alloc_sbuf_tensor(f"{tag}_gt", [P, 8], F32)
        nm8 = nc.alloc_sbuf_tensor(f"{tag}_nm", [P, 1], F32)
        dn8 = nc.alloc_sbuf_tensor(f"{tag}_dn", [P, 1], F32)
        m7 = nc.alloc_sbuf_tensor(f"{tag}_m7", [P, 1], F32)
        rs = nc.alloc_sbuf_tensor(f"{tag}_rs", [P, 1], F32)
        gsb = nc.alloc_sbuf_tensor(f"{tag}_g", [8, TOK], F32)
        pool_lg = PsumPool(nc, tag + "l", 2, [P, 8])
        pool_tr = PsumPool(nc, tag + "t", 2, [8, P])

        s.inc(g.dma_start(out=wr[:, :].rearrange("p (kt e) -> p kt e", kt=DT),
                          in_=wrT_d.rearrange("(kt p) e -> p kt e", p=P)), "ldc", 16)
        s.inc(g.dma_start(out=idf[:, :], in_=ctx.inp("identity", [P, P])), "ldc", 16)

        for tt in range(NT_TOK):
            c, b = tt // 4, (tt // 4) % 2
            if tt % 4 == 0:
                if c >= 2:
                    s.wait(sy, "t_lg", 4 * (c - 1))
                s.inc(sy.dma_start(
                    out=xf[:, b * DT * TC:(b + 1) * DT * TC].rearrange(
                        "p (kt t) -> p kt t", kt=DT),
                    in_=_fm3(xn0f, slice(None), slice(c * TC, (c + 1) * TC))),
                    f"mrd{b}", 16)
            xt_off = b * DT * TC + (tt % 4) * P
            lg, setg_lg = pool_lg.get(s, t)
            s.wait(t, f"mrd{b}", 16 * (c // 2 + 1))
            s.wait(t, "ldc", 32)
            for kt in range(DT):
                mm = t.matmul(lg[:, :], xf[:, xt_off + kt * TC: xt_off + kt * TC + P],
                              wr[:, kt * 8:(kt + 1) * 8],
                              start=(kt == 0), stop=(kt == DT - 1))
            s.inc(mm, "t_lg")
            s.wait(v, "t_lg", tt + 1)
            if tt >= 1:
                s.wait(v, "s_e8", tt)
            s.inc(v.reduce_max(nm8[:, 0:1], lg[:, :], axis=AX.X, negate=True), "v_m8")
            s.wait(sc, "v_m8", tt + 1)
            if tt >= 1:
                s.wait(sc, "v_g", tt)
            s.inc(sc.activation(probs[:, :], lg[:, :], AF.Exp, bias=nm8[:, 0:1],
                                accum_out=dn8[:, 0:1]), "s_e8")
            setg_lg("s_e8", s.c["s_e8"])
            s.wait(v, "s_e8", tt + 1)
            if tt >= 1:
                s.wait(v, "t_tr", tt)
            v.reciprocal(rs[:, 0:1], dn8[:, 0:1])
            v.drain()
            v.tensor_scalar_mul(probs[:, :], probs[:, :], rs[:, 0:1])
            v.drain()
            v.reduce_max(m7[:, 0:1], probs[:, 1:8], axis=AX.X)
            v.drain()
            v.tensor_scalar(gates_t[:, 1:8], probs[:, 1:8], m7[:, 0:1], m7[:, 0:1],
                            ALU.is_equal, ALU.mult)
            v.drain()
            v.tensor_add(rs[:, 0:1], probs[:, 0:1], m7[:, 0:1])
            v.drain()
            v.tensor_scalar_max(rs[:, 0:1], rs[:, 0:1], 1e-9)
            v.drain()
            v.reciprocal(rs[:, 0:1], rs[:, 0:1])
            v.drain()
            v.tensor_scalar_mul(gates_t[:, 1:8], gates_t[:, 1:8], rs[:, 0:1])
            v.drain()
            s.inc(v.tensor_scalar(gates_t[:, 0:1], probs[:, 0:1], rs[:, 0:1], None,
                                  ALU.mult), "v_g")
            trp, setg_tr = pool_tr.get(s, t)
            s.wait(t, "v_g", tt + 1)
            s.inc(t.transpose(trp[:, :], gates_t[:, :], idf[:, :]), "t_tr")
            s.wait(sc, "t_tr", tt + 1)
            s.inc(sc.copy(gsb[0:8, tt * P:(tt + 1) * P], trp[:, :]), "s_gt")
            setg_tr("s_gt", s.c["s_gt"])
        s.wait(g, "s_gt", NT_TOK)
        s.inc(g.dma_start(out=gates_d[:, :], in_=gsb[:, :]), "st", 16)
        s.wait(g, "st", 16)
        nc.all_engine_barrier()


def phase_moe(ctx, tag, xn0, w1T_d, w3T_d, w2T_d, gates_d, g0):
    nc = ctx.nc
    g, v, sc, t, sy = nc.gpsimd, nc.vector, nc.scalar, nc.tensor, nc.sync
    FC = 2048
    KT2 = FC // P
    NE = 8
    NROUND = NE * (4096 // FC)
    if MOE_MAX_ROUNDS is not None:
        NROUND = min(NROUND, MOE_MAX_ROUNDS)
    with nc.cleanup_on_exit():
        s = Sems(nc, tag)
        w1sb = nc.alloc_sbuf_tensor(f"{tag}_w1", [P, DT * FC], BF16)
        w3sb = nc.alloc_sbuf_tensor(f"{tag}_w3", [P, DT * FC], BF16)
        w2sb = nc.alloc_sbuf_tensor(f"{tag}_w2", [P, KT2 * D], BF16)
        xb = nc.alloc_sbuf_tensor(f"{tag}_x", [P, 2 * DT * TC], BF16)
        hsb = nc.alloc_sbuf_tensor(f"{tag}_h", [P, KT2 * TC], BF16)
        s1a = nc.alloc_sbuf_tensor(f"{tag}_s1a", [P, TC], BF16)
        s1b = nc.alloc_sbuf_tensor(f"{tag}_s1b", [P, TC], BF16)
        grow = nc.alloc_sbuf_tensor(f"{tag}_gr", [1, TOK], F32)
        onesr = nc.alloc_sbuf_tensor(f"{tag}_or", [1, P], F32)
        osb = nc.alloc_sbuf_tensor(f"{tag}_o", [P, 2 * DT * TC], F32)
        pp1 = PsumPool(nc, tag + "1", 2, [P, TC])
        pp3 = PsumPool(nc, tag + "3", 2, [P, TC])
        ppb = PsumPool(nc, tag + "b", 2, [P, TC])
        ppo = PsumPool(nc, tag + "o", 2, [P, TC])
        s.inc(g.memset(onesr[:, :], 1.0), "g")

        gidx = 0
        for r in range(NROUND):
            e, half = r // 2, r % 2
            fsl = slice(half * FC, (half + 1) * FC)
            if r >= 1:
                s.wait(g, "t_o", 8 * 16 * r)
            s.inc(g.dma_start(out=w1sb[:, :].rearrange("p (kt n) -> p kt n", kt=DT),
                              in_=w1T_d[e][:, fsl].rearrange("(kt p) n -> p kt n", p=P)),
                  "ldw", 16)
            s.inc(g.dma_start(out=w3sb[:, :].rearrange("p (kt n) -> p kt n", kt=DT),
                              in_=w3T_d[e][:, fsl].rearrange("(kt p) n -> p kt n", p=P)),
                  "ldw", 16)
            s.inc(g.dma_start(out=w2sb[:, :].rearrange("p (kt n) -> p kt n", kt=KT2),
                              in_=w2T_d[e][fsl, :].rearrange("(kt p) n -> p kt n", p=P)),
                  "ldw", 16)
            if r >= 1:
                s.wait(sy, "t_bc", 16 * r)
            s.inc(sy.dma_start(out=grow[:, :], in_=gates_d[e:e + 1, :]), "ldg", 16)
            NLDW = 48 * (r + 1)

            for c in range(NCH):
                b = c % 2
                tsl = slice(c * TC, (c + 1) * TC)
                xb_b = xb[:, b * DT * TC:(b + 1) * DT * TC]
                if gidx >= 2:
                    s.wait(g, "t_p", 16 * (gidx - 1))
                s.inc(g.dma_start(out=xb_b.rearrange("p (kt t) -> p kt t", kt=DT),
                                  in_=_fm3(xn0, slice(None), tsl)), f"xl{b}", 16)
                nxl = 16 * (gidx // 2 + 1)
                bc, setg_bc = ppb.get(s, t)
                s.wait(t, "ldg", 16 * (r + 1))
                s.wait(t, "g", 1)
                s.inc(t.matmul(bc[:, :], onesr[:, :], grow[0:1, tsl],
                               start=True, stop=True), "t_bc")
                for j in range(KT2):
                    p1, setg_1 = pp1.get(s, t)
                    s.wait(t, f"xl{b}", nxl)
                    s.wait(t, "ldw", NLDW)
                    for kt in range(DT):
                        t.matmul(p1[:, :], w1sb[:, kt * FC + j * P: kt * FC + j * P + P],
                                 xb_b[:, kt * TC:(kt + 1) * TC],
                                 start=(kt == 0), stop=(kt == DT - 1))
                    p3, setg_3 = pp3.get(s, t)
                    for kt in range(DT):
                        mm = t.matmul(p3[:, :], w3sb[:, kt * FC + j * P: kt * FC + j * P + P],
                                      xb_b[:, kt * TC:(kt + 1) * TC],
                                      start=(kt == 0), stop=(kt == DT - 1))
                    s.inc(mm, "t_p")
                    njh = s.c["t_p"]
                    s1t = [s1a, s1b][j % 2]
                    s.wait(sc, "t_p", njh)
                    if njh >= 2:
                        s.wait(sc, "v_h", njh - 1)
                    s.inc(sc.activation(s1t[:, :], p1[:, :], AF.Silu), "s_h")
                    setg_1("s_h", s.c["s_h"])
                    s.wait(v, "s_h", njh)
                    if j == 0 and gidx >= 1:
                        s.wait(v, "t_o", 8 * gidx)
                    hj = hsb[:, j * TC:(j + 1) * TC]
                    v.tensor_mul(hj, s1t[:, :], p3[:, :])
                    v.drain()
                    s.inc(v.tensor_mul(hj, hj, bc[:, :]), "v_h")
                    setg_3("v_h", s.c["v_h"])
                setg_bc("v_h", s.c["v_h"])
                for jn in range(DT):
                    po, setg_o = ppo.get(s, t)
                    s.wait(t, "v_h", 16 * (gidx + 1))
                    for kk in range(KT2):
                        mm = t.matmul(po[:, :], w2sb[:, kk * D + jn * P: kk * D + jn * P + P],
                                      hsb[:, kk * TC:(kk + 1) * TC],
                                      start=(kk == 0), stop=(kk == KT2 - 1))
                    s.inc(mm, "t_o")
                    s.wait(sc, "t_o", s.c["t_o"])
                    if gidx >= 2 and jn == 0:
                        s.wait(sc, f"acc{b}", 16 * (gidx // 2))
                    s.inc(sc.copy(osb[:, b * DT * TC + jn * TC: b * DT * TC + (jn + 1) * TC],
                                  po[:, :]), "s_o")
                    setg_o("s_o", s.c["s_o"])
                s.wait(g, "s_o", 8 * (gidx + 1))
                if c == 0 and r >= 1:
                    s.wait(g, "acc0", 16 * 8 * r)
                    s.wait(g, "acc1", 16 * 8 * r)
                if r == 0:
                    dma = g.dma_start(out=_fm3(g0, slice(None), tsl),
                                      in_=osb[:, b * DT * TC:(b + 1) * DT * TC]
                                      .rearrange("p (nt t) -> p nt t", nt=DT))
                else:
                    dma = g.dma_start(out=_fm3(g0, slice(None), tsl),
                                      in_=osb[:, b * DT * TC:(b + 1) * DT * TC]
                                      .rearrange("p (nt t) -> p nt t", nt=DT),
                                      accum_op=ALU.add)
                s.inc(dma, f"acc{b}", 16)
                gidx += 1
        s.wait(g, "acc0", 16 * 8 * NROUND)
        s.wait(g, "acc1", 16 * 8 * NROUND)
        nc.all_engine_barrier()


def phase_combine(ctx, tag, base_d, s_d, gpre_d, out_d):
    nc = ctx.nc
    g, v, sc, t, sy = nc.gpsimd, nc.vector, nc.scalar, nc.tensor, nc.sync
    W = DT * TC
    with nc.cleanup_on_exit():
        s = Sems(nc, tag)
        bsb = nc.alloc_sbuf_tensor(f"{tag}_b", [P, 2 * W], F32)
        ssb = nc.alloc_sbuf_tensor(f"{tag}_s", [P, 2 * W], F32)
        gsb = nc.alloc_sbuf_tensor(f"{tag}_g", [P, 2 * W], F32)
        for c in range(NCH):
            b = c % 2
            tsl = slice(c * TC, (c + 1) * TC)
            bb = bsb[:, b * W:(b + 1) * W]
            sb = ssb[:, b * W:(b + 1) * W]
            gb = gsb[:, b * W:(b + 1) * W]
            if c >= 2:
                s.wait(sy, "v_c", c - 1)
                s.wait(sy, f"st{b}", 16 * (c // 2))
                s.wait(g, "v_c", c - 1)
            s.inc(sy.dma_start(out=bb.rearrange("p (kt t) -> p kt t", kt=DT),
                               in_=_fm3(base_d, slice(None), tsl)), f"lb{b}", 16)
            s.inc(sy.dma_start(out=sb.rearrange("p (kt t) -> p kt t", kt=DT),
                               in_=_fm3(s_d, slice(None), tsl)), f"lb{b}", 16)
            s.inc(g.dma_start(out=gb.rearrange("p (kt t) -> p kt t", kt=DT),
                              in_=_fm3(gpre_d, slice(None), tsl)), f"lg{b}", 16)
            s.wait(sc, f"lg{b}", 16 * (c // 2 + 1))
            if c >= 2:
                s.wait(sc, "v_c", c - 1)
            s.inc(sc.activation(gb, gb, AF.Silu), "s_si")
            s.wait(v, "s_si", c + 1)
            s.wait(v, f"lb{b}", 32 * (c // 2 + 1))
            v.tensor_mul(sb, sb, gb)
            v.drain()
            v.tensor_add(bb, bb, sb)
            s.inc(v.drain(), "v_c")
            s.wait(g, "v_c", c + 1)
            s.inc(g.dma_start(out=_fm3(out_d, slice(None), tsl),
                              in_=bb.rearrange("p (kt t) -> p kt t", kt=DT)),
                  f"st{b}", 16)
        s.wait(g, "st0", 16 * (NCH // 2))
        s.wait(g, "st1", 16 * (NCH // 2))
        nc.all_engine_barrier()


def phase_attn1(ctx, tag, q1l, k1l, v1lt, xn1f, wol_d, wqg_d, wkg_d, wvg_d, wog_d, s1):
    nc = ctx.nc
    g, v, sc, t, sy = nc.gpsimd, nc.vector, nc.scalar, nc.tensor, nc.sync
    CH = 256
    NCHK = TOK // CH
    NH = 16
    with nc.cleanup_on_exit():
        s = Sems(nc, tag)
        qsb = nc.alloc_sbuf_tensor(f"{tag}_q", [P, 2 * DT * CH], BF16)
        ksb = nc.alloc_sbuf_tensor(f"{tag}_k", [P, 2 * DT * CH], BF16)
        vsb = nc.alloc_sbuf_tensor(f"{tag}_v", [P, 2 * (CH // P) * D], BF16)
        xfb = nc.alloc_sbuf_tensor(f"{tag}_xf", [P, 2 * DT * CH], F32)
        ctx_sb = nc.alloc_sbuf_tensor(f"{tag}_ctx", [P, DT * CH], BF16)
        psb = nc.alloc_sbuf_tensor(f"{tag}_p", [P, CH], BF16)
        ptsb = nc.alloc_sbuf_tensor(f"{tag}_pt", [P, CH], BF16)
        identb = nc.alloc_sbuf_tensor(f"{tag}_id", [P, P], BF16)
        negmax = nc.alloc_sbuf_tensor(f"{tag}_nm", [P, 1], F32)
        denom = nc.alloc_sbuf_tensor(f"{tag}_dn", [P, 1], F32)
        recip = nc.alloc_sbuf_tensor(f"{tag}_rc", [P, 1], F32)
        wsb = nc.alloc_sbuf_tensor(f"{tag}_wo", [P, DT * D], BF16)
        osb = nc.alloc_sbuf_tensor(f"{tag}_o", [P, 2 * DT * CH], F32)
        summ = nc.alloc_sbuf_tensor(f"{tag}_sm", [P, DT * NCHK], F32)
        summb = nc.alloc_sbuf_tensor(f"{tag}_smb", [P, DT * NCHK], BF16)
        wg = nc.alloc_sbuf_tensor(f"{tag}_wg", [P, 3 * DT * D], BF16)
        wvg = nc.alloc_sbuf_tensor(f"{tag}_wvg", [P, DT * D], BF16)
        qg = nc.alloc_sbuf_tensor(f"{tag}_qg", [P, DT * NCHK], BF16)
        kg = nc.alloc_sbuf_tensor(f"{tag}_kg", [P, DT * NCHK], BF16)
        vg = [nc.alloc_sbuf_tensor(f"{tag}_vg{s_}", [8, D], BF16) for s_ in range(NSEQ)]
        cg = nc.alloc_sbuf_tensor(f"{tag}_cg", [P, DT * NCHK], BF16)
        go = nc.alloc_sbuf_tensor(f"{tag}_go", [P, DT * NCHK], F32)
        pg = nc.alloc_sbuf_tensor(f"{tag}_pg", [8, 8], BF16)
        S_ps = nc.alloc_psum_tensor(f"{tag}_S", [P, CH], F32)
        PT_ps = nc.alloc_psum_tensor(f"{tag}_PT", [P, CH], BF16)
        cx_ps = nc.alloc_psum_tensor(f"{tag}_cx", [P, P], F32)
        op_ps = PsumPool(nc, tag + "o", 2, [P, CH])
        vg_ps = nc.alloc_psum_tensor(f"{tag}_vg", [8, D], F32)

        s.inc(g.dma_start(out=identb[:, :], in_=ctx.inp("identity", [P, P])), "ldc", 16)
        s.inc(g.dma_start(out=wsb[:, :].rearrange("p (kt n) -> p kt n", kt=DT),
                          in_=wol_d.rearrange("(kt p) n -> p kt n", p=P)), "ldc", 16)
        for wi, wd in enumerate([wqg_d, wkg_d, wog_d]):
            s.inc(g.dma_start(out=wg[:, wi * DT * D:(wi + 1) * DT * D]
                              .rearrange("p (kt n) -> p kt n", kt=DT),
                              in_=wd.rearrange("(kt p) n -> p kt n", p=P)), "ldc", 16)
        s.inc(g.dma_start(out=wvg[:, :].rearrange("p (kt n) -> p kt n", kt=DT),
                          in_=wvg_d.rearrange("(kt p) n -> p kt n", p=P)), "ldc", 16)
        LDC = s.c["ldc"]

        it = 0
        nst = 0
        for ci in range(NCHK):
            cb = ci % 2
            csl = slice(ci * CH, (ci + 1) * CH)
            qb = qsb[:, cb * DT * CH:(cb + 1) * DT * CH]
            kb = ksb[:, cb * DT * CH:(cb + 1) * DT * CH]
            vb = vsb[:, cb * (CH // P) * D:(cb + 1) * (CH // P) * D]
            xb = xfb[:, cb * DT * CH:(cb + 1) * DT * CH]
            if ci >= 2:
                s.wait(g, "t_av", (ci - 1) * NH * 2)
                s.wait(g, "v_res", ci - 1)
            s.inc(g.dma_start(out=qb.rearrange("p (kt t) -> p kt t", kt=DT),
                              in_=_fm3(q1l, slice(None), csl)), f"lq{cb}", 16)
            s.inc(g.dma_start(out=kb.rearrange("p (kt t) -> p kt t", kt=DT),
                              in_=_fm3(k1l, slice(None), csl)), f"lq{cb}", 16)
            s.inc(g.dma_start(out=vb.rearrange("p (tt n) -> p tt n", tt=CH // P),
                              in_=v1lt[csl, :].rearrange("(tt p) n -> p tt n", p=P)),
                  f"lq{cb}", 16)
            s.inc(g.dma_start(out=xb.rearrange("p (kt t) -> p kt t", kt=DT),
                              in_=_fm3(xn1f, slice(None), csl)), f"lq{cb}", 16)
            nlq = 64 * (ci // 2 + 1)
            s.wait(v, f"lq{cb}", nlq)
            if ci >= 2:
                s.wait(v, "t_s", (ci - 1) * NH * 2)
            s.inc(v.tensor_scalar_mul(qb[:, :], qb[:, :], 0.125), "v_qs")

            for h in range(NH):
                hb = (h % 2) * 64
                kt_h = h // 2
                qoff = kt_h * CH
                for qt in range(CH // P):
                    q_t = qb[hb:hb + 64, qoff + qt * P: qoff + (qt + 1) * P]
                    s.wait(t, "v_qs", ci + 1)
                    s.wait(t, "ldc", LDC)
                    if it >= 1:
                        s.wait(t, "s_e", it)
                    for kk in range(CH // P):
                        mm = t.matmul(S_ps[:, kk * P:(kk + 1) * P], q_t,
                                      kb[hb:hb + 64, qoff + kk * P: qoff + (kk + 1) * P],
                                      start=True, stop=True)
                    s.inc(mm, "t_s")
                    s.wait(v, "t_s", it + 1)
                    s.inc(v.reduce_max(negmax[:, 0:1], S_ps[:, :], axis=AX.X,
                                       negate=True), "v_m")
                    s.wait(sc, "v_m", it + 1)
                    if it >= 1:
                        s.wait(sc, "t_pt", it)
                    s.inc(sc.activation(psb[:, :], S_ps[:, :], AF.Exp,
                                        bias=negmax[:, 0:1],
                                        accum_out=denom[:, 0:1]), "s_e")
                    s.wait(v, "s_e", it + 1)
                    v.reciprocal(recip[:, 0:1], denom[:, 0:1])
                    v.drain()
                    s.inc(v.tensor_scalar_mul(psb[:, :], psb[:, :], recip[:, 0:1]), "v_r")
                    s.wait(t, "v_r", it + 1)
                    if it >= 1:
                        s.wait(t, "s_pt", it)
                    for kk in range(CH // P):
                        tp = t.transpose(PT_ps[:, kk * P:(kk + 1) * P],
                                         psb[:, kk * P:(kk + 1) * P], identb[:, :])
                    s.inc(tp, "t_pt")
                    s.wait(sc, "t_pt", it + 1)
                    if it >= 1:
                        s.wait(sc, "t_av", it)
                    s.inc(sc.copy(ptsb[:, :], PT_ps[:, :]), "s_pt")
                    s.wait(t, "s_pt", it + 1)
                    if it >= 1:
                        s.wait(t, "s_cx", it)
                    for kk in range(CH // P):
                        mm = t.matmul(cx_ps[hb:hb + 64, :],
                                      vb[:, kk * D + h * 64: kk * D + h * 64 + 64],
                                      ptsb[:, kk * P:(kk + 1) * P],
                                      start=(kk == 0), stop=(kk == CH // P - 1),
                                      tile_position=(0, hb))
                    s.inc(mm, "t_av")
                    s.wait(sc, "t_av", it + 1)
                    s.inc(sc.copy(ctx_sb[hb:hb + 64, qoff + qt * P: qoff + (qt + 1) * P],
                                  cx_ps[hb:hb + 64, :]), "s_cx")
                    it += 1

            ob = osb[:, cb * DT * CH:(cb + 1) * DT * CH]
            for j in range(DT):
                po, setg_o = op_ps.get(s, t)
                s.wait(t, "s_cx", it)
                for kt in range(DT):
                    mm = t.matmul(po[:, :], wsb[:, kt * D + j * P: kt * D + j * P + P],
                                  ctx_sb[:, kt * CH:(kt + 1) * CH],
                                  start=(kt == 0), stop=(kt == DT - 1))
                s.inc(mm, "t_op")
                s.wait(sc, "t_op", s.c["t_op"])
                if ci >= 2 and j == 0:
                    s.wait(sc, f"st{cb}", 16 * (ci // 2))
                s.inc(sc.copy(ob[:, j * CH:(j + 1) * CH], po[:, :]), "s_op")
                setg_o("s_op", s.c["s_op"])
            s.wait(v, "s_op", s.c["s_op"])
            for j in range(DT):
                v.tensor_add(ob[:, j * CH:(j + 1) * CH], ob[:, j * CH:(j + 1) * CH],
                             xb[:, j * CH:(j + 1) * CH])
            s.inc(v.drain(), "v_res")
            for j in range(DT):
                v.reduce_sum(summ[:, j * NCHK + ci: j * NCHK + ci + 1],
                             ob[:, j * CH:(j + 1) * CH], axis=AX.X)
            s.inc(v.drain(), "v_sum")
            s.wait(sy, "v_res", ci + 1)
            s.inc(sy.dma_start(out=_fm3(s1, slice(None), csl),
                               in_=ob.rearrange("p (kt t) -> p kt t", kt=DT)),
                  f"st{cb}", 16)
            nst += 1

        s.wait(v, "v_sum", NCHK)
        v.tensor_scalar_mul(summ[:, :], summ[:, :], 1.0 / CH)
        v.drain()
        s.inc(v.tensor_copy(summb[:, :], summ[:, :]), "v_gb")
        for oi, (woff, dst) in enumerate([(0, qg), (DT * D, kg)]):
            for j in range(DT):
                po, setg_o = op_ps.get(s, t)
                s.wait(t, "v_gb", 1)
                for kt in range(DT):
                    mm = t.matmul(po[:, 0:NCHK],
                                  wg[:, woff + kt * D + j * P: woff + kt * D + j * P + P],
                                  summb[:, kt * NCHK:(kt + 1) * NCHK],
                                  start=(kt == 0), stop=(kt == DT - 1))
                s.inc(mm, "t_g")
                s.wait(v, "t_g", s.c["t_g"])
                s.inc(v.tensor_copy(dst[:, j * NCHK:(j + 1) * NCHK], po[:, 0:NCHK]),
                      "v_gq")
                setg_o("v_gq", s.c["v_gq"])
        s.wait(v, "v_gq", s.c["v_gq"])
        s.inc(v.tensor_scalar_mul(qg[:, :], qg[:, :], 0.125), "v_gq2")
        for sq in range(NSEQ):
            s.wait(t, "v_gb", 1)
            if sq >= 1:
                s.wait(t, "v_gq", 16 + sq)
            for half in range(2):
                for kt in range(DT):
                    mm = t.matmul(vg_ps[0:8, half * 512:(half + 1) * 512],
                                  summb[:, kt * NCHK + 8 * sq: kt * NCHK + 8 * sq + 8],
                                  wvg[:, kt * D + half * 512: kt * D + (half + 1) * 512],
                                  start=(kt == 0), stop=(kt == DT - 1))
            s.inc(mm, "t_g")
            s.wait(v, "t_g", s.c["t_g"])
            s.inc(v.tensor_copy(vg[sq][0:8, :], vg_ps[0:8, 0:D]), "v_gq")

        git = 0
        for sq in range(NSEQ):
            for h in range(NH):
                hb = (h % 2) * 64
                kt_h = h // 2
                coff = kt_h * NCHK + 8 * sq
                s.wait(t, "v_gq2", 1)
                s.wait(t, "v_gq", 16 + NSEQ)
                if git == 0:
                    s.wait(t, "s_e", it)
                    s.wait(t, "s_pt", it)
                    s.wait(t, "s_cx", it)
                else:
                    s.wait(t, "s_ge", git)
                mm = t.matmul(S_ps[0:8, 0:8], qg[hb:hb + 64, coff:coff + 8],
                              kg[hb:hb + 64, coff:coff + 8], start=True, stop=True)
                s.inc(mm, "t_gs")
                s.wait(v, "t_gs", git + 1)
                s.inc(v.reduce_max(negmax[0:8, 0:1], S_ps[0:8, 0:8], axis=AX.X,
                                   negate=True), "v_gm")
                s.wait(sc, "v_gm", git + 1)
                if git >= 1:
                    s.wait(sc, "t_gav", git)
                s.inc(sc.activation(pg[0:8, 0:8], S_ps[0:8, 0:8], AF.Exp,
                                    bias=negmax[0:8, 0:1],
                                    accum_out=denom[0:8, 0:1]), "s_ge")
                s.wait(v, "s_ge", git + 1)
                v.reciprocal(recip[0:8, 0:1], denom[0:8, 0:1])
                v.drain()
                s.inc(v.tensor_scalar_mul(pg[0:8, 0:8], pg[0:8, 0:8],
                                          recip[0:8, 0:1]), "v_gr")
                s.wait(t, "v_gr", git + 1)
                if git >= 1:
                    s.wait(t, "s_gpt", git)
                s.inc(t.transpose(PT_ps[0:8, 0:8], pg[0:8, 0:8], identb[0:8, 0:8]),
                      "t_gpt")
                s.wait(sc, "t_gpt", git + 1)
                if git >= 1:
                    s.wait(sc, "t_gav", git)
                s.inc(sc.copy(pg[0:8, 0:8], PT_ps[0:8, 0:8]), "s_gpt")
                s.wait(t, "s_gpt", git + 1)
                if git >= 1:
                    s.wait(t, "s_gcx", git)
                s.inc(t.matmul(cx_ps[hb:hb + 64, 0:8], vg[sq][0:8, h * 64:(h + 1) * 64],
                               pg[0:8, 0:8], start=True, stop=True,
                               tile_position=(0, hb)), "t_gav")
                s.wait(sc, "t_gav", git + 1)
                s.inc(sc.copy(cg[hb:hb + 64, coff:coff + 8], cx_ps[hb:hb + 64, 0:8]),
                      "s_gcx")
                git += 1
        for j in range(DT):
            po, setg_o = op_ps.get(s, t)
            s.wait(t, "s_gcx", git)
            for kt in range(DT):
                mm = t.matmul(po[:, 0:NCHK],
                              wg[:, 2 * DT * D + kt * D + j * P: 2 * DT * D + kt * D + j * P + P],
                              cg[:, kt * NCHK:(kt + 1) * NCHK],
                              start=(kt == 0), stop=(kt == DT - 1))
            s.inc(mm, "t_go")
            s.wait(v, "t_go", s.c["t_go"])
            s.inc(v.tensor_add(go[:, j * NCHK:(j + 1) * NCHK], po[:, 0:NCHK],
                               summ[:, j * NCHK:(j + 1) * NCHK]), "v_go")
            setg_o("v_go", s.c["v_go"])

        for ci in range(NCHK):
            cb = ci % 2
            csl = slice(ci * CH, (ci + 1) * CH)
            ob = osb[:, cb * DT * CH:(cb + 1) * DT * CH]
            if ci >= 2:
                s.wait(sy, "v_up", ci - 1)
                s.wait(sy, f"su{cb}", 16 * (ci // 2))
            s.wait(sy, f"st{cb}", 16 * (NCHK // 2))
            s.inc(sy.dma_start(out=ob.rearrange("p (kt t) -> p kt t", kt=DT),
                               in_=_fm3(s1, slice(None), csl)), f"lu{cb}", 16)
            s.wait(v, f"lu{cb}", 16 * (ci // 2 + 1))
            s.wait(v, "v_go", DT)
            if ci >= 2:
                s.wait(v, f"su{cb}", 16 * (ci // 2))
            for j in range(DT):
                v.tensor_scalar_add(ob[:, j * CH:(j + 1) * CH],
                                    ob[:, j * CH:(j + 1) * CH],
                                    go[:, j * NCHK + ci: j * NCHK + ci + 1])
            s.inc(v.drain(), "v_up")
            s.wait(sy, "v_up", ci + 1)
            s.inc(sy.dma_start(out=_fm3(s1, slice(None), csl),
                               in_=ob.rearrange("p (kt t) -> p kt t", kt=DT)),
                  f"su{cb}", 16)
        s.wait(sy, "su0", 16 * (NCHK // 2))
        s.wait(sy, "su1", 16 * (NCHK // 2))
        nc.all_engine_barrier()


def phase_shiftqkv(ctx, tag, x_l, wqkv_d, qs_d, ks_d, qw, kw, vwt):
    nc = ctx.nc
    g, v, sc, t, sy = nc.gpsimd, nc.vector, nc.scalar, nc.tensor, nc.sync
    NQ = 512
    NT = NQ // P
    eff_level = 3 if SQKV_EPI_TILES < 10**9 else SQKV_LEVEL
    with nc.cleanup_on_exit():
        s = Sems(nc, tag)
        hb = nc.alloc_sbuf_tensor(f"{tag}_h", [P, 2 * DT * TC], BF16)
        wsb = nc.alloc_sbuf_tensor(f"{tag}_w", [P, DT * 1536], BF16)
        q2sb = nc.alloc_sbuf_tensor(f"{tag}_q2", [P, 2 * TC], BF16)
        qraw = nc.alloc_sbuf_tensor(f"{tag}_qr", [P, 2 * TC], BF16)
        qnsb = nc.alloc_sbuf_tensor(f"{tag}_qn", [P, 2 * TC], BF16)
        t1sb = nc.alloc_sbuf_tensor(f"{tag}_t1", [P, TC], BF16)
        t2sb = nc.alloc_sbuf_tensor(f"{tag}_t2", [P, TC], BF16)

        qscol = nc.alloc_sbuf_tensor(f"{tag}_qs", [P, 1], F32)
        kscol = nc.alloc_sbuf_tensor(f"{tag}_ks", [P, 1], F32)
        hs2 = nc.alloc_sbuf_tensor(f"{tag}_hs", [P, 2], BF16)
        Rb = nc.alloc_sbuf_tensor(f"{tag}_R", [P, P], BF16)
        ones64 = nc.alloc_sbuf_tensor(f"{tag}_o64", [1, 64], BF16)
        na = nc.alloc_sbuf_tensor(f"{tag}_na", [1, TC], F32)
        nb = nc.alloc_sbuf_tensor(f"{tag}_nb", [1, TC], F32)
        nab = nc.alloc_sbuf_tensor(f"{tag}_nab", [1, TC], BF16)
        nbb = nc.alloc_sbuf_tensor(f"{tag}_nbb", [1, TC], BF16)
        tabs = nc.alloc_sbuf_tensor(f"{tag}_tab", [P, 4 * SEQ], F32)
        qout = nc.alloc_sbuf_tensor(f"{tag}_qo", [P, 2 * NT * TC], BF16)
        kout = nc.alloc_sbuf_tensor(f"{tag}_ko", [P, 2 * NT * TC], BF16)
        vout = nc.alloc_sbuf_tensor(f"{tag}_vo", [P, 2 * (TC // P) * NQ], BF16)
        pq = PsumPool(nc, tag + "q", 2, [P, TC])
        ptok = PsumPool(nc, tag + "t", 2, [P, NQ])
        nps_a = nc.alloc_psum_tensor(f"{tag}_npa", [1, TC], F32)
        nps_b = nc.alloc_psum_tensor(f"{tag}_npb", [1, TC], F32)
        bc_ps = nc.alloc_psum_tensor(f"{tag}_bc", [P, TC], F32)
        rot_ps = nc.alloc_psum_tensor(f"{tag}_ro", [P, TC], F32)

        s.inc(g.dma_start(out=wsb[:, :].rearrange("p (kt n) -> p kt n", kt=DT),
                          in_=wqkv_d.rearrange("(kt p) n -> p kt n", p=P)), "ldc", 16)
        s.inc(g.dma_start(out=qscol[:, :], in_=qs_d), "ldc", 16)
        s.inc(g.dma_start(out=kscol[:, :], in_=ks_d), "ldc", 16)
        s.inc(g.dma_start(out=hs2[:, :], in_=ctx.inp("headsum2", [P, 2])), "ldc", 16)
        s.inc(g.dma_start(out=Rb[:, :], in_=ctx.inp("rothalf", [P, P])), "ldc", 16)
        s.inc(g.memset(ones64[:, :], 1.0), "ldc_g")
        for ti, tn in enumerate(["cs_q", "ss_q", "cs_k", "ss_k"]):
            s.inc(g.dma_start(out=tabs[:, ti * SEQ:(ti + 1) * SEQ],
                              in_=ctx.inp(tn, [P, SEQ])), "ldc", 16)
        LDC = s.c["ldc"]

        xv = x_l.rearrange("(kt p) t -> p kt t", p=P)
        zv = ctx.inp("zeros_col", [D, 1]).rearrange("(kt p) z -> p kt z", p=P)
        jq = 0
        ne = 0
        nst = 0
        for c in range(NCH):
            b = c % 2
            tsl = slice(c * TC, (c + 1) * TC)
            toff = (c % 4) * TC
            hb_b = hb[:, b * DT * TC:(b + 1) * DT * TC]
            if c >= 2:
                s.wait(g, "t_qkv", (c - 1) * (2 * NT + (0 if SQKV_NO_V else NT)))
            s.inc(g.dma_start(out=hb_b.rearrange("p (kt t) -> p kt t", kt=DT)[:, 0:4, :],
                              in_=xv[:, 0:4, tsl]), f"lh{b}", 16)
            if c % 4 == 0:
                s.inc(g.dma_start(
                    out=hb_b.rearrange("p (kt t) -> p kt t", kt=DT)[:, 4:8, 1:TC],
                    in_=xv[:, 4:8, c * TC:(c + 1) * TC - 1]), f"lh{b}", 16)
                s.inc(g.dma_start(
                    out=hb_b.rearrange("p (kt t) -> p kt t", kt=DT)[:, 4:8, 0:1],
                    in_=zv[:, 4:8, :], allow_slow_non_contiguous=True), f"lh{b}", 16)
            else:
                s.inc(g.dma_start(
                    out=hb_b.rearrange("p (kt t) -> p kt t", kt=DT)[:, 4:8, :],
                    in_=xv[:, 4:8, c * TC - 1:(c + 1) * TC - 1]), f"lh{b}", 16)
            nlh = s.c[f"lh{b}"]

            for part in range(2):
                woff = part * NQ
                scol = [qscol, kscol][part]
                cs_t = tabs[:, (2 * part) * SEQ + toff:(2 * part) * SEQ + toff + TC]
                ss_t = tabs[:, (2 * part + 1) * SEQ + toff:(2 * part + 1) * SEQ + toff + TC]
                outb = [qout, kout][part]
                outd = [qw, kw][part]
                for j in range(NT):
                    jb = jq % 2
                    ps, setg_ps = pq.get(s, t)
                    s.wait(t, f"lh{b}", nlh)
                    s.wait(t, "ldc", LDC)
                    for kt in range(DT):
                        mm = t.matmul(ps[:, :],
                                      wsb[:, kt * 1536 + woff + j * P: kt * 1536 + woff + j * P + P],
                                      hb_b[:, kt * TC:(kt + 1) * TC],
                                      start=(kt == 0), stop=(kt == DT - 1))
                    s.inc(mm, "t_qkv")
                    if SQKV_LEVEL >= 4 or jq >= SQKV_EPI_TILES:
                        s.wait(v, "t_qkv", s.c["t_qkv"])
                        s.inc(v.tensor_copy(outb[:, b * NT * TC + j * TC:
                                                 b * NT * TC + (j + 1) * TC], ps[:, :]),
                              "v_fin")
                        setg_ps("v_fin", s.c["v_fin"])
                        jq += 1
                        continue
                    jb = ne % 2
                    s.wait(v, "t_qkv", s.c["t_qkv"])
                    if ne >= 2:
                        s.wait(v, "v_qn" if eff_level in (0, 1, 2) else "v_fin2",
                               ne - 1)
                        s.wait(v, "t_nrm", ne - 1)
                    v.tensor_copy(qraw[:, jb * TC:(jb + 1) * TC], ps[:, :])
                    s.inc(v.drain(), "v_cp")
                    s.inc(v.tensor_mul(q2sb[:, jb * TC:(jb + 1) * TC],
                                       qraw[:, jb * TC:(jb + 1) * TC],
                                       qraw[:, jb * TC:(jb + 1) * TC]), "s_sq")
                    setg_ps(("v_cp", s.c["v_cp"]),)
                    s.wait(t, "s_sq", ne + 1)
                    if ne >= 1:
                        s.wait(t, "s_rt", ne)
                    t.matmul(nps_a[:, :], hs2[:, 0:1],
                             q2sb[:, jb * TC:(jb + 1) * TC], start=True, stop=True)
                    s.inc(t.matmul(nps_b[:, :], hs2[:, 1:2],
                                   q2sb[:, jb * TC:(jb + 1) * TC],
                                   start=True, stop=True), "t_nrm")
                    s.wait(sc, "t_nrm", ne + 1)
                    if ne >= 1:
                        s.wait(sc, "v_rn", ne)
                    sc.activation(na[:, :], nps_a[:, :], AF.Sqrt)
                    s.inc(sc.activation(nb[:, :], nps_b[:, :], AF.Sqrt), "s_rt")
                    s.wait(v, "s_rt", ne + 1)
                    if ne >= 1 and eff_level in (0, 1, 2):
                        s.wait(v, "t_bc", ne)
                    v.tensor_scalar_max(na[:, :], na[:, :], 1e-12)
                    v.tensor_scalar_max(nb[:, :], nb[:, :], 1e-12)
                    v.drain()
                    v.reciprocal(na[:, :], na[:, :])
                    v.reciprocal(nb[:, :], nb[:, :])
                    v.drain()
                    v.tensor_copy(nab[:, :], na[:, :])
                    s.inc(v.tensor_copy(nbb[:, :], nb[:, :]), "v_rn")
                    if SQKV_LEVEL == 3 or SQKV_EPI_TILES < 10**9:
                        s.wait(v, "v_cp", s.c["v_cp"])
                        vc = v.tensor_copy(outb[:, b * NT * TC + j * TC:
                                                b * NT * TC + (j + 1) * TC],
                                           qraw[:, jb * TC:(jb + 1) * TC])
                        s.inc(vc, "v_fin")
                        v.drain()
                        s.inc(v.drain() if False else v.nop(), "v_fin2") if False else None
                        s.inc(v.drain(), "v_fin2")
                        jq += 1
                        ne += 1
                        continue
                    s.wait(t, "v_rn", ne + 1)
                    s.wait(t, "ldc_g", 1)
                    if ne >= 1:
                        s.wait(t, "v_qn", ne)
                    t.matmul(bc_ps[0:64, :], ones64[:, :], nab[:, :],
                             start=True, stop=True)
                    s.inc(t.matmul(bc_ps[64:128, :], ones64[:, :], nbb[:, :],
                                   start=True, stop=True, tile_position=(0, 64)), "t_bc")
                    s.wait(v, "t_bc", ne + 1)
                    s.wait(v, "v_cp", s.c["v_cp"])
                    if ne >= 2:
                        s.wait(v, "t_rot", ne - 1)
                    s.inc(v.scalar_tensor_tensor(qnsb[:, jb * TC:(jb + 1) * TC],
                                                 qraw[:, jb * TC:(jb + 1) * TC],
                                                 scol[:, 0:1], bc_ps[:, :],
                                                 ALU.mult, ALU.mult), "v_qn")
                    if SQKV_LEVEL == 2:
                        s.wait(v, "v_qn", ne + 1)
                        s.inc(v.tensor_copy(outb[:, b * NT * TC + j * TC:
                                                 b * NT * TC + (j + 1) * TC],
                                            qnsb[:, jb * TC:(jb + 1) * TC]), "v_fin")
                        jq += 1
                        continue
                    s.wait(t, "v_qn", ne + 1)
                    if ne >= 1:
                        s.wait(t, "v_fin2", ne)
                    s.inc(t.matmul(rot_ps[:, :], Rb[:, :], qnsb[:, jb * TC:(jb + 1) * TC],
                                   start=True, stop=True), "t_rot")
                    if SQKV_LEVEL == 1:
                        s.wait(v, "t_rot", ne + 1)
                        s.inc(v.tensor_copy(outb[:, b * NT * TC + j * TC:
                                                 b * NT * TC + (j + 1) * TC],
                                            qnsb[:, jb * TC:(jb + 1) * TC]), "v_fin")
                        jq += 1
                        continue
                    s.wait(v, "t_rot", ne + 1)
                    if c >= 2 and part == 0 and j == 0:
                        s.wait(v, f"st{b}", 16 * 3 * (c // 2))
                    v.tensor_mul(t1sb[:, :], rot_ps[:, :], ss_t)
                    v.drain()
                    v.tensor_mul(t2sb[:, :], qnsb[:, jb * TC:(jb + 1) * TC], cs_t)
                    v.drain()
                    s.inc(v.tensor_add(outb[:, b * NT * TC + j * TC: b * NT * TC + (j + 1) * TC],
                                       t1sb[:, :], t2sb[:, :]), "v_fin")
                    s.inc(v.drain(), "v_fin2")
                    jq += 1
                    ne += 1
            for tt in (range(0) if SQKV_NO_V else range(TC // P)):
                ps, setg_ps = ptok.get(s, t)
                s.wait(t, f"lh{b}", nlh)
                for kt in range(DT):
                    mm = t.matmul(ps[:, :],
                                  hb_b[:, kt * TC + tt * P: kt * TC + (tt + 1) * P],
                                  wsb[:, kt * 1536 + 2 * NQ: kt * 1536 + 3 * NQ],
                                  start=(kt == 0), stop=(kt == DT - 1))
                s.inc(mm, "t_qkv")
                s.wait(v, "t_qkv", s.c["t_qkv"])
                if c >= 2 and tt == 0:
                    s.wait(v, f"st{b}", 16 * 3 * (c // 2))
                s.inc(v.tensor_copy(vout[:, b * (TC // P) * NQ + tt * NQ:
                                         b * (TC // P) * NQ + (tt + 1) * NQ],
                                    ps[:, :]), "v_ev")
                setg_ps("v_ev", s.c["v_ev"])
            if SQKV_NO_STORE:
                continue
            s.wait(sy, "v_fin", jq)
            if not SQKV_NO_V:
                s.wait(sy, "v_ev", s.c["v_ev"])
            s.inc(sy.dma_start(out=_fm3(qw, slice(None), tsl),
                               in_=qout[:, b * NT * TC:(b + 1) * NT * TC]
                               .rearrange("p (nt t) -> p nt t", nt=NT)), f"st{b}", 16)
            s.inc(sy.dma_start(out=_fm3(kw, slice(None), tsl),
                               in_=kout[:, b * NT * TC:(b + 1) * NT * TC]
                               .rearrange("p (nt t) -> p nt t", nt=NT)), f"st{b}", 16)
            if not SQKV_NO_V:
                s.inc(sy.dma_start(out=vwt[tsl, :].rearrange("(tt p) n -> p tt n", p=P),
                                   in_=vout[:, b * (TC // P) * NQ:(b + 1) * (TC // P) * NQ]
                                   .rearrange("p (tt n) -> p tt n", tt=TC // P)), f"st{b}", 16)
            nst += 3
        if not SQKV_NO_STORE:
            nper = 2 if SQKV_NO_V else 3
            s.wait(sy, "st0", 16 * nper * (NCH // 2))
            s.wait(sy, "st1", 16 * nper * (NCH // 2))
        nc.all_engine_barrier()


def phase_wattn(ctx, tag, qw, kw, vwt, ctxw):
    nc = ctx.nc
    g, v, sc, t, sy = nc.gpsimd, nc.vector, nc.scalar, nc.tensor, nc.sync
    NQ = 512
    NHW = 8
    WIN = 512
    NW = SEQ // WIN
    QT = WIN // P
    NTQ = NQ // P
    with nc.cleanup_on_exit():
        s = Sems(nc, tag)
        qsb = nc.alloc_sbuf_tensor(f"{tag}_q", [P, NTQ * SEQ], BF16)
        ksb = nc.alloc_sbuf_tensor(f"{tag}_k", [P, NTQ * SEQ], BF16)
        vsb = nc.alloc_sbuf_tensor(f"{tag}_v", [P, (SEQ // P) * NQ], BF16)
        ctx_sb = nc.alloc_sbuf_tensor(f"{tag}_c", [P, NTQ * SEQ], BF16)
        smsb = nc.alloc_sbuf_tensor(f"{tag}_sm", [P, 640], F32)
        psb = nc.alloc_sbuf_tensor(f"{tag}_p", [P, 640], BF16)
        ptsb = nc.alloc_sbuf_tensor(f"{tag}_pt", [P, 640], BF16)
        maskb = nc.alloc_sbuf_tensor(f"{tag}_mb", [P, 640], F32)
        maskc = nc.alloc_sbuf_tensor(f"{tag}_mc", [P, P], F32)
        identb = nc.alloc_sbuf_tensor(f"{tag}_id", [P, P], BF16)
        negmax = nc.alloc_sbuf_tensor(f"{tag}_nm", [P, 1], F32)
        denom = nc.alloc_sbuf_tensor(f"{tag}_dn", [P, 1], F32)
        recip = nc.alloc_sbuf_tensor(f"{tag}_rc", [P, 1], F32)
        S_ps = nc.alloc_psum_tensor(f"{tag}_S", [P, 1024], F32)
        PT_ps = nc.alloc_psum_tensor(f"{tag}_PT", [P, 1024], BF16)
        cx_ps = nc.alloc_psum_tensor(f"{tag}_cx", [P, P], F32)

        s.inc(g.dma_start(out=maskb[:, :], in_=ctx.inp("mask_band", [P, 640])), "ldc", 16)
        s.inc(g.dma_start(out=maskc[:, :], in_=ctx.inp("mask_causal", [P, P])), "ldc", 16)
        s.inc(g.dma_start(out=identb[:, :], in_=ctx.inp("identity", [P, P])), "ldc", 16)
        LDC = s.c["ldc"]

        it = 0
        nst = 0
        for sq in range(NSEQ):
            ssl = slice(sq * SEQ, (sq + 1) * SEQ)
            sb = sq % 2
            if sq >= 1:
                s.wait(g, "st", 16 * nst)
            s.inc(g.dma_start(out=qsb[:, :].rearrange("p (kt t) -> p kt t", kt=NTQ),
                              in_=_fm3(qw, slice(None), ssl)), f"lq{sb}", 16)
            s.inc(g.dma_start(out=ksb[:, :].rearrange("p (kt t) -> p kt t", kt=NTQ),
                              in_=_fm3(kw, slice(None), ssl)), f"lq{sb}", 16)
            s.inc(g.dma_start(out=vsb[:, :].rearrange("p (tt n) -> p tt n", tt=SEQ // P),
                              in_=vwt[ssl, :].rearrange("(tt p) n -> p tt n", p=P)),
                  f"lq{sb}", 16)
            nlq = 48 * (sq // 2 + 1)

            for w in range(NW):
                for h in range(NHW):
                    hb = (h % 2) * 64
                    kt_h = h // 2
                    hoff = kt_h * SEQ
                    for qt in range(QT):
                        q0 = qt * P
                        q_t = qsb[hb:hb + 64,
                                  hoff + w * WIN + q0: hoff + w * WIN + q0 + P]
                        if w == 0:
                            kr = q0 + P
                            kbase = hoff + 0
                            nkk = kr // P
                        else:
                            kr = 640
                            kbase = hoff + (w - 1) * WIN + q0
                            nkk = 5
                        s.wait(t, f"lq{sb}", nlq)
                        s.wait(t, "ldc", LDC)
                        if it >= 1:
                            s.wait(t, "v_sm", it)
                        for kk in range(nkk):
                            mm = t.matmul(S_ps[:, kk * P:(kk + 1) * P], q_t,
                                          ksb[hb:hb + 64, kbase + kk * P: kbase + (kk + 1) * P],
                                          start=True, stop=True)
                        s.inc(mm, "t_s")
                        s.wait(v, "t_s", it + 1)
                        if it >= 1:
                            s.wait(v, "s_e", it)
                        if w == 0:
                            if qt > 0:
                                v.tensor_copy(smsb[:, 0:q0], S_ps[:, 0:q0])
                                v.drain()
                            v.tensor_add(smsb[:, q0:kr], S_ps[:, q0:kr], maskc[:, :])
                            v.drain()
                        else:
                            v.tensor_add(smsb[:, 0:kr], S_ps[:, 0:kr], maskb[:, :])
                            v.drain()
                        s.inc(v.reduce_max(negmax[:, 0:1], smsb[:, 0:kr], axis=AX.X,
                                           negate=True), "v_sm")
                        s.wait(sc, "v_sm", it + 1)
                        if it >= 1:
                            s.wait(sc, "t_pt", it)
                        s.inc(sc.activation(psb[:, 0:kr], smsb[:, 0:kr], AF.Exp,
                                            bias=negmax[:, 0:1],
                                            accum_out=denom[:, 0:1]), "s_e")
                        s.wait(v, "s_e", it + 1)
                        v.reciprocal(recip[:, 0:1], denom[:, 0:1])
                        v.drain()
                        s.inc(v.tensor_scalar_mul(psb[:, 0:kr], psb[:, 0:kr],
                                                  recip[:, 0:1]), "v_r")
                        s.wait(t, "v_r", it + 1)
                        if it >= 1:
                            s.wait(t, "s_pt", it)
                        for kk in range(nkk):
                            tp = t.transpose(PT_ps[:, kk * P:(kk + 1) * P],
                                             psb[:, kk * P:(kk + 1) * P], identb[:, :])
                        s.inc(tp, "t_pt")
                        s.wait(sc, "t_pt", it + 1)
                        if it >= 1:
                            s.wait(sc, "t_av", it)
                        s.inc(sc.copy(ptsb[:, 0:kr], PT_ps[:, 0:kr]), "s_pt")
                        s.wait(t, "s_pt", it + 1)
                        if it >= 1:
                            s.wait(t, "s_cx", it)
                        for kk in range(nkk):
                            if w == 0:
                                vrow = kk * P
                            else:
                                vrow = (w - 1) * WIN + q0 + kk * P
                            vt_i = vrow // P
                            mm = t.matmul(cx_ps[hb:hb + 64, :],
                                          vsb[:, vt_i * NQ + h * 64: vt_i * NQ + h * 64 + 64],
                                          ptsb[:, kk * P:(kk + 1) * P],
                                          start=(kk == 0), stop=(kk == nkk - 1),
                                          tile_position=(0, hb))
                        s.inc(mm, "t_av")
                        s.wait(sc, "t_av", it + 1)
                        s.inc(sc.copy(ctx_sb[hb:hb + 64,
                                             hoff + w * WIN + q0: hoff + w * WIN + q0 + P],
                                      cx_ps[hb:hb + 64, :]), "s_cx")
                        it += 1
            s.wait(sy, "s_cx", it)
            s.inc(sy.dma_start(out=_fm3(ctxw, slice(None), ssl),
                               in_=ctx_sb[:, :].rearrange("p (nt t) -> p nt t", nt=NTQ)),
                  "st", 16)
            nst += 1
        s.wait(sy, "st", 16 * nst)
        nc.all_engine_barrier()


def phase_lnshift(ctx, tag, x_mid, lg_d, lb_d, hln, eps=1e-5):
    nc = ctx.nc
    g, v, sc, t, sy = nc.gpsimd, nc.vector, nc.scalar, nc.tensor, nc.sync
    with nc.cleanup_on_exit():
        s = Sems(nc, tag)
        xf = nc.alloc_sbuf_tensor(f"{tag}_xf", [P, 2 * DT * TC], F32)
        xb = nc.alloc_sbuf_tensor(f"{tag}_xb", [P, 2 * DT * TC], BF16)
        x2 = nc.alloc_sbuf_tensor(f"{tag}_x2", [P, DT * TC], BF16)
        tmp = nc.alloc_sbuf_tensor(f"{tag}_tmp", [P, 2 * DT * TC], F32)
        onesb = nc.alloc_sbuf_tensor(f"{tag}_ones", [P, 1], BF16)
        onesf_row = nc.alloc_sbuf_tensor(f"{tag}_onesr", [1, P], F32)
        lgc = nc.alloc_sbuf_tensor(f"{tag}_lg", [P, DT], F32)
        lbc = nc.alloc_sbuf_tensor(f"{tag}_lb", [P, DT], F32)
        ssum = nc.alloc_sbuf_tensor(f"{tag}_ssum", [1, TOK], F32)
        ssq = nc.alloc_sbuf_tensor(f"{tag}_ssq", [1, TOK], F32)
        xnout = nc.alloc_sbuf_tensor(f"{tag}_xn", [P, 2 * DT * TC], BF16)
        ps_sum = nc.alloc_psum_tensor(f"{tag}_pss", [1, TC], F32)
        ps_sq = nc.alloc_psum_tensor(f"{tag}_psq", [1, TC], F32)
        ps_a = nc.alloc_psum_tensor(f"{tag}_psa", [P, TC], F32)
        ps_b = nc.alloc_psum_tensor(f"{tag}_psb", [P, TC], F32)

        s.inc(g.memset(onesb[:, :], 1.0), "g")
        s.inc(g.memset(onesf_row[:, :], 1.0), "g")
        s.inc(g.dma_start(out=lgc[:, :], in_=lg_d), "ldc", 16)
        s.inc(g.dma_start(out=lbc[:, :], in_=lb_d), "ldc", 16)

        xv = x_mid.rearrange("(kt p) t -> p kt t", p=P)
        zv = ctx.inp("zeros_col", [D, 1]).rearrange("(kt p) z -> p kt z", p=P)

        def load_shifted(eng, dst_b, c, semkey):
            tsl = slice(c * TC, (c + 1) * TC)
            s.inc(eng.dma_start(out=dst_b.rearrange("p (kt t) -> p kt t", kt=DT)[:, 0:4, :],
                                in_=xv[:, 0:4, tsl]), semkey, 16)
            if c % 4 == 0:
                s.inc(eng.dma_start(
                    out=dst_b.rearrange("p (kt t) -> p kt t", kt=DT)[:, 4:8, 1:TC],
                    in_=xv[:, 4:8, c * TC:(c + 1) * TC - 1]), semkey, 16)
                s.inc(eng.dma_start(
                    out=dst_b.rearrange("p (kt t) -> p kt t", kt=DT)[:, 4:8, 0:1],
                    in_=zv[:, 4:8, :], allow_slow_non_contiguous=True), semkey, 16)
            else:
                s.inc(eng.dma_start(
                    out=dst_b.rearrange("p (kt t) -> p kt t", kt=DT)[:, 4:8, :],
                    in_=xv[:, 4:8, c * TC - 1:(c + 1) * TC - 1]), semkey, 16)

        for c in range(NCH):
            b = c % 2
            tsl = slice(c * TC, (c + 1) * TC)
            xf_b = xf[:, b * DT * TC:(b + 1) * DT * TC]
            xb_b = xb[:, b * DT * TC:(b + 1) * DT * TC]
            if c >= 2:
                s.wait(g, "t_stats", 2 * (c - 1))
                s.wait(sy, "s_sq", c - 1)
            load_shifted(g, xb_b, c, f"ld{b}")
            load_shifted(sy, xf_b, c, f"ldf{b}")
            nld = s.c[f"ld{b}"]
            nldf = s.c[f"ldf{b}"]
            s.wait(sc, f"ldf{b}", nldf)
            if c >= 1:
                s.wait(sc, "t_stats", 2 * c)
            s.inc(sc.activation(x2[:, :], xf_b[:, :], AF.Square), "s_sq")
            s.wait(t, f"ld{b}", nld)
            s.wait(t, "g", 2)
            s.wait(t, "s_sq", c + 1)
            if c >= 1:
                s.wait(t, "v_stats", c)
            for kt in range(DT):
                t.matmul(ps_sum[:, :], onesb[:, :], xb_b[:, kt * TC:(kt + 1) * TC],
                         start=(kt == 0), stop=(kt == DT - 1))
            for kt in range(DT):
                mm = t.matmul(ps_sq[:, :], onesb[:, :], x2[:, kt * TC:(kt + 1) * TC],
                              start=(kt == 0), stop=(kt == DT - 1))
            s.inc(mm, "t_stats", 2)
            s.wait(v, "t_stats", 2 * (c + 1))
            v.tensor_copy(ssum[:, tsl], ps_sum[:, :])
            s.inc(v.tensor_copy(ssq[:, tsl], ps_sq[:, :]), "v_stats")

        trow = tmp[0:1, 0:TOK]
        s.wait(v, "v_stats", NCH)
        v.tensor_scalar_mul(ssum[:, :], ssum[:, :], 1.0 / D)
        v.drain()
        v.tensor_scalar_mul(ssq[:, :], ssq[:, :], 1.0 / D)
        v.drain()
        v.tensor_mul(trow, ssum[:, :], ssum[:, :])
        v.drain()
        v.tensor_sub(ssq[:, :], ssq[:, :], trow)
        v.drain()
        s.inc(v.tensor_scalar_add(ssq[:, :], ssq[:, :], eps), "v1")
        s.wait(sc, "v1", 1)
        s.inc(sc.activation(ssq[:, :], ssq[:, :], AF.Sqrt), "s1")
        s.wait(v, "s1", 1)
        v.reciprocal(ssq[:, :], ssq[:, :])
        v.drain()
        s.inc(v.scalar_tensor_tensor(ssum[:, :], ssum[:, :], -1.0, ssq[:, :],
                                     ALU.mult, ALU.mult), "v2")
        arow, brow = ssq, ssum

        for c in range(NCH):
            b = c % 2
            tsl = slice(c * TC, (c + 1) * TC)
            xf_b = xf[:, b * DT * TC:(b + 1) * DT * TC]
            tmp_b = tmp[:, b * DT * TC:(b + 1) * DT * TC]
            xn_b = xnout[:, b * DT * TC:(b + 1) * DT * TC]
            s.wait(sy, "s_sq", NCH)
            if c >= 2:
                s.wait(sy, "v_app", c - 1)
            load_shifted(sy, xf_b, c, f"ldf{b}")
            s.wait(t, "v2", 1)
            if c >= 1:
                s.wait(t, "v_app", c)
            t.matmul(ps_a[:, :], onesf_row[:, :], arow[:, tsl], start=True, stop=True)
            s.inc(t.matmul(ps_b[:, :], onesf_row[:, :], brow[:, tsl], start=True,
                           stop=True), "t_ab")
            s.wait(v, f"ldf{b}", s.c[f"ldf{b}"])
            s.wait(v, "t_ab", c + 1)
            s.wait(v, "ldc", 32)
            if c >= 2:
                s.wait(v, f"st{b}", 16 * (c // 2))
            for kt in range(DT):
                v.tensor_mul(tmp_b[:, kt * TC:(kt + 1) * TC],
                             xf_b[:, kt * TC:(kt + 1) * TC], ps_a[:, :])
            v.drain()
            for kt in range(DT):
                v.tensor_add(tmp_b[:, kt * TC:(kt + 1) * TC],
                             tmp_b[:, kt * TC:(kt + 1) * TC], ps_b[:, :])
            v.drain()
            for kt in range(DT):
                v.tensor_scalar(xn_b[:, kt * TC:(kt + 1) * TC],
                                tmp_b[:, kt * TC:(kt + 1) * TC],
                                lgc[:, kt:kt + 1], lbc[:, kt:kt + 1],
                                ALU.mult, ALU.add)
            s.inc(v.drain(), "v_app")
            s.wait(g, "v_app", c + 1)
            s.inc(g.dma_start(out=_fm3(hln, slice(None), tsl),
                              in_=xn_b.rearrange("p (kt t) -> p kt t", kt=DT)),
                  f"st{b}", 16)
        s.wait(g, "st0", 16 * (NCH // 2))
        s.wait(g, "st1", 16 * (NCH // 2))
        nc.all_engine_barrier()


def phase_w1gate(ctx, tag, hln, w1A_d, w1G_d, b1_d, actw):
    nc = ctx.nc
    g, v, sc, t, sy = nc.gpsimd, nc.vector, nc.scalar, nc.tensor, nc.sync
    NHID = 2816
    NT = NHID // P
    with nc.cleanup_on_exit():
        s = Sems(nc, tag)
        wA = nc.alloc_sbuf_tensor(f"{tag}_wA", [P, DT * NHID], BF16)
        wG = nc.alloc_sbuf_tensor(f"{tag}_wG", [P, DT * NHID], BF16)
        bsb = nc.alloc_sbuf_tensor(f"{tag}_b", [P, 2 * NT], F32)
        ab = nc.alloc_sbuf_tensor(f"{tag}_act", [P, 2 * DT * TC], BF16)
        asb = nc.alloc_sbuf_tensor(f"{tag}_a", [P, 2 * TC], BF16)
        gsb = nc.alloc_sbuf_tensor(f"{tag}_g", [P, 2 * TC], BF16)
        osb = nc.alloc_sbuf_tensor(f"{tag}_o", [P, 2 * NT * TC], BF16)
        ppA = PsumPool(nc, tag + "A", 2, [P, TC])
        ppG = PsumPool(nc, tag + "G", 2, [P, TC])

        s.inc(g.dma_start(out=wA[:, :].rearrange("p (kt n) -> p kt n", kt=DT),
                          in_=w1A_d.rearrange("(kt p) n -> p kt n", p=P)), "ldw", 16)
        s.inc(g.dma_start(out=wG[:, :].rearrange("p (kt n) -> p kt n", kt=DT),
                          in_=w1G_d.rearrange("(kt p) n -> p kt n", p=P)), "ldw", 16)
        s.inc(g.dma_start(out=bsb[:, :], in_=b1_d), "ldw", 16)
        LDW = s.c["ldw"]

        for c in range(NCH):
            b = c % 2
            tsl = slice(c * TC, (c + 1) * TC)
            ab_b = ab[:, b * DT * TC:(b + 1) * DT * TC]
            if c >= 2:
                s.wait(g, "t_c", 2 * NT * (c - 1))
            s.inc(g.dma_start(out=ab_b.rearrange("p (kt t) -> p kt t", kt=DT),
                              in_=_fm3(hln, slice(None), tsl)), f"lda{b}", 16)
            nlda = 16 * (c // 2 + 1)
            for j in range(NT):
                jb = j % 2
                pA, setg_A = ppA.get(s, t)
                s.wait(t, f"lda{b}", nlda)
                s.wait(t, "ldw", LDW)
                for kt in range(DT):
                    mm = t.matmul(pA[:, :], wA[:, kt * NHID + j * P: kt * NHID + j * P + P],
                                  ab_b[:, kt * TC:(kt + 1) * TC],
                                  start=(kt == 0), stop=(kt == DT - 1))
                s.inc(mm, "t_c")
                pG, setg_G = ppG.get(s, t)
                for kt in range(DT):
                    mm = t.matmul(pG[:, :], wG[:, kt * NHID + j * P: kt * NHID + j * P + P],
                                  ab_b[:, kt * TC:(kt + 1) * TC],
                                  start=(kt == 0), stop=(kt == DT - 1))
                s.inc(mm, "t_c")
                nj = s.c["t_c"] // 2
                s.wait(sc, "t_c", s.c["t_c"])
                if nj >= 3:
                    s.wait(sc, "v_g", nj - 2)
                s.inc(sc.add(asb[:, jb * TC:(jb + 1) * TC], pA[:, :], bsb[:, j:j + 1]),
                      "s_a")
                setg_A("s_a", s.c["s_a"])
                s.inc(sc.activation(gsb[:, jb * TC:(jb + 1) * TC], pG[:, :], AF.Gelu,
                                    bias=bsb[:, NT + j:NT + j + 1]), "s_g")
                setg_G("s_g", s.c["s_g"])
                s.wait(v, "s_g", nj)
                if c >= 2 and j == 0:
                    s.wait(v, f"st{b}", 16 * (c // 2))
                s.inc(v.tensor_mul(osb[:, b * NT * TC + j * TC: b * NT * TC + (j + 1) * TC],
                                   asb[:, jb * TC:(jb + 1) * TC],
                                   gsb[:, jb * TC:(jb + 1) * TC]), "v_g")
            s.wait(sy, "v_g", s.c["v_g"])
            s.inc(sy.dma_start(out=_fm3(actw, slice(None), tsl),
                               in_=osb[:, b * NT * TC:(b + 1) * NT * TC]
                               .rearrange("p (nt t) -> p nt t", nt=NT)), f"st{b}", 16)
        s.wait(sy, "st0", 16 * (NCH // 2))
        s.wait(sy, "st1", 16 * (NCH // 2))
        nc.all_engine_barrier()


PHASES = ["ln0", "qkv0", "attn0", "router", "moe", "comb0", "qkv1", "attn1",
          "f0a", "f0b", "f0c", "f0d", "f0e", "f0f",
          "f1a", "f1b", "f1c", "f1d", "f1e", "f1f", "final"]


def build_program(upto=None, dbg=()):
    nc = bass.Bass("TRN2", target_bir_lowering=False)
    ctx = Ctx(nc)
    phases = PHASES if upto is None else PHASES[:PHASES.index(upto) + 1]

    x_fm = ctx.inp("x_fm", [D, TOK])
    xn0 = ctx.scr("xn0", [D, TOK], BF16)
    xn0f = ctx.scr("xn0f", [D, TOK], F32)

    if "ln0" in phases:
        phase_ln(ctx, "ln0", x_fm, xn0, xn_f32=xn0f)

    if "qkv0" in phases:
        qkvT = ctx.inp("qkvT", [D, 3072])
        qkvb = ctx.inp("qkvb", [P, 24])
        qkvb_row = ctx.inp("qkvb_row", [1, 3072])
        qk0 = ctx.scr("qk0", [2048, TOK], BF16)
        v0t = ctx.scr("v0t", [TOK, D], BF16)
        phase_gemm(ctx, "qkv0", xn0, D, [
            dict(kind="fm", w=qkvT[:, 0:2048], N=2048, bias=qkvb[:, 0:16],
                 outs=[(qk0, "bf16")]),
            dict(kind="tok", w=qkvT[:, 2048:3072], N=1024,
                 bias_row=qkvb_row[0:1, 2048:3072], out=v0t),
        ])

    if "attn0" in phases:
        outT_d = ctx.inp("outT", [D, D])
        outb_d = ctx.inp("outb", [P, DT])
        s0 = ctx.scr("s0", [D, TOK], F32)
        phase_attn0(ctx, "attn0", ctx.scratch["qk0"], ctx.scratch["v0t"],
                    outT_d, outb_d, s0)

    if "router" in phases:
        wrT_d = ctx.inp("wrT", [D, 8])
        gates_d = ctx.scr("gates_d", [8, TOK], F32)
        phase_router(ctx, "router", xn0f, wrT_d, gates_d)

    if "moe" in phases:
        w1T_d = ctx.inp("w1T", [8, D, 4096])
        w3T_d = ctx.inp("w3T", [8, D, 4096])
        w2T_d = ctx.inp("w2T", [8, 4096, D])
        g0 = ctx.scr("g0", [D, TOK], F32)
        phase_moe(ctx, "moe", xn0, w1T_d, w3T_d, w2T_d,
                  ctx.scratch["gates_d"], g0)

    if "comb0" in phases:
        x1 = ctx.scr("x1", [D, TOK], F32)
        xn1h = ctx.scr("xn1h", [D, TOK], BF16)
        xn1f = ctx.scr("xn1f", [D, TOK], F32)
        phase_combine(ctx, "comb0", x_fm, ctx.scratch["s0"], g0, x1)
        phase_ln(ctx, "ln1", x1, xn1h, xn_f32=xn1f)

    if "qkv1" in phases:
        q1l = ctx.scr("q1l", [D, TOK], BF16)
        k1l = ctx.scr("k1l", [D, TOK], BF16)
        v1lt = ctx.scr("v1lt", [TOK, D], BF16)
        phase_gemm(ctx, "qkv1", ctx.scratch["xn1h"], D, [
            dict(kind="fm", w=ctx.inp("wql", [D, D]), N=D, outs=[(q1l, "bf16")]),
            dict(kind="fm", w=ctx.inp("wkl", [D, D]), N=D, outs=[(k1l, "bf16")]),
            dict(kind="tok", w=ctx.inp("wvl", [D, D]), N=D, out=v1lt),
        ])

    if "attn1" in phases:
        s1 = ctx.scr("s1", [D, TOK], F32)
        phase_attn1(ctx, "attn1", ctx.scratch["q1l"], ctx.scratch["k1l"],
                    ctx.scratch["v1lt"], ctx.scratch["xn1f"],
                    ctx.inp("wol", [D, D]), ctx.inp("wqg", [D, D]),
                    ctx.inp("wkg", [D, D]), ctx.inp("wvg", [D, D]),
                    ctx.inp("wog", [D, D]), s1)

    if any(p.startswith("f0") or p.startswith("f1") for p in phases):
        wqkv_d = ctx.inp("wqkv1", [2, D, 1536])
        wo1_d = ctx.inp("wo1", [2, 512, D])
        w1p_d = ctx.inp("w1p", [2, D, 5632])
        b1p_d = ctx.inp("b1p", [2, P, 44])
        w2p_d = ctx.inp("w2p", [2, 2816, D])
        b2p_d = ctx.inp("b2p", [2, P, DT])
        qs_d = ctx.inp("qs", [2, P, 1])
        ks_d = ctx.inp("ks", [2, P, 1])
        lng_d = ctx.inp("lng", [2, P, DT])
        lnb_d = ctx.inp("lnb", [2, P, DT])
        x_l = ctx.scratch["xn1f"]
        for l in range(2):
            qw = ctx.scr(f"f{l}_qw", [512, TOK], BF16)
            kw = ctx.scr(f"f{l}_kw", [512, TOK], BF16)
            vwt = ctx.scr(f"f{l}_vwt", [TOK, 512], BF16)
            ctxw = ctx.scr(f"f{l}_ctx", [512, TOK], BF16)
            xi_mid = ctx.scr(f"f{l}_xm", [D, TOK], F32)
            hln = ctx.scr(f"f{l}_hln", [D, TOK], BF16)
            actw = ctx.scr(f"f{l}_act", [2816, TOK], BF16)
            xi_next = ctx.scr("g1" if l == 1 else f"f{l}_xn", [D, TOK], F32)
            if f"f{l}a" in phases:
                phase_shiftqkv(ctx, f"f{l}a", x_l, wqkv_d[l], qs_d[l], ks_d[l],
                               qw, kw, vwt)
            if f"f{l}b" in phases:
                phase_wattn(ctx, f"f{l}b", qw, kw, vwt, ctxw)
            if f"f{l}c" in phases:
                phase_gemm(ctx, f"f{l}c", ctxw, 512, [
                    dict(kind="fm", w=wo1_d[l], N=D, res=x_l,
                         outs=[(xi_mid, "f32")]),
                ])
            if f"f{l}d" in phases:
                phase_lnshift(ctx, f"f{l}d", xi_mid, lng_d[l], lnb_d[l], hln)
            if f"f{l}e" in phases:
                phase_w1gate(ctx, f"f{l}e", hln, w1p_d[l][:, 0:2816],
                             w1p_d[l][:, 2816:5632], b1p_d[l], actw)
            if f"f{l}f" in phases:
                phase_gemm(ctx, f"f{l}f", actw, 2816, [
                    dict(kind="fm", w=w2p_d[l], N=D, bias=b2p_d[l], res=xi_mid,
                         outs=[(xi_next, "f32")]),
                ])
            x_l = xi_next

    if "final" in phases:
        out_fm = ctx.out("out_fm", [D, TOK], F32)
        phase_combine(ctx, "final", ctx.scratch["x1"], ctx.scratch["s1"],
                      ctx.scratch["g1"], out_fm)

    for name in dbg:
        src = ctx.scratch[name]
        dst = ctx.out("dbg_" + name, list(src.shape), src.dtype)
        sem = nc.alloc_semaphore(f"dbgcp_{name}")
        nc.sync.dma_start(out=dst[:, :], in_=src[:, :]).then_inc(sem, 16)
        nc.sync.wait_ge(sem, 16)
    nc.all_engine_barrier()
    return nc, ctx


def prep_in_maps(inputs):
    shared = _prep_shared(inputs)
    x = np.asarray(inputs["L0_attn_x"], dtype=np.float32)
    in_maps = []
    for c in range(N_CORES):
        xs = x[4 * c:4 * c + 4]
        x_fm = np.ascontiguousarray(xs.transpose(2, 0, 1).reshape(D, TOK))
        m = dict(shared)
        m["x_fm"] = x_fm
        in_maps.append(m)
    return in_maps


def kernel(**inputs):
    nc, ctx = build_program()
    in_maps = prep_in_maps(inputs)
    used = set(ctx.din.keys())
    in_maps = [{k: v for k, v in m.items() if k in used} for m in in_maps]
    res = run_bass_kernel_spmd(nc, in_maps, core_ids=list(range(N_CORES)))
    outs = []
    for c in range(N_CORES):
        o = res.results[c]["out_fm"]
        outs.append(o.reshape(D, NSEQ, SEQ).transpose(1, 2, 0))
    return np.concatenate(outs, axis=0).astype(np.float32)


# version 02
# speedup vs baseline: 1.0467x; 1.0467x over previous
import numpy as np
import concourse.bass as bass
import concourse.mybir as mybir
from concourse.bass_utils import run_bass_kernel_spmd

F32 = mybir.dt.float32
BF16 = mybir.dt.bfloat16
AF = mybir.ActivationFunctionType
ALU = mybir.AluOpType
AX = mybir.AxisListType

P = 128
D = 1024
DT = D // P
TOK = 8192
SEQ = 2048
NSEQ = 4
TC = 512
NCH = TOK // TC
NEG = -1.0e30

N_CORES = 8
MOE_MAX_ROUNDS = None
SQKV_LEVEL = 0
SQKV_EPI_TILES = 10**9
SQKV_NO_V = False
SQKV_NO_STORE = False


def _host_consts():
    c = {}
    c["identity"] = np.eye(P, dtype=np.float32)
    R = np.zeros((P, P), dtype=np.float32)
    for m in range(P):
        r = m % 64
        base = m - r
        if r < 32:
            R[base + r + 32, m] = -1.0
        else:
            R[base + r - 32, m] = 1.0
    c["rothalf"] = R
    HS = np.zeros((P, 2), dtype=np.float32)
    HS[0:64, 0] = 1.0
    HS[64:128, 1] = 1.0
    c["headsum2"] = HS
    c["ones_col"] = np.ones((P, 1), dtype=np.float32)
    c["zeros_col"] = np.zeros((D, 1), dtype=np.float32)
    HSEL = np.zeros((P, P), dtype=np.float32)
    HSEL[0:64, 0] = 1.0
    HSEL[64:128, 64] = 1.0
    c["hsel"] = HSEL
    BSEL = np.zeros((P, P), dtype=np.float32)
    BSEL[0, 0:64] = 1.0
    BSEL[64, 64:128] = 1.0
    c["bsel"] = BSEL
    c["ones_row"] = np.ones((1, P), dtype=np.float32)
    i = np.arange(P)[:, None]
    j = np.arange(P)[None, :]
    c["mask_causal"] = np.where(j <= i, 0.0, NEG).astype(np.float32)
    ii = np.arange(P)[:, None]
    jj = np.arange(640)[None, :]
    c["mask_band"] = np.where((jj >= ii) & (jj <= ii + 512), 0.0, NEG).astype(np.float32)
    DIM_HEAD = 64
    WINDOW = 512
    QK_SCALE = 8.0
    n = SEQ
    half = np.arange(0, DIM_HEAD, 2, dtype=np.float64)
    inv_freq = 1.0 / 10000.0 ** (half / DIM_HEAD)
    t = np.arange(n, dtype=np.float64)
    freqs = t[:, None] * inv_freq[None, :]
    freqs = np.concatenate([freqs, freqs], axis=1)
    base = (half + 0.4 * DIM_HEAD) / (1.4 * DIM_HEAD)
    power = (t - n // 2) / float(WINDOW // 2)
    scale = base[None, :] ** power[:, None]
    scale = np.concatenate([scale, scale], axis=1)
    cos = np.cos(freqs)
    sin = np.sin(freqs)
    def fm(a):
        a = np.ascontiguousarray(a.T.astype(np.float32))
        return np.concatenate([a, a], axis=0)
    c["cs_q"] = fm(cos * scale * QK_SCALE)
    c["ss_q"] = fm(sin * scale * QK_SCALE)
    c["cs_k"] = fm(cos / scale)
    c["ss_k"] = fm(sin / scale)
    return c


def _prep_shared(inp):
    A = {}
    ct = lambda a: np.ascontiguousarray(a)
    A["qkvT"] = ct(inp["L0_attn_qkv_w"].T.astype(np.float32))
    A["outT"] = ct(inp["L0_attn_out_w"].T.astype(np.float32))
    A["qkvb"] = ct(inp["L0_attn_qkv_b"].reshape(24, P).T.astype(np.float32))
    A["qkvb_row"] = ct(inp["L0_attn_qkv_b"].reshape(1, 3072).astype(np.float32))
    A["outb"] = ct(inp["L0_attn_out_b"].reshape(8, P).T.astype(np.float32))
    A["wrT"] = ct(inp["L0_ffn_Wr"].T.astype(np.float32))
    A["w1T"] = ct(np.transpose(inp["L0_ffn_W1"], (0, 2, 1)).astype(np.float32))
    A["w3T"] = ct(np.transpose(inp["L0_ffn_W3"], (0, 2, 1)).astype(np.float32))
    A["w2T"] = ct(np.transpose(inp["L0_ffn_W2"], (0, 2, 1)).astype(np.float32))
    for nm, key in [("wql", "L1_attn_Wq_l"), ("wkl", "L1_attn_Wk_l"),
                    ("wvl", "L1_attn_Wv_l"), ("wol", "L1_attn_Wo_l"),
                    ("wqg", "L1_attn_Wq_g"), ("wkg", "L1_attn_Wk_g"),
                    ("wvg", "L1_attn_Wv_g"), ("wog", "L1_attn_Wo_g")]:
        A[nm] = ct(inp[key].astype(np.float32))
    A["wqkv1"] = ct(inp["L1_ffn_wqkv"].astype(np.float32))
    A["wo1"] = ct(inp["L1_ffn_wo"].astype(np.float32))
    w1 = inp["L1_ffn_w1"].astype(np.float32)
    HID = 2730
    HIDP = 2816
    w1p = np.zeros((2, D, 2 * HIDP), dtype=np.float32)
    w1p[:, :, :HID] = w1[:, :, :HID]
    w1p[:, :, HIDP:HIDP + HID] = w1[:, :, HID:]
    A["w1p"] = ct(w1p)
    b1 = inp["L1_ffn_b1"].astype(np.float32)
    b1p = np.zeros((2, 2 * HIDP), dtype=np.float32)
    b1p[:, :HID] = b1[:, :HID]
    b1p[:, HIDP:HIDP + HID] = b1[:, HID:]
    A["b1p"] = ct(np.transpose(b1p.reshape(2, 44, P), (0, 2, 1)))
    w2 = inp["L1_ffn_w2"].astype(np.float32)
    w2p = np.zeros((2, HIDP, D), dtype=np.float32)
    w2p[:, :HID, :] = w2
    A["w2p"] = ct(w2p)
    A["b2p"] = ct(np.transpose(inp["L1_ffn_b2"].astype(np.float32).reshape(2, 8, P), (0, 2, 1)))
    A["qs"] = ct(np.concatenate([inp["L1_ffn_q_scale"]] * 2, axis=1).astype(np.float32)[:, :, None])
    A["ks"] = ct(np.concatenate([inp["L1_ffn_k_scale"]] * 2, axis=1).astype(np.float32)[:, :, None])
    A["lng"] = ct(np.transpose(inp["L1_ffn_ln_g"].astype(np.float32).reshape(2, 8, P), (0, 2, 1)))
    A["lnb"] = ct(np.transpose(inp["L1_ffn_ln_b"].astype(np.float32).reshape(2, 8, P), (0, 2, 1)))
    A.update(_host_consts())
    return A


class Ctx:

    def __init__(self, nc):
        self.nc = nc
        self.din = {}
        self.scratch = {}
        self.dout = {}

    def inp(self, name, shape):
        if name not in self.din:
            self.din[name] = self.nc.dram_tensor(name, list(shape), F32,
                                                 kind="ExternalInput").ap()
        return self.din[name]

    def scr(self, name, shape, dtype=F32):
        if name not in self.scratch:
            self.scratch[name] = self.nc.dram_tensor(name, list(shape), dtype,
                                                     kind="Internal").ap()
        return self.scratch[name]

    def out(self, name, shape, dtype=F32):
        if name not in self.dout:
            self.dout[name] = self.nc.dram_tensor(name, list(shape), dtype,
                                                  kind="ExternalOutput").ap()
        return self.dout[name]


class Sems:

    def __init__(self, nc, tag):
        self.nc = nc
        self.tag = tag
        self.h = {}
        self.c = {}

    def inc(self, instr, key, n=1):
        if key not in self.h:
            self.h[key] = self.nc.alloc_semaphore(f"{self.tag}_{key}")
            self.c[key] = 0
        instr.then_inc(self.h[key], n)
        self.c[key] += n
        return self.c[key]

    def wait(self, eng, key, val=None):
        if key not in self.h:
            self.h[key] = self.nc.alloc_semaphore(f"{self.tag}_{key}")
            self.c[key] = 0
        eng.wait_ge(self.h[key], self.c[key] if val is None else val)


def _fm3(dram2d, kt, tsl):
    K = dram2d.shape[0]
    v = dram2d.rearrange("(kt p) t -> p kt t", p=P)
    return v[:, kt, tsl]


def phase_ln(ctx, tag, xsrc, xn_bf, xn_f32=None, eps=1e-5):
    nc = ctx.nc
    nstores = 1 if xn_f32 is None else 2
    with nc.cleanup_on_exit():
        s = Sems(nc, tag)
        xf = nc.alloc_sbuf_tensor(f"{tag}_xf", [P, 2 * DT * TC], F32)
        xb = nc.alloc_sbuf_tensor(f"{tag}_xb", [P, 2 * DT * TC], BF16)
        x2 = nc.alloc_sbuf_tensor(f"{tag}_x2", [P, DT * TC], BF16)
        tmp = nc.alloc_sbuf_tensor(f"{tag}_tmp", [P, 2 * DT * TC], F32)
        onesb = nc.alloc_sbuf_tensor(f"{tag}_ones", [P, 1], BF16)
        onesf_row = nc.alloc_sbuf_tensor(f"{tag}_onesr", [1, P], F32)
        ssum = nc.alloc_sbuf_tensor(f"{tag}_ssum", [1, TOK], F32)
        ssq = nc.alloc_sbuf_tensor(f"{tag}_ssq", [1, TOK], F32)
        xnout = nc.alloc_sbuf_tensor(f"{tag}_xn", [P, 2 * DT * TC], BF16)
        ps_sum = nc.alloc_psum_tensor(f"{tag}_pss", [1, TC], F32)
        ps_sq = nc.alloc_psum_tensor(f"{tag}_psq", [1, TC], F32)
        ps_a = nc.alloc_psum_tensor(f"{tag}_psa", [P, TC], F32)
        ps_b = nc.alloc_psum_tensor(f"{tag}_psb", [P, TC], F32)

        g, v, sc, t, sy = nc.gpsimd, nc.vector, nc.scalar, nc.tensor, nc.sync

        s.inc(g.memset(onesb[:, :], 1.0), "g")
        s.inc(g.memset(onesf_row[:, :], 1.0), "g")

        for c in range(NCH):
            b = c % 2
            tsl = slice(c * TC, (c + 1) * TC)
            xf_b = xf[:, b * DT * TC:(b + 1) * DT * TC]
            xb_b = xb[:, b * DT * TC:(b + 1) * DT * TC]
            if c >= 2:
                s.wait(g, "t_stats", 2 * (c - 1))
                s.wait(sy, "s_sq", c - 1)
            s.inc(g.dma_start(out=xb_b.rearrange("p (kt t) -> p kt t", kt=DT),
                              in_=_fm3(xsrc, slice(None), tsl)), f"ld{b}", 16)
            s.inc(sy.dma_start(out=xf_b.rearrange("p (kt t) -> p kt t", kt=DT),
                               in_=_fm3(xsrc, slice(None), tsl)), f"ldf{b}", 16)
            s.wait(sc, f"ldf{b}", 16 * (c // 2 + 1))
            if c >= 1:
                s.wait(sc, "t_stats", 2 * c)
            s.inc(sc.activation(x2[:, :], xf_b[:, :], AF.Square), "s_sq")
            s.wait(t, f"ld{b}", 16 * (c // 2 + 1))
            s.wait(t, "g", 2)
            s.wait(t, "s_sq", c + 1)
            if c >= 1:
                s.wait(t, "v_stats", c)
            for kt in range(DT):
                t.matmul(ps_sum[:, :], onesb[:, :], xb_b[:, kt * TC:(kt + 1) * TC],
                         start=(kt == 0), stop=(kt == DT - 1))
            for kt in range(DT):
                mm = t.matmul(ps_sq[:, :], onesb[:, :], x2[:, kt * TC:(kt + 1) * TC],
                              start=(kt == 0), stop=(kt == DT - 1))
            s.inc(mm, "t_stats", 2)
            s.wait(v, "t_stats", 2 * (c + 1))
            v.tensor_copy(ssum[:, tsl], ps_sum[:, :])
            s.inc(v.tensor_copy(ssq[:, tsl], ps_sq[:, :]), "v_stats")

        trow = tmp[0:1, 0:TOK]
        s.wait(v, "v_stats", NCH)
        v.tensor_scalar_mul(ssum[:, :], ssum[:, :], 1.0 / D)
        v.drain()
        v.tensor_scalar_mul(ssq[:, :], ssq[:, :], 1.0 / D)
        v.drain()
        v.tensor_mul(trow, ssum[:, :], ssum[:, :])
        v.drain()
        v.tensor_sub(ssq[:, :], ssq[:, :], trow)
        v.drain()
        s.inc(v.tensor_scalar_add(ssq[:, :], ssq[:, :], eps), "v1")
        s.wait(sc, "v1", 1)
        s.inc(sc.activation(ssq[:, :], ssq[:, :], AF.Sqrt), "s1")
        s.wait(v, "s1", 1)
        v.reciprocal(ssq[:, :], ssq[:, :])
        v.drain()
        s.inc(v.scalar_tensor_tensor(ssum[:, :], ssum[:, :], -1.0, ssq[:, :],
                                     ALU.mult, ALU.mult), "v2")
        arow, brow = ssq, ssum

        for c in range(NCH):
            b = c % 2
            tsl = slice(c * TC, (c + 1) * TC)
            xf_b = xf[:, b * DT * TC:(b + 1) * DT * TC]
            tmp_b = tmp[:, b * DT * TC:(b + 1) * DT * TC]
            xn_b = xnout[:, b * DT * TC:(b + 1) * DT * TC]
            s.wait(sy, "s_sq", NCH)
            if c >= 2:
                s.wait(sy, "v_app", c - 1)
            s.inc(sy.dma_start(out=xf_b.rearrange("p (kt t) -> p kt t", kt=DT),
                               in_=_fm3(xsrc, slice(None), tsl)), f"ldf{b}", 16)
            s.wait(t, "v2", 1)
            if c >= 1:
                s.wait(t, "v_app", c)
            t.matmul(ps_a[:, :], onesf_row[:, :], arow[:, tsl], start=True, stop=True)
            s.inc(t.matmul(ps_b[:, :], onesf_row[:, :], brow[:, tsl], start=True, stop=True), "t_ab")
            s.wait(v, f"ldf{b}", 16 * (NCH // 2 + c // 2 + 1))
            s.wait(v, "t_ab", c + 1)
            if c >= 2:
                s.wait(v, f"st{b}", 16 * nstores * (c // 2))
            for kt in range(DT):
                v.tensor_mul(tmp_b[:, kt * TC:(kt + 1) * TC],
                             xf_b[:, kt * TC:(kt + 1) * TC], ps_a[:, :])
            v.drain()
            for kt in range(DT):
                v.tensor_add(tmp_b[:, kt * TC:(kt + 1) * TC],
                             tmp_b[:, kt * TC:(kt + 1) * TC], ps_b[:, :])
            v.drain()
            for kt in range(DT):
                v.tensor_copy(xn_b[:, kt * TC:(kt + 1) * TC],
                              tmp_b[:, kt * TC:(kt + 1) * TC])
            s.inc(v.drain(), "v_app")
            s.wait(g, "v_app", c + 1)
            s.inc(g.dma_start(out=_fm3(xn_bf, slice(None), tsl),
                              in_=xn_b.rearrange("p (kt t) -> p kt t", kt=DT)), f"st{b}", 16)
            if xn_f32 is not None:
                s.inc(g.dma_start(out=_fm3(xn_f32, slice(None), tsl),
                                  in_=tmp_b.rearrange("p (kt t) -> p kt t", kt=DT)), f"st{b}", 16)
        s.wait(g, "st0", 16 * nstores * (NCH // 2))
        s.wait(g, "st1", 16 * nstores * (NCH // 2))
        nc.all_engine_barrier()


class PsumPool:

    def __init__(self, nc, tag, n, shape, dtype=F32):
        self.tiles = [nc.alloc_psum_tensor(f"{tag}_ps{i}", list(shape), dtype)
                      for i in range(n)]
        self.gate = [None] * n
        self.i = 0

    def get(self, s, eng):
        i = self.i
        self.i = (self.i + 1) % len(self.tiles)
        if self.gate[i] is not None:
            for kv in self.gate[i]:
                s.wait(eng, *kv)
        tile = self.tiles[i]

        def setgate(*pairs):
            self.gate[i] = [pairs] if pairs and not isinstance(pairs[0], tuple)                 else list(pairs)
        return tile, setgate


def phase_gemm(ctx, tag, act_src, Kdim, specs):
    nc = ctx.nc
    KT = Kdim // P
    g, v, sc, t, sy = nc.gpsimd, nc.vector, nc.scalar, nc.tensor, nc.sync
    with nc.cleanup_on_exit():
        s = Sems(nc, tag)
        nw = 0
        for sp in specs:
            N = sp["N"]
            sp["wsb"] = nc.alloc_sbuf_tensor(f"{tag}_w{nw}", [P, KT * N], BF16)
            s.inc(g.dma_start(out=sp["wsb"][:, :].rearrange("p (kt n) -> p kt n", kt=KT),
                              in_=sp["w"].rearrange("(kt p) n -> p kt n", p=P)), "ldw", 16)
            if sp["kind"] == "fm" and sp.get("bias") is not None:
                NT = N // P
                sp["bsb"] = nc.alloc_sbuf_tensor(f"{tag}_b{nw}", [P, NT], F32)
                s.inc(g.dma_start(out=sp["bsb"][:, :], in_=sp["bias"][:, :]), "ldw", 16)
            if sp["kind"] == "tok" and sp.get("bias_row") is not None:
                sp["brsb"] = nc.alloc_sbuf_tensor(f"{tag}_br{nw}", [1, N], F32)
                s.inc(g.dma_start(out=sp["brsb"][:, :], in_=sp["bias_row"]), "ldw", 16)
            nw += 1
        LDW = s.c["ldw"]
        ab = nc.alloc_sbuf_tensor(f"{tag}_act", [P, 2 * KT * TC], BF16)
        onesf_row = nc.alloc_sbuf_tensor(f"{tag}_onesr", [1, P], F32)
        s.inc(g.memset(onesf_row[:, :], 1.0), "g")
        for i, sp in enumerate(specs):
            N = sp["N"]
            if sp["kind"] == "fm":
                NT = N // P
                dt_o = F32 if any(d == "f32" for _, d in sp["outs"]) else BF16
                sp["osb"] = nc.alloc_sbuf_tensor(f"{tag}_o{i}", [P, 2 * NT * TC], dt_o)
                if any(d == "bf16" for _, d in sp["outs"]) and dt_o == F32:
                    sp["osb_bf"] = nc.alloc_sbuf_tensor(f"{tag}_ob{i}", [P, 2 * NT * TC], BF16)
                if sp.get("res") is not None:
                    sp["rsb"] = nc.alloc_sbuf_tensor(f"{tag}_r{i}", [P, 2 * NT * TC], F32)
            else:
                sp["osb"] = nc.alloc_sbuf_tensor(f"{tag}_o{i}", [P, 2 * (TC // P) * N], BF16)
                if sp.get("bias_row") is not None:
                    sp["vbias"] = nc.alloc_sbuf_tensor(f"{tag}_vb{i}", [P, N], F32)
        pool_fm = PsumPool(nc, tag + "f", 4, [P, TC])
        pool_tok = PsumPool(nc, tag + "t", 2, [P, 1024]) if any(
            sp["kind"] == "tok" for sp in specs) else None

        for sp in specs:
            if sp["kind"] == "tok" and sp.get("bias_row") is not None:
                N = sp["N"]
                ps, setg = pool_tok.get(s, t)
                s.wait(t, "ldw", LDW)
                s.wait(t, "g", 1)
                for h in range(N // 512):
                    mm = t.matmul(ps[:, h * 512:(h + 1) * 512], onesf_row[:, :],
                                  sp["brsb"][0:1, h * 512:(h + 1) * 512],
                                  start=True, stop=True)
                s.inc(mm, "tb")
                s.wait(v, "tb", s.c["tb"])
                s.inc(v.tensor_copy(sp["vbias"][:, :], ps[:, :]), "vb")
                setg("vb", s.c["vb"])

        per_chunk_t = sum((sp["N"] // P if sp["kind"] == "fm" else TC // P)
                          for sp in specs)
        nst_chunk = sum((len(sp["outs"]) if sp["kind"] == "fm" else 1)
                        for sp in specs)
        for c in range(NCH):
            b = c % 2
            tsl = slice(c * TC, (c + 1) * TC)
            ab_b = ab[:, b * KT * TC:(b + 1) * KT * TC]
            if c >= 2:
                s.wait(g, "t_c", (c - 1) * per_chunk_t)
            s.inc(g.dma_start(out=ab_b.rearrange("p (kt t) -> p kt t", kt=KT),
                              in_=_fm3(act_src, slice(None), tsl)), f"lda{b}", 16)
            nlda = 16 * (c // 2 + 1)
            st_gate = 16 * nst_chunk * (c // 2) if c >= 2 else 0

            for i, sp in enumerate(specs):
                N = sp["N"]
                if sp["kind"] == "fm":
                    NT = N // P
                    osb_b = sp["osb"][:, b * NT * TC:(b + 1) * NT * TC]
                    if sp.get("res") is not None:
                        rsb_b = sp["rsb"][:, b * NT * TC:(b + 1) * NT * TC]
                        if c >= 2:
                            s.wait(sy, f"v_res{i}", c - 1)
                        s.inc(sy.dma_start(
                            out=rsb_b.rearrange("p (nt t) -> p nt t", nt=NT),
                            in_=_fm3(sp["res"], slice(None), tsl)), f"ldr{i}{b}", 16)
                    for j in range(NT):
                        ps, setg = pool_fm.get(s, t)
                        s.wait(t, f"lda{b}", nlda)
                        s.wait(t, "ldw", LDW)
                        for kt in range(KT):
                            mm = t.matmul(ps[:, :],
                                          sp["wsb"][:, kt * N + j * P: kt * N + j * P + P],
                                          ab_b[:, kt * TC:(kt + 1) * TC],
                                          start=(kt == 0), stop=(kt == KT - 1))
                        s.inc(mm, "t_c")
                        s.wait(sc, "t_c", s.c["t_c"])
                        if c >= 2 and j == 0:
                            s.wait(sc, f"st{b}", st_gate)
                        dst = osb_b[:, j * TC:(j + 1) * TC]
                        if sp.get("bias") is not None:
                            ev = sc.add(dst, ps[:, :], sp["bsb"][:, j:j + 1])
                        elif sp.get("act_fn") is not None:
                            ev = sc.activation(dst, ps[:, :], sp["act_fn"])
                        else:
                            ev = sc.copy(dst, ps[:, :])
                        s.inc(ev, f"s_ev{i}")
                        setg(f"s_ev{i}", s.c[f"s_ev{i}"])
                    if sp.get("res") is not None:
                        s.wait(v, f"s_ev{i}", s.c[f"s_ev{i}"])
                        s.wait(v, f"ldr{i}{b}", 16 * (c // 2 + 1))
                        for j in range(NT):
                            vv = v.tensor_add(osb_b[:, j * TC:(j + 1) * TC],
                                              osb_b[:, j * TC:(j + 1) * TC],
                                              rsb_b[:, j * TC:(j + 1) * TC])
                        s.inc(v.drain(), f"v_res{i}")
                    if "osb_bf" in sp:
                        src_sem = f"v_res{i}" if sp.get("res") is not None else f"s_ev{i}"
                        s.wait(v, src_sem, s.c[src_sem])
                        for j in range(NT):
                            v.tensor_copy(
                                sp["osb_bf"][:, b * NT * TC + j * TC:
                                             b * NT * TC + (j + 1) * TC],
                                osb_b[:, j * TC:(j + 1) * TC])
                        s.inc(v.drain(), f"v_bf{i}")
                else:
                    osb_b = sp["osb"][:, b * (TC // P) * N:(b + 1) * (TC // P) * N]
                    for tt in range(TC // P):
                        ps, setg = pool_tok.get(s, t)
                        s.wait(t, f"lda{b}", nlda)
                        s.wait(t, "ldw", LDW)
                        for h in range(N // 512):
                            for kt in range(KT):
                                mm = t.matmul(
                                    ps[:, h * 512:(h + 1) * 512],
                                    ab_b[:, kt * TC + tt * P: kt * TC + (tt + 1) * P],
                                    sp["wsb"][:, kt * N + h * 512: kt * N + (h + 1) * 512],
                                    start=(kt == 0), stop=(kt == KT - 1))
                        s.inc(mm, "t_c")
                        s.wait(v, "t_c", s.c["t_c"])
                        if c == 0 and tt == 0 and "vb" in s.c:
                            s.wait(v, "vb", s.c["vb"])
                        if c >= 2 and tt == 0:
                            s.wait(v, f"st{b}", st_gate)
                        dst = osb_b[:, tt * N:(tt + 1) * N]
                        if sp.get("bias_row") is not None:
                            ev = v.tensor_add(dst, ps[:, :], sp["vbias"][:, :])
                        else:
                            ev = v.tensor_copy(dst, ps[:, :])
                        s.inc(ev, f"v_ev{i}")
                        setg(f"v_ev{i}", s.c[f"v_ev{i}"])

            for i, sp in enumerate(specs):
                N = sp["N"]
                if sp["kind"] == "fm":
                    NT = N // P
                    osb_b = sp["osb"][:, b * NT * TC:(b + 1) * NT * TC]
                    done_key = (f"v_bf{i}" if "osb_bf" in sp
                                else (f"v_res{i}" if sp.get("res") is not None
                                      else f"s_ev{i}"))
                    s.wait(g, done_key, s.c[done_key])
                    for od, odt in sp["outs"]:
                        src = (sp["osb_bf"][:, b * NT * TC:(b + 1) * NT * TC]
                               if (odt == "bf16" and "osb_bf" in sp) else osb_b)
                        s.inc(g.dma_start(
                            out=_fm3(od, slice(None), tsl),
                            in_=src.rearrange("p (nt t) -> p nt t", nt=NT)),
                            f"st{b}", 16)
                else:
                    osb_b = sp["osb"][:, b * (TC // P) * N:(b + 1) * (TC // P) * N]
                    s.wait(g, f"v_ev{i}", s.c[f"v_ev{i}"])
                    s.inc(g.dma_start(
                        out=sp["out"][tsl, :].rearrange("(tt p) n -> p tt n", p=P),
                        in_=osb_b.rearrange("p (tt n) -> p tt n", tt=TC // P)),
                        f"st{b}", 16)
        for b in range(2):
            if f"st{b}" in s.c:
                s.wait(g, f"st{b}", s.c[f"st{b}"])
        nc.all_engine_barrier()


def phase_attn0(ctx, tag, qk0, v0t, outT_d, outb_d, s0):
    nc = ctx.nc
    g, v, sc, t, sy = nc.gpsimd, nc.vector, nc.scalar, nc.tensor, nc.sync
    NH = 16
    QT = SEQ // P
    with nc.cleanup_on_exit():
        s = Sems(nc, tag)
        qsb = nc.alloc_sbuf_tensor(f"{tag}_q", [P, DT * SEQ], BF16)
        ksb = nc.alloc_sbuf_tensor(f"{tag}_k", [P, DT * SEQ], BF16)
        vsb = nc.alloc_sbuf_tensor(f"{tag}_v", [P, (SEQ // P) * D], BF16)
        ctx_sb = nc.alloc_sbuf_tensor(f"{tag}_ctx", [P, DT * SEQ], BF16)
        psb = nc.alloc_sbuf_tensor(f"{tag}_p", [P, SEQ], BF16)
        ptsb = nc.alloc_sbuf_tensor(f"{tag}_pt", [P, SEQ], BF16)
        maskc = nc.alloc_sbuf_tensor(f"{tag}_mask", [P, P], F32)
        identb = nc.alloc_sbuf_tensor(f"{tag}_id", [P, P], BF16)
        negmax = nc.alloc_sbuf_tensor(f"{tag}_nm", [P, 2], F32)
        denom = nc.alloc_sbuf_tensor(f"{tag}_dn", [P, 2], F32)
        recip = nc.alloc_sbuf_tensor(f"{tag}_rc", [P, 1], F32)
        wsb = nc.alloc_sbuf_tensor(f"{tag}_w", [P, DT * D], BF16)
        bsb = nc.alloc_sbuf_tensor(f"{tag}_b", [P, DT], F32)
        osb = nc.alloc_sbuf_tensor(f"{tag}_o", [P, 2 * DT * TC], F32)
        S_ps = nc.alloc_psum_tensor(f"{tag}_S", [P, SEQ], F32)
        PT_ps = nc.alloc_psum_tensor(f"{tag}_PT", [P, SEQ], BF16)
        cxA = nc.alloc_psum_tensor(f"{tag}_cxA", [P, P], F32)
        cxB = nc.alloc_psum_tensor(f"{tag}_cxB", [P, P], F32)
        st_reg = {"rc": 0, "last": {0: None, 1: None}}

        s.inc(g.dma_start(out=maskc[:, :], in_=ctx.inp("mask_causal", [P, P])), "ldc", 16)
        s.inc(g.dma_start(out=identb[:, :], in_=ctx.inp("identity", [P, P])), "ldc", 16)
        s.inc(g.dma_start(out=wsb[:, :].rearrange("p (kt n) -> p kt n", kt=DT),
                          in_=outT_d.rearrange("(kt p) n -> p kt n", p=P)), "ldc", 16)
        s.inc(g.dma_start(out=bsb[:, :], in_=outb_d[:, :]), "ldc", 16)
        LDC = s.c["ldc"]

        it = 0
        nst = 0
        for sq in range(NSEQ):
            ssl = slice(sq * SEQ, (sq + 1) * SEQ)
            if sq >= 1:
                s.wait(g, "st0", 16 * ((nst + 1) // 2))
                s.wait(g, "st1", 16 * (nst // 2))
            s.inc(g.dma_start(out=qsb[:, :].rearrange("p (kt t) -> p kt t", kt=DT),
                              in_=qk0.rearrange("(kt p) t -> p kt t", p=P)[:, 0:DT, ssl]),
                  f"ldq{sq % 2}", 16)
            s.inc(g.dma_start(out=ksb[:, :].rearrange("p (kt t) -> p kt t", kt=DT),
                              in_=qk0.rearrange("(kt p) t -> p kt t", p=P)[:, DT:2 * DT, ssl]),
                  f"ldq{sq % 2}", 16)
            s.inc(g.dma_start(out=vsb[:, :].rearrange("p (tt n) -> p tt n", tt=SEQ // P),
                              in_=v0t[ssl, :].rearrange("(tt p) n -> p tt n", p=P)),
                  f"ldq{sq % 2}", 16)
            nldq = 48 * (sq // 2 + 1)
            s.wait(v, f"ldq{sq % 2}", nldq)
            s.inc(v.tensor_scalar_mul(qsb[:, :], qsb[:, :], 0.125), "v_qs")

            s.wait(t, "s_op", s.c.get("s_op", 0))

            HALF = SEQ // 2
            def emit_S(iti, h, qt):
                hb = (h % 2) * 64
                qoff = (h // 2) * SEQ
                kr = (qt + 1) * P
                if kr <= HALF:
                    reg = st_reg["rc"] % 2
                    st_reg["rc"] += 1
                    off = reg * HALF
                    gate = st_reg["last"][reg]
                    regs = (reg,)
                else:
                    off = 0
                    gate = max(st_reg["last"][0], st_reg["last"][1])
                    regs = (0, 1)
                q_t = qsb[hb:hb + 64, qoff + qt * P: qoff + (qt + 1) * P]
                k_r = ksb[hb:hb + 64, qoff: qoff + kr]
                s.wait(t, f"ldq{sq % 2}", nldq)
                s.wait(t, "v_qs", sq + 1)
                s.wait(t, "ldc", LDC)
                if gate is not None:
                    s.wait(t, "s_e", gate + 1)
                for kk in range(qt + 1):
                    mm = t.matmul(S_ps[:, off + kk * P: off + (kk + 1) * P], q_t,
                                  k_r[:, kk * P:(kk + 1) * P],
                                  start=True, stop=True)
                s.inc(mm, "t_s")
                for r in regs:
                    st_reg["last"][r] = iti
                return off, gate, kr

            def emit_epi(iti, h, qt, off, gate, kr):
                hb = (h % 2) * 64
                qoff = (h // 2) * SEQ
                cxp = [cxA, cxB][iti % 2]
                s.wait(v, "t_s", iti + 1)
                if iti >= 2:
                    s.wait(v, "s_e", iti - 1)
                v.tensor_add(S_ps[:, off + qt * P: off + qt * P + P],
                             S_ps[:, off + qt * P: off + qt * P + P], maskc[:, :])
                v.drain()
                s.inc(v.reduce_max(negmax[:, iti % 2: iti % 2 + 1],
                                   S_ps[:, off: off + kr], axis=AX.X,
                                   negate=True), "v_m")
                s.wait(sc, "v_m", iti + 1)
                if gate is not None:
                    s.wait(sc, "t_pt", gate + 1)
                if iti >= 2:
                    s.wait(sc, "v_r", iti - 1)
                s.inc(sc.activation(psb[:, off: off + kr], S_ps[:, off: off + kr],
                                    AF.Exp, bias=negmax[:, iti % 2: iti % 2 + 1],
                                    accum_out=denom[:, iti % 2: iti % 2 + 1]), "s_e")
                s.wait(v, "s_e", iti + 1)
                v.reciprocal(recip[:, 0:1], denom[:, iti % 2: iti % 2 + 1])
                v.drain()
                s.inc(v.tensor_scalar_mul(psb[:, off: off + kr],
                                          psb[:, off: off + kr],
                                          recip[:, 0:1]), "v_r")
                s.wait(t, "v_r", iti + 1)
                if gate is not None:
                    s.wait(t, "s_pt", gate + 1)
                for kk in range(qt + 1):
                    tp = t.transpose(PT_ps[:, off + kk * P: off + (kk + 1) * P],
                                     psb[:, off + kk * P: off + (kk + 1) * P],
                                     identb[:, :])
                s.inc(tp, "t_pt")
                s.wait(sc, "t_pt", iti + 1)
                if gate is not None:
                    s.wait(sc, "t_av", gate + 1)
                s.inc(sc.copy(ptsb[:, off: off + kr], PT_ps[:, off: off + kr]),
                      "s_pt")
                s.wait(t, "s_pt", iti + 1)
                if iti >= 2:
                    s.wait(t, "s_cx", iti - 1)
                for kk in range(qt + 1):
                    mm = t.matmul(cxp[hb:hb + 64, :],
                                  vsb[:, kk * D + h * 64: kk * D + h * 64 + 64],
                                  ptsb[:, off + kk * P: off + (kk + 1) * P],
                                  start=(kk == 0), stop=(kk == qt),
                                  tile_position=(0, hb))
                s.inc(mm, "t_av")
                s.wait(sc, "t_av", iti + 1)
                s.inc(sc.copy(ctx_sb[hb:hb + 64, qoff + qt * P: qoff + (qt + 1) * P],
                              cxp[hb:hb + 64, :]), "s_cx")

            pend = None
            for h in range(NH):
                for qt in range(QT):
                    off, gate, kr = emit_S(it, h, qt)
                    if pend is not None:
                        emit_epi(*pend)
                    pend = (it, h, qt, off, gate, kr)
                    it += 1
            if pend is not None:
                emit_epi(*pend)
                pend = None

            for c4 in range(SEQ // TC):
                b = c4 % 2
                osb_b = osb[:, b * DT * TC:(b + 1) * DT * TC]
                csl = slice(c4 * TC, (c4 + 1) * TC)
                for j in range(DT):
                    op_n = s.c.get("s_op", 0)
                    ps_o = S_ps[:, (op_n % 2) * TC:(op_n % 2 + 1) * TC]
                    s.wait(t, "s_cx", it)
                    if op_n >= 2:
                        s.wait(t, "s_op", op_n - 1)
                    for kt in range(DT):
                        mm = t.matmul(ps_o, wsb[:, kt * D + j * P: kt * D + j * P + P],
                                      ctx_sb[:, kt * SEQ + c4 * TC: kt * SEQ + (c4 + 1) * TC],
                                      start=(kt == 0), stop=(kt == DT - 1))
                    s.inc(mm, "t_op")
                    s.wait(sc, "t_op", s.c["t_op"])
                    if nst >= 2 and j == 0:
                        s.wait(sc, f"st{nst % 2}", 16 * (nst // 2))
                    s.inc(sc.add(osb_b[:, j * TC:(j + 1) * TC], ps_o, bsb[:, j:j + 1]),
                          "s_op")
                s.wait(sy, "s_op", s.c["s_op"])
                s.inc(sy.dma_start(out=_fm3(s0, slice(None), slice(sq * SEQ + c4 * TC,
                                                                   sq * SEQ + (c4 + 1) * TC)),
                                   in_=osb_b.rearrange("p (nt t) -> p nt t", nt=DT)),
                      f"st{nst % 2}", 16)
                nst += 1
        s.wait(sy, "st0", 16 * ((nst + 1) // 2))
        s.wait(sy, "st1", 16 * (nst // 2))
        nc.all_engine_barrier()


def phase_router(ctx, tag, xn0f, wrT_d, gates_d):
    nc = ctx.nc
    g, v, sc, t, sy = nc.gpsimd, nc.vector, nc.scalar, nc.tensor, nc.sync
    NT_TOK = TOK // P
    with nc.cleanup_on_exit():
        s = Sems(nc, tag)
        xf = nc.alloc_sbuf_tensor(f"{tag}_xf", [P, 2 * DT * TC], F32)
        wr = nc.alloc_sbuf_tensor(f"{tag}_wr", [P, DT * 8], F32)
        idf = nc.alloc_sbuf_tensor(f"{tag}_id", [P, P], F32)
        probs = nc.alloc_sbuf_tensor(f"{tag}_pr", [P, 8], F32)
        gates_t = nc.alloc_sbuf_tensor(f"{tag}_gt", [P, 8], F32)
        nm8 = nc.alloc_sbuf_tensor(f"{tag}_nm", [P, 1], F32)
        dn8 = nc.alloc_sbuf_tensor(f"{tag}_dn", [P, 1], F32)
        m7 = nc.alloc_sbuf_tensor(f"{tag}_m7", [P, 1], F32)
        rs = nc.alloc_sbuf_tensor(f"{tag}_rs", [P, 1], F32)
        gsb = nc.alloc_sbuf_tensor(f"{tag}_g", [8, TOK], F32)
        pool_lg = PsumPool(nc, tag + "l", 2, [P, 8])
        pool_tr = PsumPool(nc, tag + "t", 2, [8, P])

        s.inc(g.dma_start(out=wr[:, :].rearrange("p (kt e) -> p kt e", kt=DT),
                          in_=wrT_d.rearrange("(kt p) e -> p kt e", p=P)), "ldc", 16)
        s.inc(g.dma_start(out=idf[:, :], in_=ctx.inp("identity", [P, P])), "ldc", 16)

        for tt in range(NT_TOK):
            c, b = tt // 4, (tt // 4) % 2
            if tt % 4 == 0:
                if c >= 2:
                    s.wait(sy, "t_lg", 4 * (c - 1))
                s.inc(sy.dma_start(
                    out=xf[:, b * DT * TC:(b + 1) * DT * TC].rearrange(
                        "p (kt t) -> p kt t", kt=DT),
                    in_=_fm3(xn0f, slice(None), slice(c * TC, (c + 1) * TC))),
                    f"mrd{b}", 16)
            xt_off = b * DT * TC + (tt % 4) * P
            lg, setg_lg = pool_lg.get(s, t)
            s.wait(t, f"mrd{b}", 16 * (c // 2 + 1))
            s.wait(t, "ldc", 32)
            for kt in range(DT):
                mm = t.matmul(lg[:, :], xf[:, xt_off + kt * TC: xt_off + kt * TC + P],
                              wr[:, kt * 8:(kt + 1) * 8],
                              start=(kt == 0), stop=(kt == DT - 1))
            s.inc(mm, "t_lg")
            s.wait(v, "t_lg", tt + 1)
            if tt >= 1:
                s.wait(v, "s_e8", tt)
            s.inc(v.reduce_max(nm8[:, 0:1], lg[:, :], axis=AX.X, negate=True), "v_m8")
            s.wait(sc, "v_m8", tt + 1)
            if tt >= 1:
                s.wait(sc, "v_g", tt)
            s.inc(sc.activation(probs[:, :], lg[:, :], AF.Exp, bias=nm8[:, 0:1],
                                accum_out=dn8[:, 0:1]), "s_e8")
            setg_lg("s_e8", s.c["s_e8"])
            s.wait(v, "s_e8", tt + 1)
            if tt >= 1:
                s.wait(v, "t_tr", tt)
            v.reciprocal(rs[:, 0:1], dn8[:, 0:1])
            v.drain()
            v.tensor_scalar_mul(probs[:, :], probs[:, :], rs[:, 0:1])
            v.drain()
            v.reduce_max(m7[:, 0:1], probs[:, 1:8], axis=AX.X)
            v.drain()
            v.tensor_scalar(gates_t[:, 1:8], probs[:, 1:8], m7[:, 0:1], m7[:, 0:1],
                            ALU.is_equal, ALU.mult)
            v.drain()
            v.tensor_add(rs[:, 0:1], probs[:, 0:1], m7[:, 0:1])
            v.drain()
            v.tensor_scalar_max(rs[:, 0:1], rs[:, 0:1], 1e-9)
            v.drain()
            v.reciprocal(rs[:, 0:1], rs[:, 0:1])
            v.drain()
            v.tensor_scalar_mul(gates_t[:, 1:8], gates_t[:, 1:8], rs[:, 0:1])
            v.drain()
            s.inc(v.tensor_scalar(gates_t[:, 0:1], probs[:, 0:1], rs[:, 0:1], None,
                                  ALU.mult), "v_g")
            trp, setg_tr = pool_tr.get(s, t)
            s.wait(t, "v_g", tt + 1)
            s.inc(t.transpose(trp[:, :], gates_t[:, :], idf[:, :]), "t_tr")
            s.wait(sc, "t_tr", tt + 1)
            s.inc(sc.copy(gsb[0:8, tt * P:(tt + 1) * P], trp[:, :]), "s_gt")
            setg_tr("s_gt", s.c["s_gt"])
        s.wait(g, "s_gt", NT_TOK)
        s.inc(g.dma_start(out=gates_d[:, :], in_=gsb[:, :]), "st", 16)
        s.wait(g, "st", 16)
        nc.all_engine_barrier()


def phase_moe(ctx, tag, xn0, w1T_d, w3T_d, w2T_d, gates_d, g0):
    nc = ctx.nc
    g, v, sc, t, sy = nc.gpsimd, nc.vector, nc.scalar, nc.tensor, nc.sync
    FC = 2048
    KT2 = FC // P
    NE = 8
    NROUND = NE * (4096 // FC)
    if MOE_MAX_ROUNDS is not None:
        NROUND = min(NROUND, MOE_MAX_ROUNDS)
    with nc.cleanup_on_exit():
        s = Sems(nc, tag)
        w1sb = nc.alloc_sbuf_tensor(f"{tag}_w1", [P, DT * FC], BF16)
        w3sb = nc.alloc_sbuf_tensor(f"{tag}_w3", [P, DT * FC], BF16)
        w2sb = nc.alloc_sbuf_tensor(f"{tag}_w2", [P, KT2 * D], BF16)
        xb = nc.alloc_sbuf_tensor(f"{tag}_x", [P, 2 * DT * TC], BF16)
        hsb = nc.alloc_sbuf_tensor(f"{tag}_h", [P, KT2 * TC], BF16)
        s1a = nc.alloc_sbuf_tensor(f"{tag}_s1a", [P, TC], BF16)
        s1b = nc.alloc_sbuf_tensor(f"{tag}_s1b", [P, TC], BF16)
        grow = nc.alloc_sbuf_tensor(f"{tag}_gr", [1, TOK], F32)
        onesr = nc.alloc_sbuf_tensor(f"{tag}_or", [1, P], F32)
        osb = nc.alloc_sbuf_tensor(f"{tag}_o", [P, 2 * DT * TC], F32)
        pp1 = PsumPool(nc, tag + "1", 2, [P, TC])
        pp3 = PsumPool(nc, tag + "3", 2, [P, TC])
        ppb = PsumPool(nc, tag + "b", 2, [P, TC])
        ppo = PsumPool(nc, tag + "o", 2, [P, TC])
        s.inc(g.memset(onesr[:, :], 1.0), "g")

        gidx = 0
        for r in range(NROUND):
            e, half = r // 2, r % 2
            fsl = slice(half * FC, (half + 1) * FC)
            if r >= 1:
                s.wait(g, "t_o", 8 * 16 * r)
            s.inc(g.dma_start(out=w1sb[:, :].rearrange("p (kt n) -> p kt n", kt=DT),
                              in_=w1T_d[e][:, fsl].rearrange("(kt p) n -> p kt n", p=P)),
                  "ldw", 16)
            s.inc(g.dma_start(out=w3sb[:, :].rearrange("p (kt n) -> p kt n", kt=DT),
                              in_=w3T_d[e][:, fsl].rearrange("(kt p) n -> p kt n", p=P)),
                  "ldw", 16)
            s.inc(g.dma_start(out=w2sb[:, :].rearrange("p (kt n) -> p kt n", kt=KT2),
                              in_=w2T_d[e][fsl, :].rearrange("(kt p) n -> p kt n", p=P)),
                  "ldw", 16)
            if r >= 1:
                s.wait(sy, "t_bc", 16 * r)
            s.inc(sy.dma_start(out=grow[:, :], in_=gates_d[e:e + 1, :]), "ldg", 16)
            NLDW = 48 * (r + 1)

            for c in range(NCH):
                b = c % 2
                tsl = slice(c * TC, (c + 1) * TC)
                xb_b = xb[:, b * DT * TC:(b + 1) * DT * TC]
                if gidx >= 2:
                    s.wait(g, "t_p", 16 * (gidx - 1))
                s.inc(g.dma_start(out=xb_b.rearrange("p (kt t) -> p kt t", kt=DT),
                                  in_=_fm3(xn0, slice(None), tsl)), f"xl{b}", 16)
                nxl = 16 * (gidx // 2 + 1)
                bc, setg_bc = ppb.get(s, t)
                s.wait(t, "ldg", 16 * (r + 1))
                s.wait(t, "g", 1)
                s.inc(t.matmul(bc[:, :], onesr[:, :], grow[0:1, tsl],
                               start=True, stop=True), "t_bc")
                for j in range(KT2):
                    p1, setg_1 = pp1.get(s, t)
                    s.wait(t, f"xl{b}", nxl)
                    s.wait(t, "ldw", NLDW)
                    for kt in range(DT):
                        t.matmul(p1[:, :], w1sb[:, kt * FC + j * P: kt * FC + j * P + P],
                                 xb_b[:, kt * TC:(kt + 1) * TC],
                                 start=(kt == 0), stop=(kt == DT - 1))
                    p3, setg_3 = pp3.get(s, t)
                    for kt in range(DT):
                        mm = t.matmul(p3[:, :], w3sb[:, kt * FC + j * P: kt * FC + j * P + P],
                                      xb_b[:, kt * TC:(kt + 1) * TC],
                                      start=(kt == 0), stop=(kt == DT - 1))
                    s.inc(mm, "t_p")
                    njh = s.c["t_p"]
                    s1t = [s1a, s1b][j % 2]
                    s.wait(sc, "t_p", njh)
                    if njh >= 2:
                        s.wait(sc, "v_h", njh - 1)
                    s.inc(sc.activation(s1t[:, :], p1[:, :], AF.Silu), "s_h")
                    setg_1("s_h", s.c["s_h"])
                    s.wait(v, "s_h", njh)
                    if j == 0 and gidx >= 1:
                        s.wait(v, "t_o", 8 * gidx)
                    hj = hsb[:, j * TC:(j + 1) * TC]
                    v.tensor_mul(hj, s1t[:, :], p3[:, :])
                    v.drain()
                    s.inc(v.tensor_mul(hj, hj, bc[:, :]), "v_h")
                    setg_3("v_h", s.c["v_h"])
                setg_bc("v_h", s.c["v_h"])
                for jn in range(DT):
                    po, setg_o = ppo.get(s, t)
                    s.wait(t, "v_h", 16 * (gidx + 1))
                    for kk in range(KT2):
                        mm = t.matmul(po[:, :], w2sb[:, kk * D + jn * P: kk * D + jn * P + P],
                                      hsb[:, kk * TC:(kk + 1) * TC],
                                      start=(kk == 0), stop=(kk == KT2 - 1))
                    s.inc(mm, "t_o")
                    s.wait(sc, "t_o", s.c["t_o"])
                    if gidx >= 2 and jn == 0:
                        s.wait(sc, f"acc{b}", 16 * (gidx // 2))
                    s.inc(sc.copy(osb[:, b * DT * TC + jn * TC: b * DT * TC + (jn + 1) * TC],
                                  po[:, :]), "s_o")
                    setg_o("s_o", s.c["s_o"])
                s.wait(g, "s_o", 8 * (gidx + 1))
                if c == 0 and r >= 1:
                    s.wait(g, "acc0", 16 * 8 * r)
                    s.wait(g, "acc1", 16 * 8 * r)
                if r == 0:
                    dma = g.dma_start(out=_fm3(g0, slice(None), tsl),
                                      in_=osb[:, b * DT * TC:(b + 1) * DT * TC]
                                      .rearrange("p (nt t) -> p nt t", nt=DT))
                else:
                    dma = g.dma_start(out=_fm3(g0, slice(None), tsl),
                                      in_=osb[:, b * DT * TC:(b + 1) * DT * TC]
                                      .rearrange("p (nt t) -> p nt t", nt=DT),
                                      accum_op=ALU.add)
                s.inc(dma, f"acc{b}", 16)
                gidx += 1
        s.wait(g, "acc0", 16 * 8 * NROUND)
        s.wait(g, "acc1", 16 * 8 * NROUND)
        nc.all_engine_barrier()


def phase_combine(ctx, tag, base_d, s_d, gpre_d, out_d):
    nc = ctx.nc
    g, v, sc, t, sy = nc.gpsimd, nc.vector, nc.scalar, nc.tensor, nc.sync
    W = DT * TC
    with nc.cleanup_on_exit():
        s = Sems(nc, tag)
        bsb = nc.alloc_sbuf_tensor(f"{tag}_b", [P, 2 * W], F32)
        ssb = nc.alloc_sbuf_tensor(f"{tag}_s", [P, 2 * W], F32)
        gsb = nc.alloc_sbuf_tensor(f"{tag}_g", [P, 2 * W], F32)
        for c in range(NCH):
            b = c % 2
            tsl = slice(c * TC, (c + 1) * TC)
            bb = bsb[:, b * W:(b + 1) * W]
            sb = ssb[:, b * W:(b + 1) * W]
            gb = gsb[:, b * W:(b + 1) * W]
            if c >= 2:
                s.wait(sy, "v_c", c - 1)
                s.wait(sy, f"st{b}", 16 * (c // 2))
                s.wait(g, "v_c", c - 1)
            s.inc(sy.dma_start(out=bb.rearrange("p (kt t) -> p kt t", kt=DT),
                               in_=_fm3(base_d, slice(None), tsl)), f"lb{b}", 16)
            s.inc(sy.dma_start(out=sb.rearrange("p (kt t) -> p kt t", kt=DT),
                               in_=_fm3(s_d, slice(None), tsl)), f"lb{b}", 16)
            s.inc(g.dma_start(out=gb.rearrange("p (kt t) -> p kt t", kt=DT),
                              in_=_fm3(gpre_d, slice(None), tsl)), f"lg{b}", 16)
            s.wait(sc, f"lg{b}", 16 * (c // 2 + 1))
            if c >= 2:
                s.wait(sc, "v_c", c - 1)
            s.inc(sc.activation(gb, gb, AF.Silu), "s_si")
            s.wait(v, "s_si", c + 1)
            s.wait(v, f"lb{b}", 32 * (c // 2 + 1))
            v.tensor_mul(sb, sb, gb)
            v.drain()
            v.tensor_add(bb, bb, sb)
            s.inc(v.drain(), "v_c")
            s.wait(g, "v_c", c + 1)
            s.inc(g.dma_start(out=_fm3(out_d, slice(None), tsl),
                              in_=bb.rearrange("p (kt t) -> p kt t", kt=DT)),
                  f"st{b}", 16)
        s.wait(g, "st0", 16 * (NCH // 2))
        s.wait(g, "st1", 16 * (NCH // 2))
        nc.all_engine_barrier()


def phase_attn1(ctx, tag, q1l, k1l, v1lt, xn1f, wol_d, wqg_d, wkg_d, wvg_d, wog_d, s1):
    nc = ctx.nc
    g, v, sc, t, sy = nc.gpsimd, nc.vector, nc.scalar, nc.tensor, nc.sync
    CH = 256
    NCHK = TOK // CH
    NH = 16
    with nc.cleanup_on_exit():
        s = Sems(nc, tag)
        qsb = nc.alloc_sbuf_tensor(f"{tag}_q", [P, 2 * DT * CH], BF16)
        ksb = nc.alloc_sbuf_tensor(f"{tag}_k", [P, 2 * DT * CH], BF16)
        vsb = nc.alloc_sbuf_tensor(f"{tag}_v", [P, 2 * (CH // P) * D], BF16)
        xfb = nc.alloc_sbuf_tensor(f"{tag}_xf", [P, 2 * DT * CH], F32)
        ctx_sb = nc.alloc_sbuf_tensor(f"{tag}_ctx", [P, DT * CH], BF16)
        psb = nc.alloc_sbuf_tensor(f"{tag}_p", [P, CH], BF16)
        ptsb = nc.alloc_sbuf_tensor(f"{tag}_pt", [P, CH], BF16)
        identb = nc.alloc_sbuf_tensor(f"{tag}_id", [P, P], BF16)
        negmax = nc.alloc_sbuf_tensor(f"{tag}_nm", [P, 1], F32)
        denom = nc.alloc_sbuf_tensor(f"{tag}_dn", [P, 1], F32)
        recip = nc.alloc_sbuf_tensor(f"{tag}_rc", [P, 1], F32)
        wsb = nc.alloc_sbuf_tensor(f"{tag}_wo", [P, DT * D], BF16)
        osb = nc.alloc_sbuf_tensor(f"{tag}_o", [P, 2 * DT * CH], F32)
        summ = nc.alloc_sbuf_tensor(f"{tag}_sm", [P, DT * NCHK], F32)
        summb = nc.alloc_sbuf_tensor(f"{tag}_smb", [P, DT * NCHK], BF16)
        wg = nc.alloc_sbuf_tensor(f"{tag}_wg", [P, 3 * DT * D], BF16)
        wvg = nc.alloc_sbuf_tensor(f"{tag}_wvg", [P, DT * D], BF16)
        qg = nc.alloc_sbuf_tensor(f"{tag}_qg", [P, DT * NCHK], BF16)
        kg = nc.alloc_sbuf_tensor(f"{tag}_kg", [P, DT * NCHK], BF16)
        vg = [nc.alloc_sbuf_tensor(f"{tag}_vg{s_}", [8, D], BF16) for s_ in range(NSEQ)]
        cg = nc.alloc_sbuf_tensor(f"{tag}_cg", [P, DT * NCHK], BF16)
        go = nc.alloc_sbuf_tensor(f"{tag}_go", [P, DT * NCHK], F32)
        pg = nc.alloc_sbuf_tensor(f"{tag}_pg", [8, 8], BF16)
        S_ps = nc.alloc_psum_tensor(f"{tag}_S", [P, CH], F32)
        PT_ps = nc.alloc_psum_tensor(f"{tag}_PT", [P, CH], BF16)
        cx_ps = nc.alloc_psum_tensor(f"{tag}_cx", [P, P], F32)
        op_ps = PsumPool(nc, tag + "o", 2, [P, CH])
        vg_ps = nc.alloc_psum_tensor(f"{tag}_vg", [8, D], F32)

        s.inc(g.dma_start(out=identb[:, :], in_=ctx.inp("identity", [P, P])), "ldc", 16)
        s.inc(g.dma_start(out=wsb[:, :].rearrange("p (kt n) -> p kt n", kt=DT),
                          in_=wol_d.rearrange("(kt p) n -> p kt n", p=P)), "ldc", 16)
        for wi, wd in enumerate([wqg_d, wkg_d, wog_d]):
            s.inc(g.dma_start(out=wg[:, wi * DT * D:(wi + 1) * DT * D]
                              .rearrange("p (kt n) -> p kt n", kt=DT),
                              in_=wd.rearrange("(kt p) n -> p kt n", p=P)), "ldc", 16)
        s.inc(g.dma_start(out=wvg[:, :].rearrange("p (kt n) -> p kt n", kt=DT),
                          in_=wvg_d.rearrange("(kt p) n -> p kt n", p=P)), "ldc", 16)
        LDC = s.c["ldc"]

        it = 0
        nst = 0
        for ci in range(NCHK):
            cb = ci % 2
            csl = slice(ci * CH, (ci + 1) * CH)
            qb = qsb[:, cb * DT * CH:(cb + 1) * DT * CH]
            kb = ksb[:, cb * DT * CH:(cb + 1) * DT * CH]
            vb = vsb[:, cb * (CH // P) * D:(cb + 1) * (CH // P) * D]
            xb = xfb[:, cb * DT * CH:(cb + 1) * DT * CH]
            if ci >= 2:
                s.wait(g, "t_av", (ci - 1) * NH * 2)
                s.wait(g, "v_res", ci - 1)
            s.inc(g.dma_start(out=qb.rearrange("p (kt t) -> p kt t", kt=DT),
                              in_=_fm3(q1l, slice(None), csl)), f"lq{cb}", 16)
            s.inc(g.dma_start(out=kb.rearrange("p (kt t) -> p kt t", kt=DT),
                              in_=_fm3(k1l, slice(None), csl)), f"lq{cb}", 16)
            s.inc(g.dma_start(out=vb.rearrange("p (tt n) -> p tt n", tt=CH // P),
                              in_=v1lt[csl, :].rearrange("(tt p) n -> p tt n", p=P)),
                  f"lq{cb}", 16)
            s.inc(g.dma_start(out=xb.rearrange("p (kt t) -> p kt t", kt=DT),
                              in_=_fm3(xn1f, slice(None), csl)), f"lq{cb}", 16)
            nlq = 64 * (ci // 2 + 1)
            s.wait(v, f"lq{cb}", nlq)
            if ci >= 2:
                s.wait(v, "t_s", (ci - 1) * NH * 2)
            s.inc(v.tensor_scalar_mul(qb[:, :], qb[:, :], 0.125), "v_qs")

            for h in range(NH):
                hb = (h % 2) * 64
                kt_h = h // 2
                qoff = kt_h * CH
                for qt in range(CH // P):
                    q_t = qb[hb:hb + 64, qoff + qt * P: qoff + (qt + 1) * P]
                    s.wait(t, "v_qs", ci + 1)
                    s.wait(t, "ldc", LDC)
                    if it >= 1:
                        s.wait(t, "s_e", it)
                    for kk in range(CH // P):
                        mm = t.matmul(S_ps[:, kk * P:(kk + 1) * P], q_t,
                                      kb[hb:hb + 64, qoff + kk * P: qoff + (kk + 1) * P],
                                      start=True, stop=True)
                    s.inc(mm, "t_s")
                    s.wait(v, "t_s", it + 1)
                    s.inc(v.reduce_max(negmax[:, 0:1], S_ps[:, :], axis=AX.X,
                                       negate=True), "v_m")
                    s.wait(sc, "v_m", it + 1)
                    if it >= 1:
                        s.wait(sc, "t_pt", it)
                    s.inc(sc.activation(psb[:, :], S_ps[:, :], AF.Exp,
                                        bias=negmax[:, 0:1],
                                        accum_out=denom[:, 0:1]), "s_e")
                    s.wait(v, "s_e", it + 1)
                    v.reciprocal(recip[:, 0:1], denom[:, 0:1])
                    v.drain()
                    s.inc(v.tensor_scalar_mul(psb[:, :], psb[:, :], recip[:, 0:1]), "v_r")
                    s.wait(t, "v_r", it + 1)
                    if it >= 1:
                        s.wait(t, "s_pt", it)
                    for kk in range(CH // P):
                        tp = t.transpose(PT_ps[:, kk * P:(kk + 1) * P],
                                         psb[:, kk * P:(kk + 1) * P], identb[:, :])
                    s.inc(tp, "t_pt")
                    s.wait(sc, "t_pt", it + 1)
                    if it >= 1:
                        s.wait(sc, "t_av", it)
                    s.inc(sc.copy(ptsb[:, :], PT_ps[:, :]), "s_pt")
                    s.wait(t, "s_pt", it + 1)
                    if it >= 1:
                        s.wait(t, "s_cx", it)
                    for kk in range(CH // P):
                        mm = t.matmul(cx_ps[hb:hb + 64, :],
                                      vb[:, kk * D + h * 64: kk * D + h * 64 + 64],
                                      ptsb[:, kk * P:(kk + 1) * P],
                                      start=(kk == 0), stop=(kk == CH // P - 1),
                                      tile_position=(0, hb))
                    s.inc(mm, "t_av")
                    s.wait(sc, "t_av", it + 1)
                    s.inc(sc.copy(ctx_sb[hb:hb + 64, qoff + qt * P: qoff + (qt + 1) * P],
                                  cx_ps[hb:hb + 64, :]), "s_cx")
                    it += 1

            ob = osb[:, cb * DT * CH:(cb + 1) * DT * CH]
            for j in range(DT):
                po, setg_o = op_ps.get(s, t)
                s.wait(t, "s_cx", it)
                for kt in range(DT):
                    mm = t.matmul(po[:, :], wsb[:, kt * D + j * P: kt * D + j * P + P],
                                  ctx_sb[:, kt * CH:(kt + 1) * CH],
                                  start=(kt == 0), stop=(kt == DT - 1))
                s.inc(mm, "t_op")
                s.wait(sc, "t_op", s.c["t_op"])
                if ci >= 2 and j == 0:
                    s.wait(sc, f"st{cb}", 16 * (ci // 2))
                s.inc(sc.copy(ob[:, j * CH:(j + 1) * CH], po[:, :]), "s_op")
                setg_o("s_op", s.c["s_op"])
            s.wait(v, "s_op", s.c["s_op"])
            for j in range(DT):
                v.tensor_add(ob[:, j * CH:(j + 1) * CH], ob[:, j * CH:(j + 1) * CH],
                             xb[:, j * CH:(j + 1) * CH])
            s.inc(v.drain(), "v_res")
            for j in range(DT):
                v.reduce_sum(summ[:, j * NCHK + ci: j * NCHK + ci + 1],
                             ob[:, j * CH:(j + 1) * CH], axis=AX.X)
            s.inc(v.drain(), "v_sum")
            s.wait(sy, "v_res", ci + 1)
            s.inc(sy.dma_start(out=_fm3(s1, slice(None), csl),
                               in_=ob.rearrange("p (kt t) -> p kt t", kt=DT)),
                  f"st{cb}", 16)
            nst += 1

        s.wait(v, "v_sum", NCHK)
        v.tensor_scalar_mul(summ[:, :], summ[:, :], 1.0 / CH)
        v.drain()
        s.inc(v.tensor_copy(summb[:, :], summ[:, :]), "v_gb")
        for oi, (woff, dst) in enumerate([(0, qg), (DT * D, kg)]):
            for j in range(DT):
                po, setg_o = op_ps.get(s, t)
                s.wait(t, "v_gb", 1)
                for kt in range(DT):
                    mm = t.matmul(po[:, 0:NCHK],
                                  wg[:, woff + kt * D + j * P: woff + kt * D + j * P + P],
                                  summb[:, kt * NCHK:(kt + 1) * NCHK],
                                  start=(kt == 0), stop=(kt == DT - 1))
                s.inc(mm, "t_g")
                s.wait(v, "t_g", s.c["t_g"])
                s.inc(v.tensor_copy(dst[:, j * NCHK:(j + 1) * NCHK], po[:, 0:NCHK]),
                      "v_gq")
                setg_o("v_gq", s.c["v_gq"])
        s.wait(v, "v_gq", s.c["v_gq"])
        s.inc(v.tensor_scalar_mul(qg[:, :], qg[:, :], 0.125), "v_gq2")
        for sq in range(NSEQ):
            s.wait(t, "v_gb", 1)
            if sq >= 1:
                s.wait(t, "v_gq", 16 + sq)
            for half in range(2):
                for kt in range(DT):
                    mm = t.matmul(vg_ps[0:8, half * 512:(half + 1) * 512],
                                  summb[:, kt * NCHK + 8 * sq: kt * NCHK + 8 * sq + 8],
                                  wvg[:, kt * D + half * 512: kt * D + (half + 1) * 512],
                                  start=(kt == 0), stop=(kt == DT - 1))
            s.inc(mm, "t_g")
            s.wait(v, "t_g", s.c["t_g"])
            s.inc(v.tensor_copy(vg[sq][0:8, :], vg_ps[0:8, 0:D]), "v_gq")

        git = 0
        for sq in range(NSEQ):
            for h in range(NH):
                hb = (h % 2) * 64
                kt_h = h // 2
                coff = kt_h * NCHK + 8 * sq
                s.wait(t, "v_gq2", 1)
                s.wait(t, "v_gq", 16 + NSEQ)
                if git == 0:
                    s.wait(t, "s_e", it)
                    s.wait(t, "s_pt", it)
                    s.wait(t, "s_cx", it)
                else:
                    s.wait(t, "s_ge", git)
                mm = t.matmul(S_ps[0:8, 0:8], qg[hb:hb + 64, coff:coff + 8],
                              kg[hb:hb + 64, coff:coff + 8], start=True, stop=True)
                s.inc(mm, "t_gs")
                s.wait(v, "t_gs", git + 1)
                s.inc(v.reduce_max(negmax[0:8, 0:1], S_ps[0:8, 0:8], axis=AX.X,
                                   negate=True), "v_gm")
                s.wait(sc, "v_gm", git + 1)
                if git >= 1:
                    s.wait(sc, "t_gav", git)
                s.inc(sc.activation(pg[0:8, 0:8], S_ps[0:8, 0:8], AF.Exp,
                                    bias=negmax[0:8, 0:1],
                                    accum_out=denom[0:8, 0:1]), "s_ge")
                s.wait(v, "s_ge", git + 1)
                v.reciprocal(recip[0:8, 0:1], denom[0:8, 0:1])
                v.drain()
                s.inc(v.tensor_scalar_mul(pg[0:8, 0:8], pg[0:8, 0:8],
                                          recip[0:8, 0:1]), "v_gr")
                s.wait(t, "v_gr", git + 1)
                if git >= 1:
                    s.wait(t, "s_gpt", git)
                s.inc(t.transpose(PT_ps[0:8, 0:8], pg[0:8, 0:8], identb[0:8, 0:8]),
                      "t_gpt")
                s.wait(sc, "t_gpt", git + 1)
                if git >= 1:
                    s.wait(sc, "t_gav", git)
                s.inc(sc.copy(pg[0:8, 0:8], PT_ps[0:8, 0:8]), "s_gpt")
                s.wait(t, "s_gpt", git + 1)
                if git >= 1:
                    s.wait(t, "s_gcx", git)
                s.inc(t.matmul(cx_ps[hb:hb + 64, 0:8], vg[sq][0:8, h * 64:(h + 1) * 64],
                               pg[0:8, 0:8], start=True, stop=True,
                               tile_position=(0, hb)), "t_gav")
                s.wait(sc, "t_gav", git + 1)
                s.inc(sc.copy(cg[hb:hb + 64, coff:coff + 8], cx_ps[hb:hb + 64, 0:8]),
                      "s_gcx")
                git += 1
        for j in range(DT):
            po, setg_o = op_ps.get(s, t)
            s.wait(t, "s_gcx", git)
            for kt in range(DT):
                mm = t.matmul(po[:, 0:NCHK],
                              wg[:, 2 * DT * D + kt * D + j * P: 2 * DT * D + kt * D + j * P + P],
                              cg[:, kt * NCHK:(kt + 1) * NCHK],
                              start=(kt == 0), stop=(kt == DT - 1))
            s.inc(mm, "t_go")
            s.wait(v, "t_go", s.c["t_go"])
            s.inc(v.tensor_add(go[:, j * NCHK:(j + 1) * NCHK], po[:, 0:NCHK],
                               summ[:, j * NCHK:(j + 1) * NCHK]), "v_go")
            setg_o("v_go", s.c["v_go"])

        for ci in range(NCHK):
            cb = ci % 2
            csl = slice(ci * CH, (ci + 1) * CH)
            ob = osb[:, cb * DT * CH:(cb + 1) * DT * CH]
            if ci >= 2:
                s.wait(sy, "v_up", ci - 1)
                s.wait(sy, f"su{cb}", 16 * (ci // 2))
            s.wait(sy, f"st{cb}", 16 * (NCHK // 2))
            s.inc(sy.dma_start(out=ob.rearrange("p (kt t) -> p kt t", kt=DT),
                               in_=_fm3(s1, slice(None), csl)), f"lu{cb}", 16)
            s.wait(v, f"lu{cb}", 16 * (ci // 2 + 1))
            s.wait(v, "v_go", DT)
            if ci >= 2:
                s.wait(v, f"su{cb}", 16 * (ci // 2))
            for j in range(DT):
                v.tensor_scalar_add(ob[:, j * CH:(j + 1) * CH],
                                    ob[:, j * CH:(j + 1) * CH],
                                    go[:, j * NCHK + ci: j * NCHK + ci + 1])
            s.inc(v.drain(), "v_up")
            s.wait(sy, "v_up", ci + 1)
            s.inc(sy.dma_start(out=_fm3(s1, slice(None), csl),
                               in_=ob.rearrange("p (kt t) -> p kt t", kt=DT)),
                  f"su{cb}", 16)
        s.wait(sy, "su0", 16 * (NCHK // 2))
        s.wait(sy, "su1", 16 * (NCHK // 2))
        nc.all_engine_barrier()


def phase_shiftqkv(ctx, tag, x_l, wqkv_d, qs_d, ks_d, qw, kw, vwt):
    nc = ctx.nc
    g, v, sc, t, sy = nc.gpsimd, nc.vector, nc.scalar, nc.tensor, nc.sync
    NQ = 512
    NT = NQ // P
    eff_level = 3 if SQKV_EPI_TILES < 10**9 else SQKV_LEVEL
    with nc.cleanup_on_exit():
        s = Sems(nc, tag)
        hb = nc.alloc_sbuf_tensor(f"{tag}_h", [P, 2 * DT * TC], BF16)
        wsb = nc.alloc_sbuf_tensor(f"{tag}_w", [P, DT * 1536], BF16)
        q2sb = nc.alloc_sbuf_tensor(f"{tag}_q2", [P, 2 * TC], BF16)
        qraw = nc.alloc_sbuf_tensor(f"{tag}_qr", [P, 2 * TC], BF16)
        qnsb = nc.alloc_sbuf_tensor(f"{tag}_qn", [P, 2 * TC], BF16)
        t1sb = nc.alloc_sbuf_tensor(f"{tag}_t1", [P, TC], BF16)
        t2sb = nc.alloc_sbuf_tensor(f"{tag}_t2", [P, TC], BF16)

        qscol = nc.alloc_sbuf_tensor(f"{tag}_qs", [P, 1], F32)
        kscol = nc.alloc_sbuf_tensor(f"{tag}_ks", [P, 1], F32)
        hs2 = nc.alloc_sbuf_tensor(f"{tag}_hs", [P, 2], BF16)
        Rb = nc.alloc_sbuf_tensor(f"{tag}_R", [P, P], BF16)
        ones64 = nc.alloc_sbuf_tensor(f"{tag}_o64", [1, 64], BF16)
        na = nc.alloc_sbuf_tensor(f"{tag}_na", [1, TC], F32)
        nb = nc.alloc_sbuf_tensor(f"{tag}_nb", [1, TC], F32)
        nab = nc.alloc_sbuf_tensor(f"{tag}_nab", [1, TC], BF16)
        nbb = nc.alloc_sbuf_tensor(f"{tag}_nbb", [1, TC], BF16)
        tabs = nc.alloc_sbuf_tensor(f"{tag}_tab", [P, 4 * SEQ], F32)
        qout = nc.alloc_sbuf_tensor(f"{tag}_qo", [P, 2 * NT * TC], BF16)
        kout = nc.alloc_sbuf_tensor(f"{tag}_ko", [P, 2 * NT * TC], BF16)
        vout = nc.alloc_sbuf_tensor(f"{tag}_vo", [P, 2 * (TC // P) * NQ], BF16)
        pq = PsumPool(nc, tag + "q", 2, [P, TC])
        ptok = PsumPool(nc, tag + "t", 2, [P, NQ])
        nps_a = nc.alloc_psum_tensor(f"{tag}_npa", [1, TC], F32)
        nps_b = nc.alloc_psum_tensor(f"{tag}_npb", [1, TC], F32)
        bc_ps = nc.alloc_psum_tensor(f"{tag}_bc", [P, TC], F32)
        rot_ps = nc.alloc_psum_tensor(f"{tag}_ro", [P, TC], F32)

        s.inc(g.dma_start(out=wsb[:, :].rearrange("p (kt n) -> p kt n", kt=DT),
                          in_=wqkv_d.rearrange("(kt p) n -> p kt n", p=P)), "ldc", 16)
        s.inc(g.dma_start(out=qscol[:, :], in_=qs_d), "ldc", 16)
        s.inc(g.dma_start(out=kscol[:, :], in_=ks_d), "ldc", 16)
        s.inc(g.dma_start(out=hs2[:, :], in_=ctx.inp("headsum2", [P, 2])), "ldc", 16)
        s.inc(g.dma_start(out=Rb[:, :], in_=ctx.inp("rothalf", [P, P])), "ldc", 16)
        s.inc(g.memset(ones64[:, :], 1.0), "ldc_g")
        for ti, tn in enumerate(["cs_q", "ss_q", "cs_k", "ss_k"]):
            s.inc(g.dma_start(out=tabs[:, ti * SEQ:(ti + 1) * SEQ],
                              in_=ctx.inp(tn, [P, SEQ])), "ldc", 16)
        LDC = s.c["ldc"]

        xv = x_l.rearrange("(kt p) t -> p kt t", p=P)
        zv = ctx.inp("zeros_col", [D, 1]).rearrange("(kt p) z -> p kt z", p=P)
        jq = 0
        ne = 0
        nst = 0
        for c in range(NCH):
            b = c % 2
            tsl = slice(c * TC, (c + 1) * TC)
            toff = (c % 4) * TC
            hb_b = hb[:, b * DT * TC:(b + 1) * DT * TC]
            if c >= 2:
                s.wait(g, "t_qkv", (c - 1) * (2 * NT + (0 if SQKV_NO_V else NT)))
            s.inc(g.dma_start(out=hb_b.rearrange("p (kt t) -> p kt t", kt=DT)[:, 0:4, :],
                              in_=xv[:, 0:4, tsl]), f"lh{b}", 16)
            if c % 4 == 0:
                s.inc(g.dma_start(
                    out=hb_b.rearrange("p (kt t) -> p kt t", kt=DT)[:, 4:8, 1:TC],
                    in_=xv[:, 4:8, c * TC:(c + 1) * TC - 1]), f"lh{b}", 16)
                s.inc(g.dma_start(
                    out=hb_b.rearrange("p (kt t) -> p kt t", kt=DT)[:, 4:8, 0:1],
                    in_=zv[:, 4:8, :], allow_slow_non_contiguous=True), f"lh{b}", 16)
            else:
                s.inc(g.dma_start(
                    out=hb_b.rearrange("p (kt t) -> p kt t", kt=DT)[:, 4:8, :],
                    in_=xv[:, 4:8, c * TC - 1:(c + 1) * TC - 1]), f"lh{b}", 16)
            nlh = s.c[f"lh{b}"]

            for part in range(2):
                woff = part * NQ
                scol = [qscol, kscol][part]
                cs_t = tabs[:, (2 * part) * SEQ + toff:(2 * part) * SEQ + toff + TC]
                ss_t = tabs[:, (2 * part + 1) * SEQ + toff:(2 * part + 1) * SEQ + toff + TC]
                outb = [qout, kout][part]
                outd = [qw, kw][part]
                for j in range(NT):
                    jb = jq % 2
                    ps, setg_ps = pq.get(s, t)
                    s.wait(t, f"lh{b}", nlh)
                    s.wait(t, "ldc", LDC)
                    for kt in range(DT):
                        mm = t.matmul(ps[:, :],
                                      wsb[:, kt * 1536 + woff + j * P: kt * 1536 + woff + j * P + P],
                                      hb_b[:, kt * TC:(kt + 1) * TC],
                                      start=(kt == 0), stop=(kt == DT - 1))
                    s.inc(mm, "t_qkv")
                    if SQKV_LEVEL >= 4 or jq >= SQKV_EPI_TILES:
                        s.wait(v, "t_qkv", s.c["t_qkv"])
                        s.inc(v.tensor_copy(outb[:, b * NT * TC + j * TC:
                                                 b * NT * TC + (j + 1) * TC], ps[:, :]),
                              "v_fin")
                        setg_ps("v_fin", s.c["v_fin"])
                        jq += 1
                        continue
                    jb = ne % 2
                    s.wait(v, "t_qkv", s.c["t_qkv"])
                    if ne >= 2:
                        s.wait(v, "v_qn" if eff_level in (0, 1, 2) else "v_fin2",
                               ne - 1)
                        s.wait(v, "t_nrm", ne - 1)
                    v.tensor_copy(qraw[:, jb * TC:(jb + 1) * TC], ps[:, :])
                    s.inc(v.drain(), "v_cp")
                    s.inc(v.tensor_mul(q2sb[:, jb * TC:(jb + 1) * TC],
                                       qraw[:, jb * TC:(jb + 1) * TC],
                                       qraw[:, jb * TC:(jb + 1) * TC]), "s_sq")
                    setg_ps(("v_cp", s.c["v_cp"]),)
                    s.wait(t, "s_sq", ne + 1)
                    if ne >= 1:
                        s.wait(t, "s_rt", ne)
                    t.matmul(nps_a[:, :], hs2[:, 0:1],
                             q2sb[:, jb * TC:(jb + 1) * TC], start=True, stop=True)
                    s.inc(t.matmul(nps_b[:, :], hs2[:, 1:2],
                                   q2sb[:, jb * TC:(jb + 1) * TC],
                                   start=True, stop=True), "t_nrm")
                    s.wait(sc, "t_nrm", ne + 1)
                    if ne >= 1:
                        s.wait(sc, "v_rn", ne)
                    sc.activation(na[:, :], nps_a[:, :], AF.Sqrt)
                    s.inc(sc.activation(nb[:, :], nps_b[:, :], AF.Sqrt), "s_rt")
                    s.wait(v, "s_rt", ne + 1)
                    if ne >= 1 and eff_level in (0, 1, 2):
                        s.wait(v, "t_bc", ne)
                    v.tensor_scalar_max(na[:, :], na[:, :], 1e-12)
                    v.tensor_scalar_max(nb[:, :], nb[:, :], 1e-12)
                    v.drain()
                    v.reciprocal(na[:, :], na[:, :])
                    v.reciprocal(nb[:, :], nb[:, :])
                    v.drain()
                    v.tensor_copy(nab[:, :], na[:, :])
                    s.inc(v.tensor_copy(nbb[:, :], nb[:, :]), "v_rn")
                    if SQKV_LEVEL == 3 or SQKV_EPI_TILES < 10**9:
                        s.wait(v, "v_cp", s.c["v_cp"])
                        vc = v.tensor_copy(outb[:, b * NT * TC + j * TC:
                                                b * NT * TC + (j + 1) * TC],
                                           qraw[:, jb * TC:(jb + 1) * TC])
                        s.inc(vc, "v_fin")
                        v.drain()
                        s.inc(v.drain() if False else v.nop(), "v_fin2") if False else None
                        s.inc(v.drain(), "v_fin2")
                        jq += 1
                        ne += 1
                        continue
                    s.wait(t, "v_rn", ne + 1)
                    s.wait(t, "ldc_g", 1)
                    if ne >= 1:
                        s.wait(t, "v_qn", ne)
                    t.matmul(bc_ps[0:64, :], ones64[:, :], nab[:, :],
                             start=True, stop=True)
                    s.inc(t.matmul(bc_ps[64:128, :], ones64[:, :], nbb[:, :],
                                   start=True, stop=True, tile_position=(0, 64)), "t_bc")
                    s.wait(v, "t_bc", ne + 1)
                    s.wait(v, "v_cp", s.c["v_cp"])
                    if ne >= 2:
                        s.wait(v, "t_rot", ne - 1)
                    s.inc(v.scalar_tensor_tensor(qnsb[:, jb * TC:(jb + 1) * TC],
                                                 qraw[:, jb * TC:(jb + 1) * TC],
                                                 scol[:, 0:1], bc_ps[:, :],
                                                 ALU.mult, ALU.mult), "v_qn")
                    if SQKV_LEVEL == 2:
                        s.wait(v, "v_qn", ne + 1)
                        s.inc(v.tensor_copy(outb[:, b * NT * TC + j * TC:
                                                 b * NT * TC + (j + 1) * TC],
                                            qnsb[:, jb * TC:(jb + 1) * TC]), "v_fin")
                        jq += 1
                        continue
                    s.wait(t, "v_qn", ne + 1)
                    if ne >= 1:
                        s.wait(t, "v_fin2", ne)
                    s.inc(t.matmul(rot_ps[:, :], Rb[:, :], qnsb[:, jb * TC:(jb + 1) * TC],
                                   start=True, stop=True), "t_rot")
                    if SQKV_LEVEL == 1:
                        s.wait(v, "t_rot", ne + 1)
                        s.inc(v.tensor_copy(outb[:, b * NT * TC + j * TC:
                                                 b * NT * TC + (j + 1) * TC],
                                            qnsb[:, jb * TC:(jb + 1) * TC]), "v_fin")
                        jq += 1
                        continue
                    s.wait(v, "t_rot", ne + 1)
                    if c >= 2 and part == 0 and j == 0:
                        s.wait(v, f"st{b}", 16 * 3 * (c // 2))
                    v.tensor_mul(t1sb[:, :], rot_ps[:, :], ss_t)
                    v.drain()
                    v.tensor_mul(t2sb[:, :], qnsb[:, jb * TC:(jb + 1) * TC], cs_t)
                    v.drain()
                    s.inc(v.tensor_add(outb[:, b * NT * TC + j * TC: b * NT * TC + (j + 1) * TC],
                                       t1sb[:, :], t2sb[:, :]), "v_fin")
                    s.inc(v.drain(), "v_fin2")
                    jq += 1
                    ne += 1
            for tt in (range(0) if SQKV_NO_V else range(TC // P)):
                ps, setg_ps = ptok.get(s, t)
                s.wait(t, f"lh{b}", nlh)
                for kt in range(DT):
                    mm = t.matmul(ps[:, :],
                                  hb_b[:, kt * TC + tt * P: kt * TC + (tt + 1) * P],
                                  wsb[:, kt * 1536 + 2 * NQ: kt * 1536 + 3 * NQ],
                                  start=(kt == 0), stop=(kt == DT - 1))
                s.inc(mm, "t_qkv")
                s.wait(v, "t_qkv", s.c["t_qkv"])
                if c >= 2 and tt == 0:
                    s.wait(v, f"st{b}", 16 * 3 * (c // 2))
                s.inc(v.tensor_copy(vout[:, b * (TC // P) * NQ + tt * NQ:
                                         b * (TC // P) * NQ + (tt + 1) * NQ],
                                    ps[:, :]), "v_ev")
                setg_ps("v_ev", s.c["v_ev"])
            if SQKV_NO_STORE:
                continue
            s.wait(sy, "v_fin", jq)
            if not SQKV_NO_V:
                s.wait(sy, "v_ev", s.c["v_ev"])
            s.inc(sy.dma_start(out=_fm3(qw, slice(None), tsl),
                               in_=qout[:, b * NT * TC:(b + 1) * NT * TC]
                               .rearrange("p (nt t) -> p nt t", nt=NT)), f"st{b}", 16)
            s.inc(sy.dma_start(out=_fm3(kw, slice(None), tsl),
                               in_=kout[:, b * NT * TC:(b + 1) * NT * TC]
                               .rearrange("p (nt t) -> p nt t", nt=NT)), f"st{b}", 16)
            if not SQKV_NO_V:
                s.inc(sy.dma_start(out=vwt[tsl, :].rearrange("(tt p) n -> p tt n", p=P),
                                   in_=vout[:, b * (TC // P) * NQ:(b + 1) * (TC // P) * NQ]
                                   .rearrange("p (tt n) -> p tt n", tt=TC // P)), f"st{b}", 16)
            nst += 3
        if not SQKV_NO_STORE:
            nper = 2 if SQKV_NO_V else 3
            s.wait(sy, "st0", 16 * nper * (NCH // 2))
            s.wait(sy, "st1", 16 * nper * (NCH // 2))
        nc.all_engine_barrier()


def phase_wattn(ctx, tag, qw, kw, vwt, ctxw):
    nc = ctx.nc
    g, v, sc, t, sy = nc.gpsimd, nc.vector, nc.scalar, nc.tensor, nc.sync
    NQ = 512
    NHW = 8
    WIN = 512
    NW = SEQ // WIN
    QT = WIN // P
    NTQ = NQ // P
    with nc.cleanup_on_exit():
        s = Sems(nc, tag)
        qsb = nc.alloc_sbuf_tensor(f"{tag}_q", [P, NTQ * SEQ], BF16)
        ksb = nc.alloc_sbuf_tensor(f"{tag}_k", [P, NTQ * SEQ], BF16)
        vsb = nc.alloc_sbuf_tensor(f"{tag}_v", [P, (SEQ // P) * NQ], BF16)
        ctx_sb = nc.alloc_sbuf_tensor(f"{tag}_c", [P, NTQ * SEQ], BF16)
        smsb = nc.alloc_sbuf_tensor(f"{tag}_sm", [P, 640], F32)
        psb = nc.alloc_sbuf_tensor(f"{tag}_p", [P, 640], BF16)
        ptsb = nc.alloc_sbuf_tensor(f"{tag}_pt", [P, 640], BF16)
        maskb = nc.alloc_sbuf_tensor(f"{tag}_mb", [P, 640], F32)
        maskc = nc.alloc_sbuf_tensor(f"{tag}_mc", [P, P], F32)
        identb = nc.alloc_sbuf_tensor(f"{tag}_id", [P, P], BF16)
        negmax = nc.alloc_sbuf_tensor(f"{tag}_nm", [P, 1], F32)
        denom = nc.alloc_sbuf_tensor(f"{tag}_dn", [P, 1], F32)
        recip = nc.alloc_sbuf_tensor(f"{tag}_rc", [P, 1], F32)
        S_ps = nc.alloc_psum_tensor(f"{tag}_S", [P, 1024], F32)
        PT_ps = nc.alloc_psum_tensor(f"{tag}_PT", [P, 1024], BF16)
        cx_ps = nc.alloc_psum_tensor(f"{tag}_cx", [P, P], F32)

        s.inc(g.dma_start(out=maskb[:, :], in_=ctx.inp("mask_band", [P, 640])), "ldc", 16)
        s.inc(g.dma_start(out=maskc[:, :], in_=ctx.inp("mask_causal", [P, P])), "ldc", 16)
        s.inc(g.dma_start(out=identb[:, :], in_=ctx.inp("identity", [P, P])), "ldc", 16)
        LDC = s.c["ldc"]

        it = 0
        nst = 0
        for sq in range(NSEQ):
            ssl = slice(sq * SEQ, (sq + 1) * SEQ)
            sb = sq % 2
            if sq >= 1:
                s.wait(g, "st", 16 * nst)
            s.inc(g.dma_start(out=qsb[:, :].rearrange("p (kt t) -> p kt t", kt=NTQ),
                              in_=_fm3(qw, slice(None), ssl)), f"lq{sb}", 16)
            s.inc(g.dma_start(out=ksb[:, :].rearrange("p (kt t) -> p kt t", kt=NTQ),
                              in_=_fm3(kw, slice(None), ssl)), f"lq{sb}", 16)
            s.inc(g.dma_start(out=vsb[:, :].rearrange("p (tt n) -> p tt n", tt=SEQ // P),
                              in_=vwt[ssl, :].rearrange("(tt p) n -> p tt n", p=P)),
                  f"lq{sb}", 16)
            nlq = 48 * (sq // 2 + 1)

            for w in range(NW):
                for h in range(NHW):
                    hb = (h % 2) * 64
                    kt_h = h // 2
                    hoff = kt_h * SEQ
                    for qt in range(QT):
                        q0 = qt * P
                        q_t = qsb[hb:hb + 64,
                                  hoff + w * WIN + q0: hoff + w * WIN + q0 + P]
                        if w == 0:
                            kr = q0 + P
                            kbase = hoff + 0
                            nkk = kr // P
                        else:
                            kr = 640
                            kbase = hoff + (w - 1) * WIN + q0
                            nkk = 5
                        s.wait(t, f"lq{sb}", nlq)
                        s.wait(t, "ldc", LDC)
                        if it >= 1:
                            s.wait(t, "v_sm", it)
                        for kk in range(nkk):
                            mm = t.matmul(S_ps[:, kk * P:(kk + 1) * P], q_t,
                                          ksb[hb:hb + 64, kbase + kk * P: kbase + (kk + 1) * P],
                                          start=True, stop=True)
                        s.inc(mm, "t_s")
                        s.wait(v, "t_s", it + 1)
                        if it >= 1:
                            s.wait(v, "s_e", it)
                        if w == 0:
                            if qt > 0:
                                v.tensor_copy(smsb[:, 0:q0], S_ps[:, 0:q0])
                                v.drain()
                            v.tensor_add(smsb[:, q0:kr], S_ps[:, q0:kr], maskc[:, :])
                            v.drain()
                        else:
                            v.tensor_add(smsb[:, 0:kr], S_ps[:, 0:kr], maskb[:, :])
                            v.drain()
                        s.inc(v.reduce_max(negmax[:, 0:1], smsb[:, 0:kr], axis=AX.X,
                                           negate=True), "v_sm")
                        s.wait(sc, "v_sm", it + 1)
                        if it >= 1:
                            s.wait(sc, "t_pt", it)
                        s.inc(sc.activation(psb[:, 0:kr], smsb[:, 0:kr], AF.Exp,
                                            bias=negmax[:, 0:1],
                                            accum_out=denom[:, 0:1]), "s_e")
                        s.wait(v, "s_e", it + 1)
                        v.reciprocal(recip[:, 0:1], denom[:, 0:1])
                        v.drain()
                        s.inc(v.tensor_scalar_mul(psb[:, 0:kr], psb[:, 0:kr],
                                                  recip[:, 0:1]), "v_r")
                        s.wait(t, "v_r", it + 1)
                        if it >= 1:
                            s.wait(t, "s_pt", it)
                        for kk in range(nkk):
                            tp = t.transpose(PT_ps[:, kk * P:(kk + 1) * P],
                                             psb[:, kk * P:(kk + 1) * P], identb[:, :])
                        s.inc(tp, "t_pt")
                        s.wait(sc, "t_pt", it + 1)
                        if it >= 1:
                            s.wait(sc, "t_av", it)
                        s.inc(sc.copy(ptsb[:, 0:kr], PT_ps[:, 0:kr]), "s_pt")
                        s.wait(t, "s_pt", it + 1)
                        if it >= 1:
                            s.wait(t, "s_cx", it)
                        for kk in range(nkk):
                            if w == 0:
                                vrow = kk * P
                            else:
                                vrow = (w - 1) * WIN + q0 + kk * P
                            vt_i = vrow // P
                            mm = t.matmul(cx_ps[hb:hb + 64, :],
                                          vsb[:, vt_i * NQ + h * 64: vt_i * NQ + h * 64 + 64],
                                          ptsb[:, kk * P:(kk + 1) * P],
                                          start=(kk == 0), stop=(kk == nkk - 1),
                                          tile_position=(0, hb))
                        s.inc(mm, "t_av")
                        s.wait(sc, "t_av", it + 1)
                        s.inc(sc.copy(ctx_sb[hb:hb + 64,
                                             hoff + w * WIN + q0: hoff + w * WIN + q0 + P],
                                      cx_ps[hb:hb + 64, :]), "s_cx")
                        it += 1
            s.wait(sy, "s_cx", it)
            s.inc(sy.dma_start(out=_fm3(ctxw, slice(None), ssl),
                               in_=ctx_sb[:, :].rearrange("p (nt t) -> p nt t", nt=NTQ)),
                  "st", 16)
            nst += 1
        s.wait(sy, "st", 16 * nst)
        nc.all_engine_barrier()


def phase_lnshift(ctx, tag, x_mid, lg_d, lb_d, hln, eps=1e-5):
    nc = ctx.nc
    g, v, sc, t, sy = nc.gpsimd, nc.vector, nc.scalar, nc.tensor, nc.sync
    with nc.cleanup_on_exit():
        s = Sems(nc, tag)
        xf = nc.alloc_sbuf_tensor(f"{tag}_xf", [P, 2 * DT * TC], F32)
        xb = nc.alloc_sbuf_tensor(f"{tag}_xb", [P, 2 * DT * TC], BF16)
        x2 = nc.alloc_sbuf_tensor(f"{tag}_x2", [P, DT * TC], BF16)
        tmp = nc.alloc_sbuf_tensor(f"{tag}_tmp", [P, 2 * DT * TC], F32)
        onesb = nc.alloc_sbuf_tensor(f"{tag}_ones", [P, 1], BF16)
        onesf_row = nc.alloc_sbuf_tensor(f"{tag}_onesr", [1, P], F32)
        lgc = nc.alloc_sbuf_tensor(f"{tag}_lg", [P, DT], F32)
        lbc = nc.alloc_sbuf_tensor(f"{tag}_lb", [P, DT], F32)
        ssum = nc.alloc_sbuf_tensor(f"{tag}_ssum", [1, TOK], F32)
        ssq = nc.alloc_sbuf_tensor(f"{tag}_ssq", [1, TOK], F32)
        xnout = nc.alloc_sbuf_tensor(f"{tag}_xn", [P, 2 * DT * TC], BF16)
        ps_sum = nc.alloc_psum_tensor(f"{tag}_pss", [1, TC], F32)
        ps_sq = nc.alloc_psum_tensor(f"{tag}_psq", [1, TC], F32)
        ps_a = nc.alloc_psum_tensor(f"{tag}_psa", [P, TC], F32)
        ps_b = nc.alloc_psum_tensor(f"{tag}_psb", [P, TC], F32)

        s.inc(g.memset(onesb[:, :], 1.0), "g")
        s.inc(g.memset(onesf_row[:, :], 1.0), "g")
        s.inc(g.dma_start(out=lgc[:, :], in_=lg_d), "ldc", 16)
        s.inc(g.dma_start(out=lbc[:, :], in_=lb_d), "ldc", 16)

        xv = x_mid.rearrange("(kt p) t -> p kt t", p=P)
        zv = ctx.inp("zeros_col", [D, 1]).rearrange("(kt p) z -> p kt z", p=P)

        def load_shifted(eng, dst_b, c, semkey):
            tsl = slice(c * TC, (c + 1) * TC)
            s.inc(eng.dma_start(out=dst_b.rearrange("p (kt t) -> p kt t", kt=DT)[:, 0:4, :],
                                in_=xv[:, 0:4, tsl]), semkey, 16)
            if c % 4 == 0:
                s.inc(eng.dma_start(
                    out=dst_b.rearrange("p (kt t) -> p kt t", kt=DT)[:, 4:8, 1:TC],
                    in_=xv[:, 4:8, c * TC:(c + 1) * TC - 1]), semkey, 16)
                s.inc(eng.dma_start(
                    out=dst_b.rearrange("p (kt t) -> p kt t", kt=DT)[:, 4:8, 0:1],
                    in_=zv[:, 4:8, :], allow_slow_non_contiguous=True), semkey, 16)
            else:
                s.inc(eng.dma_start(
                    out=dst_b.rearrange("p (kt t) -> p kt t", kt=DT)[:, 4:8, :],
                    in_=xv[:, 4:8, c * TC - 1:(c + 1) * TC - 1]), semkey, 16)

        for c in range(NCH):
            b = c % 2
            tsl = slice(c * TC, (c + 1) * TC)
            xf_b = xf[:, b * DT * TC:(b + 1) * DT * TC]
            xb_b = xb[:, b * DT * TC:(b + 1) * DT * TC]
            if c >= 2:
                s.wait(g, "t_stats", 2 * (c - 1))
                s.wait(sy, "s_sq", c - 1)
            load_shifted(g, xb_b, c, f"ld{b}")
            load_shifted(sy, xf_b, c, f"ldf{b}")
            nld = s.c[f"ld{b}"]
            nldf = s.c[f"ldf{b}"]
            s.wait(sc, f"ldf{b}", nldf)
            if c >= 1:
                s.wait(sc, "t_stats", 2 * c)
            s.inc(sc.activation(x2[:, :], xf_b[:, :], AF.Square), "s_sq")
            s.wait(t, f"ld{b}", nld)
            s.wait(t, "g", 2)
            s.wait(t, "s_sq", c + 1)
            if c >= 1:
                s.wait(t, "v_stats", c)
            for kt in range(DT):
                t.matmul(ps_sum[:, :], onesb[:, :], xb_b[:, kt * TC:(kt + 1) * TC],
                         start=(kt == 0), stop=(kt == DT - 1))
            for kt in range(DT):
                mm = t.matmul(ps_sq[:, :], onesb[:, :], x2[:, kt * TC:(kt + 1) * TC],
                              start=(kt == 0), stop=(kt == DT - 1))
            s.inc(mm, "t_stats", 2)
            s.wait(v, "t_stats", 2 * (c + 1))
            v.tensor_copy(ssum[:, tsl], ps_sum[:, :])
            s.inc(v.tensor_copy(ssq[:, tsl], ps_sq[:, :]), "v_stats")

        trow = tmp[0:1, 0:TOK]
        s.wait(v, "v_stats", NCH)
        v.tensor_scalar_mul(ssum[:, :], ssum[:, :], 1.0 / D)
        v.drain()
        v.tensor_scalar_mul(ssq[:, :], ssq[:, :], 1.0 / D)
        v.drain()
        v.tensor_mul(trow, ssum[:, :], ssum[:, :])
        v.drain()
        v.tensor_sub(ssq[:, :], ssq[:, :], trow)
        v.drain()
        s.inc(v.tensor_scalar_add(ssq[:, :], ssq[:, :], eps), "v1")
        s.wait(sc, "v1", 1)
        s.inc(sc.activation(ssq[:, :], ssq[:, :], AF.Sqrt), "s1")
        s.wait(v, "s1", 1)
        v.reciprocal(ssq[:, :], ssq[:, :])
        v.drain()
        s.inc(v.scalar_tensor_tensor(ssum[:, :], ssum[:, :], -1.0, ssq[:, :],
                                     ALU.mult, ALU.mult), "v2")
        arow, brow = ssq, ssum

        for c in range(NCH):
            b = c % 2
            tsl = slice(c * TC, (c + 1) * TC)
            xf_b = xf[:, b * DT * TC:(b + 1) * DT * TC]
            tmp_b = tmp[:, b * DT * TC:(b + 1) * DT * TC]
            xn_b = xnout[:, b * DT * TC:(b + 1) * DT * TC]
            s.wait(sy, "s_sq", NCH)
            if c >= 2:
                s.wait(sy, "v_app", c - 1)
            load_shifted(sy, xf_b, c, f"ldf{b}")
            s.wait(t, "v2", 1)
            if c >= 1:
                s.wait(t, "v_app", c)
            t.matmul(ps_a[:, :], onesf_row[:, :], arow[:, tsl], start=True, stop=True)
            s.inc(t.matmul(ps_b[:, :], onesf_row[:, :], brow[:, tsl], start=True,
                           stop=True), "t_ab")
            s.wait(v, f"ldf{b}", s.c[f"ldf{b}"])
            s.wait(v, "t_ab", c + 1)
            s.wait(v, "ldc", 32)
            if c >= 2:
                s.wait(v, f"st{b}", 16 * (c // 2))
            for kt in range(DT):
                v.tensor_mul(tmp_b[:, kt * TC:(kt + 1) * TC],
                             xf_b[:, kt * TC:(kt + 1) * TC], ps_a[:, :])
            v.drain()
            for kt in range(DT):
                v.tensor_add(tmp_b[:, kt * TC:(kt + 1) * TC],
                             tmp_b[:, kt * TC:(kt + 1) * TC], ps_b[:, :])
            v.drain()
            for kt in range(DT):
                v.tensor_scalar(xn_b[:, kt * TC:(kt + 1) * TC],
                                tmp_b[:, kt * TC:(kt + 1) * TC],
                                lgc[:, kt:kt + 1], lbc[:, kt:kt + 1],
                                ALU.mult, ALU.add)
            s.inc(v.drain(), "v_app")
            s.wait(g, "v_app", c + 1)
            s.inc(g.dma_start(out=_fm3(hln, slice(None), tsl),
                              in_=xn_b.rearrange("p (kt t) -> p kt t", kt=DT)),
                  f"st{b}", 16)
        s.wait(g, "st0", 16 * (NCH // 2))
        s.wait(g, "st1", 16 * (NCH // 2))
        nc.all_engine_barrier()


def phase_w1gate(ctx, tag, hln, w1A_d, w1G_d, b1_d, actw):
    nc = ctx.nc
    g, v, sc, t, sy = nc.gpsimd, nc.vector, nc.scalar, nc.tensor, nc.sync
    NHID = 2816
    NT = NHID // P
    with nc.cleanup_on_exit():
        s = Sems(nc, tag)
        wA = nc.alloc_sbuf_tensor(f"{tag}_wA", [P, DT * NHID], BF16)
        wG = nc.alloc_sbuf_tensor(f"{tag}_wG", [P, DT * NHID], BF16)
        bsb = nc.alloc_sbuf_tensor(f"{tag}_b", [P, 2 * NT], F32)
        ab = nc.alloc_sbuf_tensor(f"{tag}_act", [P, 2 * DT * TC], BF16)
        asb = nc.alloc_sbuf_tensor(f"{tag}_a", [P, 2 * TC], BF16)
        gsb = nc.alloc_sbuf_tensor(f"{tag}_g", [P, 2 * TC], BF16)
        osb = nc.alloc_sbuf_tensor(f"{tag}_o", [P, 2 * NT * TC], BF16)
        ppA = PsumPool(nc, tag + "A", 2, [P, TC])
        ppG = PsumPool(nc, tag + "G", 2, [P, TC])

        s.inc(g.dma_start(out=wA[:, :].rearrange("p (kt n) -> p kt n", kt=DT),
                          in_=w1A_d.rearrange("(kt p) n -> p kt n", p=P)), "ldw", 16)
        s.inc(g.dma_start(out=wG[:, :].rearrange("p (kt n) -> p kt n", kt=DT),
                          in_=w1G_d.rearrange("(kt p) n -> p kt n", p=P)), "ldw", 16)
        s.inc(g.dma_start(out=bsb[:, :], in_=b1_d), "ldw", 16)
        LDW = s.c["ldw"]

        for c in range(NCH):
            b = c % 2
            tsl = slice(c * TC, (c + 1) * TC)
            ab_b = ab[:, b * DT * TC:(b + 1) * DT * TC]
            if c >= 2:
                s.wait(g, "t_c", 2 * NT * (c - 1))
            s.inc(g.dma_start(out=ab_b.rearrange("p (kt t) -> p kt t", kt=DT),
                              in_=_fm3(hln, slice(None), tsl)), f"lda{b}", 16)
            nlda = 16 * (c // 2 + 1)
            for j in range(NT):
                jb = j % 2
                pA, setg_A = ppA.get(s, t)
                s.wait(t, f"lda{b}", nlda)
                s.wait(t, "ldw", LDW)
                for kt in range(DT):
                    mm = t.matmul(pA[:, :], wA[:, kt * NHID + j * P: kt * NHID + j * P + P],
                                  ab_b[:, kt * TC:(kt + 1) * TC],
                                  start=(kt == 0), stop=(kt == DT - 1))
                s.inc(mm, "t_c")
                pG, setg_G = ppG.get(s, t)
                for kt in range(DT):
                    mm = t.matmul(pG[:, :], wG[:, kt * NHID + j * P: kt * NHID + j * P + P],
                                  ab_b[:, kt * TC:(kt + 1) * TC],
                                  start=(kt == 0), stop=(kt == DT - 1))
                s.inc(mm, "t_c")
                nj = s.c["t_c"] // 2
                s.wait(sc, "t_c", s.c["t_c"])
                if nj >= 3:
                    s.wait(sc, "v_g", nj - 2)
                s.inc(sc.add(asb[:, jb * TC:(jb + 1) * TC], pA[:, :], bsb[:, j:j + 1]),
                      "s_a")
                setg_A("s_a", s.c["s_a"])
                s.inc(sc.activation(gsb[:, jb * TC:(jb + 1) * TC], pG[:, :], AF.Gelu,
                                    bias=bsb[:, NT + j:NT + j + 1]), "s_g")
                setg_G("s_g", s.c["s_g"])
                s.wait(v, "s_g", nj)
                if c >= 2 and j == 0:
                    s.wait(v, f"st{b}", 16 * (c // 2))
                s.inc(v.tensor_mul(osb[:, b * NT * TC + j * TC: b * NT * TC + (j + 1) * TC],
                                   asb[:, jb * TC:(jb + 1) * TC],
                                   gsb[:, jb * TC:(jb + 1) * TC]), "v_g")
            s.wait(sy, "v_g", s.c["v_g"])
            s.inc(sy.dma_start(out=_fm3(actw, slice(None), tsl),
                               in_=osb[:, b * NT * TC:(b + 1) * NT * TC]
                               .rearrange("p (nt t) -> p nt t", nt=NT)), f"st{b}", 16)
        s.wait(sy, "st0", 16 * (NCH // 2))
        s.wait(sy, "st1", 16 * (NCH // 2))
        nc.all_engine_barrier()


PHASES = ["ln0", "qkv0", "attn0", "router", "moe", "comb0", "qkv1", "attn1",
          "f0a", "f0b", "f0c", "f0d", "f0e", "f0f",
          "f1a", "f1b", "f1c", "f1d", "f1e", "f1f", "final"]


def build_program(upto=None, dbg=()):
    nc = bass.Bass("TRN2", target_bir_lowering=False)
    ctx = Ctx(nc)
    phases = PHASES if upto is None else PHASES[:PHASES.index(upto) + 1]

    x_fm = ctx.inp("x_fm", [D, TOK])
    xn0 = ctx.scr("xn0", [D, TOK], BF16)
    xn0f = ctx.scr("xn0f", [D, TOK], F32)

    if "ln0" in phases:
        phase_ln(ctx, "ln0", x_fm, xn0, xn_f32=xn0f)

    if "qkv0" in phases:
        qkvT = ctx.inp("qkvT", [D, 3072])
        qkvb = ctx.inp("qkvb", [P, 24])
        qkvb_row = ctx.inp("qkvb_row", [1, 3072])
        qk0 = ctx.scr("qk0", [2048, TOK], BF16)
        v0t = ctx.scr("v0t", [TOK, D], BF16)
        phase_gemm(ctx, "qkv0", xn0, D, [
            dict(kind="fm", w=qkvT[:, 0:2048], N=2048, bias=qkvb[:, 0:16],
                 outs=[(qk0, "bf16")]),
            dict(kind="tok", w=qkvT[:, 2048:3072], N=1024,
                 bias_row=qkvb_row[0:1, 2048:3072], out=v0t),
        ])

    if "attn0" in phases:
        outT_d = ctx.inp("outT", [D, D])
        outb_d = ctx.inp("outb", [P, DT])
        s0 = ctx.scr("s0", [D, TOK], F32)
        phase_attn0(ctx, "attn0", ctx.scratch["qk0"], ctx.scratch["v0t"],
                    outT_d, outb_d, s0)

    if "router" in phases:
        wrT_d = ctx.inp("wrT", [D, 8])
        gates_d = ctx.scr("gates_d", [8, TOK], F32)
        phase_router(ctx, "router", xn0f, wrT_d, gates_d)

    if "moe" in phases:
        w1T_d = ctx.inp("w1T", [8, D, 4096])
        w3T_d = ctx.inp("w3T", [8, D, 4096])
        w2T_d = ctx.inp("w2T", [8, 4096, D])
        g0 = ctx.scr("g0", [D, TOK], F32)
        phase_moe(ctx, "moe", xn0, w1T_d, w3T_d, w2T_d,
                  ctx.scratch["gates_d"], g0)

    if "comb0" in phases:
        x1 = ctx.scr("x1", [D, TOK], F32)
        xn1h = ctx.scr("xn1h", [D, TOK], BF16)
        xn1f = ctx.scr("xn1f", [D, TOK], F32)
        phase_combine(ctx, "comb0", x_fm, ctx.scratch["s0"], g0, x1)
        phase_ln(ctx, "ln1", x1, xn1h, xn_f32=xn1f)

    if "qkv1" in phases:
        q1l = ctx.scr("q1l", [D, TOK], BF16)
        k1l = ctx.scr("k1l", [D, TOK], BF16)
        v1lt = ctx.scr("v1lt", [TOK, D], BF16)
        phase_gemm(ctx, "qkv1", ctx.scratch["xn1h"], D, [
            dict(kind="fm", w=ctx.inp("wql", [D, D]), N=D, outs=[(q1l, "bf16")]),
            dict(kind="fm", w=ctx.inp("wkl", [D, D]), N=D, outs=[(k1l, "bf16")]),
            dict(kind="tok", w=ctx.inp("wvl", [D, D]), N=D, out=v1lt),
        ])

    if "attn1" in phases:
        s1 = ctx.scr("s1", [D, TOK], F32)
        phase_attn1(ctx, "attn1", ctx.scratch["q1l"], ctx.scratch["k1l"],
                    ctx.scratch["v1lt"], ctx.scratch["xn1f"],
                    ctx.inp("wol", [D, D]), ctx.inp("wqg", [D, D]),
                    ctx.inp("wkg", [D, D]), ctx.inp("wvg", [D, D]),
                    ctx.inp("wog", [D, D]), s1)

    if any(p.startswith("f0") or p.startswith("f1") for p in phases):
        wqkv_d = ctx.inp("wqkv1", [2, D, 1536])
        wo1_d = ctx.inp("wo1", [2, 512, D])
        w1p_d = ctx.inp("w1p", [2, D, 5632])
        b1p_d = ctx.inp("b1p", [2, P, 44])
        w2p_d = ctx.inp("w2p", [2, 2816, D])
        b2p_d = ctx.inp("b2p", [2, P, DT])
        qs_d = ctx.inp("qs", [2, P, 1])
        ks_d = ctx.inp("ks", [2, P, 1])
        lng_d = ctx.inp("lng", [2, P, DT])
        lnb_d = ctx.inp("lnb", [2, P, DT])
        x_l = ctx.scratch["xn1f"]
        for l in range(2):
            qw = ctx.scr(f"f{l}_qw", [512, TOK], BF16)
            kw = ctx.scr(f"f{l}_kw", [512, TOK], BF16)
            vwt = ctx.scr(f"f{l}_vwt", [TOK, 512], BF16)
            ctxw = ctx.scr(f"f{l}_ctx", [512, TOK], BF16)
            xi_mid = ctx.scr(f"f{l}_xm", [D, TOK], F32)
            hln = ctx.scr(f"f{l}_hln", [D, TOK], BF16)
            actw = ctx.scr(f"f{l}_act", [2816, TOK], BF16)
            xi_next = ctx.scr("g1" if l == 1 else f"f{l}_xn", [D, TOK], F32)
            if f"f{l}a" in phases:
                phase_shiftqkv(ctx, f"f{l}a", x_l, wqkv_d[l], qs_d[l], ks_d[l],
                               qw, kw, vwt)
            if f"f{l}b" in phases:
                phase_wattn(ctx, f"f{l}b", qw, kw, vwt, ctxw)
            if f"f{l}c" in phases:
                phase_gemm(ctx, f"f{l}c", ctxw, 512, [
                    dict(kind="fm", w=wo1_d[l], N=D, res=x_l,
                         outs=[(xi_mid, "f32")]),
                ])
            if f"f{l}d" in phases:
                phase_lnshift(ctx, f"f{l}d", xi_mid, lng_d[l], lnb_d[l], hln)
            if f"f{l}e" in phases:
                phase_w1gate(ctx, f"f{l}e", hln, w1p_d[l][:, 0:2816],
                             w1p_d[l][:, 2816:5632], b1p_d[l], actw)
            if f"f{l}f" in phases:
                phase_gemm(ctx, f"f{l}f", actw, 2816, [
                    dict(kind="fm", w=w2p_d[l], N=D, bias=b2p_d[l], res=xi_mid,
                         outs=[(xi_next, "f32")]),
                ])
            x_l = xi_next

    if "final" in phases:
        out_fm = ctx.out("out_fm", [D, TOK], F32)
        phase_combine(ctx, "final", ctx.scratch["x1"], ctx.scratch["s1"],
                      ctx.scratch["g1"], out_fm)

    for name in dbg:
        src = ctx.scratch[name]
        dst = ctx.out("dbg_" + name, list(src.shape), src.dtype)
        sem = nc.alloc_semaphore(f"dbgcp_{name}")
        nc.sync.dma_start(out=dst[:, :], in_=src[:, :]).then_inc(sem, 16)
        nc.sync.wait_ge(sem, 16)
    nc.all_engine_barrier()
    return nc, ctx


def prep_in_maps(inputs):
    shared = _prep_shared(inputs)
    x = np.asarray(inputs["L0_attn_x"], dtype=np.float32)
    in_maps = []
    for c in range(N_CORES):
        xs = x[4 * c:4 * c + 4]
        x_fm = np.ascontiguousarray(xs.transpose(2, 0, 1).reshape(D, TOK))
        m = dict(shared)
        m["x_fm"] = x_fm
        in_maps.append(m)
    return in_maps


def kernel(**inputs):
    nc, ctx = build_program()
    in_maps = prep_in_maps(inputs)
    used = set(ctx.din.keys())
    in_maps = [{k: v for k, v in m.items() if k in used} for m in in_maps]
    res = run_bass_kernel_spmd(nc, in_maps, core_ids=list(range(N_CORES)))
    outs = []
    for c in range(N_CORES):
        o = res.results[c]["out_fm"]
        outs.append(o.reshape(D, NSEQ, SEQ).transpose(1, 2, 0))
    return np.concatenate(outs, axis=0).astype(np.float32)


# version 03 (final)
# speedup vs baseline: 1.0886x; 1.0401x over previous
import numpy as np
import concourse.bass as bass
import concourse.mybir as mybir
from concourse.bass_utils import run_bass_kernel_spmd

F32 = mybir.dt.float32
BF16 = mybir.dt.bfloat16
AF = mybir.ActivationFunctionType
ALU = mybir.AluOpType
AX = mybir.AxisListType

P = 128
D = 1024
DT = D // P
TOK = 8192
SEQ = 2048
NSEQ = 4
TC = 512
NCH = TOK // TC
NEG = -1.0e30

N_CORES = 8
MOE_MAX_ROUNDS = None
SQKV_LEVEL = 0
SQKV_EPI_TILES = 10**9
SQKV_NO_V = False
SQKV_NO_STORE = False


def _host_consts():
    c = {}
    c["identity"] = np.eye(P, dtype=np.float32)
    R = np.zeros((P, P), dtype=np.float32)
    for m in range(P):
        r = m % 64
        base = m - r
        if r < 32:
            R[base + r + 32, m] = -1.0
        else:
            R[base + r - 32, m] = 1.0
    c["rothalf"] = R
    HS = np.zeros((P, 2), dtype=np.float32)
    HS[0:64, 0] = 1.0
    HS[64:128, 1] = 1.0
    c["headsum2"] = HS
    c["ones_col"] = np.ones((P, 1), dtype=np.float32)
    c["zeros_col"] = np.zeros((D, 1), dtype=np.float32)
    HSEL = np.zeros((P, P), dtype=np.float32)
    HSEL[0:64, 0] = 1.0
    HSEL[64:128, 64] = 1.0
    c["hsel"] = HSEL
    BSEL = np.zeros((P, P), dtype=np.float32)
    BSEL[0, 0:64] = 1.0
    BSEL[64, 64:128] = 1.0
    c["bsel"] = BSEL
    c["ones_row"] = np.ones((1, P), dtype=np.float32)
    i = np.arange(P)[:, None]
    j = np.arange(P)[None, :]
    c["mask_causal"] = np.where(j <= i, 0.0, NEG).astype(np.float32)
    ii = np.arange(P)[:, None]
    jj = np.arange(640)[None, :]
    c["mask_band"] = np.where((jj >= ii) & (jj <= ii + 512), 0.0, NEG).astype(np.float32)
    DIM_HEAD = 64
    WINDOW = 512
    QK_SCALE = 8.0
    n = SEQ
    half = np.arange(0, DIM_HEAD, 2, dtype=np.float64)
    inv_freq = 1.0 / 10000.0 ** (half / DIM_HEAD)
    t = np.arange(n, dtype=np.float64)
    freqs = t[:, None] * inv_freq[None, :]
    freqs = np.concatenate([freqs, freqs], axis=1)
    base = (half + 0.4 * DIM_HEAD) / (1.4 * DIM_HEAD)
    power = (t - n // 2) / float(WINDOW // 2)
    scale = base[None, :] ** power[:, None]
    scale = np.concatenate([scale, scale], axis=1)
    cos = np.cos(freqs)
    sin = np.sin(freqs)
    def fm(a):
        a = np.ascontiguousarray(a.T.astype(np.float32))
        return np.concatenate([a, a], axis=0)
    c["cs_q"] = fm(cos * scale * QK_SCALE)
    c["ss_q"] = fm(sin * scale * QK_SCALE)
    c["cs_k"] = fm(cos / scale)
    c["ss_k"] = fm(sin / scale)
    return c


def _prep_shared(inp):
    A = {}
    ct = lambda a: np.ascontiguousarray(a)
    A["qkvT"] = ct(inp["L0_attn_qkv_w"].T.astype(np.float32))
    A["outT"] = ct(inp["L0_attn_out_w"].T.astype(np.float32))
    A["qkvb"] = ct(inp["L0_attn_qkv_b"].reshape(24, P).T.astype(np.float32))
    A["qkvb_row"] = ct(inp["L0_attn_qkv_b"].reshape(1, 3072).astype(np.float32))
    A["outb"] = ct(inp["L0_attn_out_b"].reshape(8, P).T.astype(np.float32))
    A["wrT"] = ct(inp["L0_ffn_Wr"].T.astype(np.float32))
    A["w1T"] = ct(np.transpose(inp["L0_ffn_W1"], (0, 2, 1)).astype(np.float32))
    A["w3T"] = ct(np.transpose(inp["L0_ffn_W3"], (0, 2, 1)).astype(np.float32))
    A["w2T"] = ct(np.transpose(inp["L0_ffn_W2"], (0, 2, 1)).astype(np.float32))
    for nm, key in [("wql", "L1_attn_Wq_l"), ("wkl", "L1_attn_Wk_l"),
                    ("wvl", "L1_attn_Wv_l"), ("wol", "L1_attn_Wo_l"),
                    ("wqg", "L1_attn_Wq_g"), ("wkg", "L1_attn_Wk_g"),
                    ("wvg", "L1_attn_Wv_g"), ("wog", "L1_attn_Wo_g")]:
        A[nm] = ct(inp[key].astype(np.float32))
    A["wqkv1"] = ct(inp["L1_ffn_wqkv"].astype(np.float32))
    A["wo1"] = ct(inp["L1_ffn_wo"].astype(np.float32))
    w1 = inp["L1_ffn_w1"].astype(np.float32)
    HID = 2730
    HIDP = 2816
    w1p = np.zeros((2, D, 2 * HIDP), dtype=np.float32)
    w1p[:, :, :HID] = w1[:, :, :HID]
    w1p[:, :, HIDP:HIDP + HID] = w1[:, :, HID:]
    A["w1p"] = ct(w1p)
    b1 = inp["L1_ffn_b1"].astype(np.float32)
    b1p = np.zeros((2, 2 * HIDP), dtype=np.float32)
    b1p[:, :HID] = b1[:, :HID]
    b1p[:, HIDP:HIDP + HID] = b1[:, HID:]
    A["b1p"] = ct(np.transpose(b1p.reshape(2, 44, P), (0, 2, 1)))
    w2 = inp["L1_ffn_w2"].astype(np.float32)
    w2p = np.zeros((2, HIDP, D), dtype=np.float32)
    w2p[:, :HID, :] = w2
    A["w2p"] = ct(w2p)
    A["b2p"] = ct(np.transpose(inp["L1_ffn_b2"].astype(np.float32).reshape(2, 8, P), (0, 2, 1)))
    A["qs"] = ct(np.concatenate([inp["L1_ffn_q_scale"]] * 2, axis=1).astype(np.float32)[:, :, None])
    A["ks"] = ct(np.concatenate([inp["L1_ffn_k_scale"]] * 2, axis=1).astype(np.float32)[:, :, None])
    A["lng"] = ct(np.transpose(inp["L1_ffn_ln_g"].astype(np.float32).reshape(2, 8, P), (0, 2, 1)))
    A["lnb"] = ct(np.transpose(inp["L1_ffn_ln_b"].astype(np.float32).reshape(2, 8, P), (0, 2, 1)))
    A.update(_host_consts())
    return A


class Ctx:

    def __init__(self, nc):
        self.nc = nc
        self.din = {}
        self.scratch = {}
        self.dout = {}

    def inp(self, name, shape):
        if name not in self.din:
            self.din[name] = self.nc.dram_tensor(name, list(shape), F32,
                                                 kind="ExternalInput").ap()
        return self.din[name]

    def scr(self, name, shape, dtype=F32):
        if name not in self.scratch:
            self.scratch[name] = self.nc.dram_tensor(name, list(shape), dtype,
                                                     kind="Internal").ap()
        return self.scratch[name]

    def out(self, name, shape, dtype=F32):
        if name not in self.dout:
            self.dout[name] = self.nc.dram_tensor(name, list(shape), dtype,
                                                  kind="ExternalOutput").ap()
        return self.dout[name]


class Sems:

    def __init__(self, nc, tag):
        self.nc = nc
        self.tag = tag
        self.h = {}
        self.c = {}

    def inc(self, instr, key, n=1):
        if key not in self.h:
            self.h[key] = self.nc.alloc_semaphore(f"{self.tag}_{key}")
            self.c[key] = 0
        instr.then_inc(self.h[key], n)
        self.c[key] += n
        return self.c[key]

    def wait(self, eng, key, val=None):
        if key not in self.h:
            self.h[key] = self.nc.alloc_semaphore(f"{self.tag}_{key}")
            self.c[key] = 0
        eng.wait_ge(self.h[key], self.c[key] if val is None else val)


def _fm3(dram2d, kt, tsl):
    K = dram2d.shape[0]
    v = dram2d.rearrange("(kt p) t -> p kt t", p=P)
    return v[:, kt, tsl]


def phase_ln(ctx, tag, xsrc, xn_bf, xn_f32=None, eps=1e-5):
    nc = ctx.nc
    nstores = 1 if xn_f32 is None else 2
    with nc.cleanup_on_exit():
        s = Sems(nc, tag)
        xf = nc.alloc_sbuf_tensor(f"{tag}_xf", [P, 2 * DT * TC], F32)
        xb = nc.alloc_sbuf_tensor(f"{tag}_xb", [P, 2 * DT * TC], BF16)
        x2 = nc.alloc_sbuf_tensor(f"{tag}_x2", [P, DT * TC], BF16)
        tmp = nc.alloc_sbuf_tensor(f"{tag}_tmp", [P, 2 * DT * TC], F32)
        onesb = nc.alloc_sbuf_tensor(f"{tag}_ones", [P, 1], BF16)
        onesf_row = nc.alloc_sbuf_tensor(f"{tag}_onesr", [1, P], F32)
        ssum = nc.alloc_sbuf_tensor(f"{tag}_ssum", [1, TOK], F32)
        ssq = nc.alloc_sbuf_tensor(f"{tag}_ssq", [1, TOK], F32)
        xnout = nc.alloc_sbuf_tensor(f"{tag}_xn", [P, 2 * DT * TC], BF16)
        ps_sum = nc.alloc_psum_tensor(f"{tag}_pss", [1, TC], F32)
        ps_sq = nc.alloc_psum_tensor(f"{tag}_psq", [1, TC], F32)
        ps_a = nc.alloc_psum_tensor(f"{tag}_psa", [P, TC], F32)
        ps_b = nc.alloc_psum_tensor(f"{tag}_psb", [P, TC], F32)

        g, v, sc, t, sy = nc.gpsimd, nc.vector, nc.scalar, nc.tensor, nc.sync

        s.inc(g.memset(onesb[:, :], 1.0), "g")
        s.inc(g.memset(onesf_row[:, :], 1.0), "g")

        for c in range(NCH):
            b = c % 2
            tsl = slice(c * TC, (c + 1) * TC)
            xf_b = xf[:, b * DT * TC:(b + 1) * DT * TC]
            xb_b = xb[:, b * DT * TC:(b + 1) * DT * TC]
            if c >= 2:
                s.wait(g, "t_stats", 2 * (c - 1))
                s.wait(sy, "s_sq", c - 1)
            s.inc(g.dma_start(out=xb_b.rearrange("p (kt t) -> p kt t", kt=DT),
                              in_=_fm3(xsrc, slice(None), tsl)), f"ld{b}", 16)
            s.inc(sy.dma_start(out=xf_b.rearrange("p (kt t) -> p kt t", kt=DT),
                               in_=_fm3(xsrc, slice(None), tsl)), f"ldf{b}", 16)
            s.wait(sc, f"ldf{b}", 16 * (c // 2 + 1))
            if c >= 1:
                s.wait(sc, "t_stats", 2 * c)
            s.inc(sc.activation(x2[:, :], xf_b[:, :], AF.Square), "s_sq")
            s.wait(t, f"ld{b}", 16 * (c // 2 + 1))
            s.wait(t, "g", 2)
            s.wait(t, "s_sq", c + 1)
            if c >= 1:
                s.wait(t, "v_stats", c)
            for kt in range(DT):
                t.matmul(ps_sum[:, :], onesb[:, :], xb_b[:, kt * TC:(kt + 1) * TC],
                         start=(kt == 0), stop=(kt == DT - 1))
            for kt in range(DT):
                mm = t.matmul(ps_sq[:, :], onesb[:, :], x2[:, kt * TC:(kt + 1) * TC],
                              start=(kt == 0), stop=(kt == DT - 1))
            s.inc(mm, "t_stats", 2)
            s.wait(v, "t_stats", 2 * (c + 1))
            v.tensor_copy(ssum[:, tsl], ps_sum[:, :])
            s.inc(v.tensor_copy(ssq[:, tsl], ps_sq[:, :]), "v_stats")

        trow = tmp[0:1, 0:TOK]
        s.wait(v, "v_stats", NCH)
        v.tensor_scalar_mul(ssum[:, :], ssum[:, :], 1.0 / D)
        v.drain()
        v.tensor_scalar_mul(ssq[:, :], ssq[:, :], 1.0 / D)
        v.drain()
        v.tensor_mul(trow, ssum[:, :], ssum[:, :])
        v.drain()
        v.tensor_sub(ssq[:, :], ssq[:, :], trow)
        v.drain()
        s.inc(v.tensor_scalar_add(ssq[:, :], ssq[:, :], eps), "v1")
        s.wait(sc, "v1", 1)
        s.inc(sc.activation(ssq[:, :], ssq[:, :], AF.Sqrt), "s1")
        s.wait(v, "s1", 1)
        v.reciprocal(ssq[:, :], ssq[:, :])
        v.drain()
        s.inc(v.scalar_tensor_tensor(ssum[:, :], ssum[:, :], -1.0, ssq[:, :],
                                     ALU.mult, ALU.mult), "v2")
        arow, brow = ssq, ssum

        for c in range(NCH):
            b = c % 2
            tsl = slice(c * TC, (c + 1) * TC)
            xf_b = xf[:, b * DT * TC:(b + 1) * DT * TC]
            tmp_b = tmp[:, b * DT * TC:(b + 1) * DT * TC]
            xn_b = xnout[:, b * DT * TC:(b + 1) * DT * TC]
            s.wait(sy, "s_sq", NCH)
            if c >= 2:
                s.wait(sy, "v_app", c - 1)
            s.inc(sy.dma_start(out=xf_b.rearrange("p (kt t) -> p kt t", kt=DT),
                               in_=_fm3(xsrc, slice(None), tsl)), f"ldf{b}", 16)
            s.wait(t, "v2", 1)
            if c >= 1:
                s.wait(t, "v_app", c)
            t.matmul(ps_a[:, :], onesf_row[:, :], arow[:, tsl], start=True, stop=True)
            s.inc(t.matmul(ps_b[:, :], onesf_row[:, :], brow[:, tsl], start=True, stop=True), "t_ab")
            s.wait(v, f"ldf{b}", 16 * (NCH // 2 + c // 2 + 1))
            s.wait(v, "t_ab", c + 1)
            if c >= 2:
                s.wait(v, f"st{b}", 16 * nstores * (c // 2))
            for kt in range(DT):
                v.tensor_mul(tmp_b[:, kt * TC:(kt + 1) * TC],
                             xf_b[:, kt * TC:(kt + 1) * TC], ps_a[:, :])
            v.drain()
            for kt in range(DT):
                v.tensor_add(tmp_b[:, kt * TC:(kt + 1) * TC],
                             tmp_b[:, kt * TC:(kt + 1) * TC], ps_b[:, :])
            v.drain()
            for kt in range(DT):
                v.tensor_copy(xn_b[:, kt * TC:(kt + 1) * TC],
                              tmp_b[:, kt * TC:(kt + 1) * TC])
            s.inc(v.drain(), "v_app")
            s.wait(g, "v_app", c + 1)
            s.inc(g.dma_start(out=_fm3(xn_bf, slice(None), tsl),
                              in_=xn_b.rearrange("p (kt t) -> p kt t", kt=DT)), f"st{b}", 16)
            if xn_f32 is not None:
                s.inc(g.dma_start(out=_fm3(xn_f32, slice(None), tsl),
                                  in_=tmp_b.rearrange("p (kt t) -> p kt t", kt=DT)), f"st{b}", 16)
        s.wait(g, "st0", 16 * nstores * (NCH // 2))
        s.wait(g, "st1", 16 * nstores * (NCH // 2))
        nc.all_engine_barrier()


class PsumPool:

    def __init__(self, nc, tag, n, shape, dtype=F32):
        self.tiles = [nc.alloc_psum_tensor(f"{tag}_ps{i}", list(shape), dtype)
                      for i in range(n)]
        self.gate = [None] * n
        self.i = 0

    def get(self, s, eng):
        i = self.i
        self.i = (self.i + 1) % len(self.tiles)
        if self.gate[i] is not None:
            for kv in self.gate[i]:
                s.wait(eng, *kv)
        tile = self.tiles[i]

        def setgate(*pairs):
            self.gate[i] = [pairs] if pairs and not isinstance(pairs[0], tuple)                 else list(pairs)
        return tile, setgate


def phase_gemm(ctx, tag, act_src, Kdim, specs):
    nc = ctx.nc
    KT = Kdim // P
    g, v, sc, t, sy = nc.gpsimd, nc.vector, nc.scalar, nc.tensor, nc.sync
    with nc.cleanup_on_exit():
        s = Sems(nc, tag)
        nw = 0
        for sp in specs:
            N = sp["N"]
            sp["wsb"] = nc.alloc_sbuf_tensor(f"{tag}_w{nw}", [P, KT * N], BF16)
            s.inc(g.dma_start(out=sp["wsb"][:, :].rearrange("p (kt n) -> p kt n", kt=KT),
                              in_=sp["w"].rearrange("(kt p) n -> p kt n", p=P)), "ldw", 16)
            if sp["kind"] == "fm" and sp.get("bias") is not None:
                NT = N // P
                sp["bsb"] = nc.alloc_sbuf_tensor(f"{tag}_b{nw}", [P, NT], F32)
                s.inc(g.dma_start(out=sp["bsb"][:, :], in_=sp["bias"][:, :]), "ldw", 16)
            if sp["kind"] == "tok" and sp.get("bias_row") is not None:
                sp["brsb"] = nc.alloc_sbuf_tensor(f"{tag}_br{nw}", [1, N], F32)
                s.inc(g.dma_start(out=sp["brsb"][:, :], in_=sp["bias_row"]), "ldw", 16)
            nw += 1
        LDW = s.c["ldw"]
        ab = nc.alloc_sbuf_tensor(f"{tag}_act", [P, 2 * KT * TC], BF16)
        onesf_row = nc.alloc_sbuf_tensor(f"{tag}_onesr", [1, P], F32)
        s.inc(g.memset(onesf_row[:, :], 1.0), "g")
        for i, sp in enumerate(specs):
            N = sp["N"]
            if sp["kind"] == "fm":
                NT = N // P
                dt_o = F32 if any(d == "f32" for _, d in sp["outs"]) else BF16
                sp["osb"] = nc.alloc_sbuf_tensor(f"{tag}_o{i}", [P, 2 * NT * TC], dt_o)
                if any(d == "bf16" for _, d in sp["outs"]) and dt_o == F32:
                    sp["osb_bf"] = nc.alloc_sbuf_tensor(f"{tag}_ob{i}", [P, 2 * NT * TC], BF16)
                if sp.get("res") is not None:
                    sp["rsb"] = nc.alloc_sbuf_tensor(f"{tag}_r{i}", [P, 2 * NT * TC], F32)
            else:
                sp["osb"] = nc.alloc_sbuf_tensor(f"{tag}_o{i}", [P, 2 * (TC // P) * N], BF16)
                if sp.get("bias_row") is not None:
                    sp["vbias"] = nc.alloc_sbuf_tensor(f"{tag}_vb{i}", [P, N], F32)
        pool_fm = PsumPool(nc, tag + "f", 4, [P, TC])
        pool_tok = PsumPool(nc, tag + "t", 2, [P, 1024]) if any(
            sp["kind"] == "tok" for sp in specs) else None

        for sp in specs:
            if sp["kind"] == "tok" and sp.get("bias_row") is not None:
                N = sp["N"]
                ps, setg = pool_tok.get(s, t)
                s.wait(t, "ldw", LDW)
                s.wait(t, "g", 1)
                for h in range(N // 512):
                    mm = t.matmul(ps[:, h * 512:(h + 1) * 512], onesf_row[:, :],
                                  sp["brsb"][0:1, h * 512:(h + 1) * 512],
                                  start=True, stop=True)
                s.inc(mm, "tb")
                s.wait(v, "tb", s.c["tb"])
                s.inc(v.tensor_copy(sp["vbias"][:, :], ps[:, :]), "vb")
                setg("vb", s.c["vb"])

        per_chunk_t = sum((sp["N"] // P if sp["kind"] == "fm" else TC // P)
                          for sp in specs)
        nst_chunk = sum((len(sp["outs"]) if sp["kind"] == "fm" else 1)
                        for sp in specs)
        for c in range(NCH):
            b = c % 2
            tsl = slice(c * TC, (c + 1) * TC)
            ab_b = ab[:, b * KT * TC:(b + 1) * KT * TC]
            if c >= 2:
                s.wait(g, "t_c", (c - 1) * per_chunk_t)
            s.inc(g.dma_start(out=ab_b.rearrange("p (kt t) -> p kt t", kt=KT),
                              in_=_fm3(act_src, slice(None), tsl)), f"lda{b}", 16)
            nlda = 16 * (c // 2 + 1)
            st_gate = 16 * nst_chunk * (c // 2) if c >= 2 else 0

            for i, sp in enumerate(specs):
                N = sp["N"]
                if sp["kind"] == "fm":
                    NT = N // P
                    osb_b = sp["osb"][:, b * NT * TC:(b + 1) * NT * TC]
                    if sp.get("res") is not None:
                        rsb_b = sp["rsb"][:, b * NT * TC:(b + 1) * NT * TC]
                        if c >= 2:
                            s.wait(sy, f"v_res{i}", c - 1)
                        s.inc(sy.dma_start(
                            out=rsb_b.rearrange("p (nt t) -> p nt t", nt=NT),
                            in_=_fm3(sp["res"], slice(None), tsl)), f"ldr{i}{b}", 16)
                    for j in range(NT):
                        ps, setg = pool_fm.get(s, t)
                        s.wait(t, f"lda{b}", nlda)
                        s.wait(t, "ldw", LDW)
                        for kt in range(KT):
                            mm = t.matmul(ps[:, :],
                                          sp["wsb"][:, kt * N + j * P: kt * N + j * P + P],
                                          ab_b[:, kt * TC:(kt + 1) * TC],
                                          start=(kt == 0), stop=(kt == KT - 1))
                        s.inc(mm, "t_c")
                        s.wait(sc, "t_c", s.c["t_c"])
                        if c >= 2 and j == 0:
                            s.wait(sc, f"st{b}", st_gate)
                        dst = osb_b[:, j * TC:(j + 1) * TC]
                        if sp.get("bias") is not None:
                            ev = sc.add(dst, ps[:, :], sp["bsb"][:, j:j + 1])
                        elif sp.get("act_fn") is not None:
                            ev = sc.activation(dst, ps[:, :], sp["act_fn"])
                        else:
                            ev = sc.copy(dst, ps[:, :])
                        s.inc(ev, f"s_ev{i}")
                        setg(f"s_ev{i}", s.c[f"s_ev{i}"])
                    if sp.get("res") is not None:
                        s.wait(v, f"s_ev{i}", s.c[f"s_ev{i}"])
                        s.wait(v, f"ldr{i}{b}", 16 * (c // 2 + 1))
                        for j in range(NT):
                            vv = v.tensor_add(osb_b[:, j * TC:(j + 1) * TC],
                                              osb_b[:, j * TC:(j + 1) * TC],
                                              rsb_b[:, j * TC:(j + 1) * TC])
                        s.inc(v.drain(), f"v_res{i}")
                    if "osb_bf" in sp:
                        src_sem = f"v_res{i}" if sp.get("res") is not None else f"s_ev{i}"
                        s.wait(v, src_sem, s.c[src_sem])
                        for j in range(NT):
                            v.tensor_copy(
                                sp["osb_bf"][:, b * NT * TC + j * TC:
                                             b * NT * TC + (j + 1) * TC],
                                osb_b[:, j * TC:(j + 1) * TC])
                        s.inc(v.drain(), f"v_bf{i}")
                else:
                    osb_b = sp["osb"][:, b * (TC // P) * N:(b + 1) * (TC // P) * N]
                    for tt in range(TC // P):
                        ps, setg = pool_tok.get(s, t)
                        s.wait(t, f"lda{b}", nlda)
                        s.wait(t, "ldw", LDW)
                        for h in range(N // 512):
                            for kt in range(KT):
                                mm = t.matmul(
                                    ps[:, h * 512:(h + 1) * 512],
                                    ab_b[:, kt * TC + tt * P: kt * TC + (tt + 1) * P],
                                    sp["wsb"][:, kt * N + h * 512: kt * N + (h + 1) * 512],
                                    start=(kt == 0), stop=(kt == KT - 1))
                        s.inc(mm, "t_c")
                        s.wait(v, "t_c", s.c["t_c"])
                        if c == 0 and tt == 0 and "vb" in s.c:
                            s.wait(v, "vb", s.c["vb"])
                        if c >= 2 and tt == 0:
                            s.wait(v, f"st{b}", st_gate)
                        dst = osb_b[:, tt * N:(tt + 1) * N]
                        if sp.get("bias_row") is not None:
                            ev = v.tensor_add(dst, ps[:, :], sp["vbias"][:, :])
                        else:
                            ev = v.tensor_copy(dst, ps[:, :])
                        s.inc(ev, f"v_ev{i}")
                        setg(f"v_ev{i}", s.c[f"v_ev{i}"])

            for i, sp in enumerate(specs):
                N = sp["N"]
                if sp["kind"] == "fm":
                    NT = N // P
                    osb_b = sp["osb"][:, b * NT * TC:(b + 1) * NT * TC]
                    done_key = (f"v_bf{i}" if "osb_bf" in sp
                                else (f"v_res{i}" if sp.get("res") is not None
                                      else f"s_ev{i}"))
                    s.wait(g, done_key, s.c[done_key])
                    for od, odt in sp["outs"]:
                        src = (sp["osb_bf"][:, b * NT * TC:(b + 1) * NT * TC]
                               if (odt == "bf16" and "osb_bf" in sp) else osb_b)
                        s.inc(g.dma_start(
                            out=_fm3(od, slice(None), tsl),
                            in_=src.rearrange("p (nt t) -> p nt t", nt=NT)),
                            f"st{b}", 16)
                else:
                    osb_b = sp["osb"][:, b * (TC // P) * N:(b + 1) * (TC // P) * N]
                    s.wait(g, f"v_ev{i}", s.c[f"v_ev{i}"])
                    s.inc(g.dma_start(
                        out=sp["out"][tsl, :].rearrange("(tt p) n -> p tt n", p=P),
                        in_=osb_b.rearrange("p (tt n) -> p tt n", tt=TC // P)),
                        f"st{b}", 16)
        for b in range(2):
            if f"st{b}" in s.c:
                s.wait(g, f"st{b}", s.c[f"st{b}"])
        nc.all_engine_barrier()


def phase_attn0(ctx, tag, qk0, v0t, outT_d, outb_d, s0):
    nc = ctx.nc
    g, v, sc, t, sy = nc.gpsimd, nc.vector, nc.scalar, nc.tensor, nc.sync
    NH = 16
    QT = SEQ // P
    with nc.cleanup_on_exit():
        s = Sems(nc, tag)
        qsb = nc.alloc_sbuf_tensor(f"{tag}_q", [P, DT * SEQ], BF16)
        ksb = nc.alloc_sbuf_tensor(f"{tag}_k", [P, DT * SEQ], BF16)
        vsb = nc.alloc_sbuf_tensor(f"{tag}_v", [P, (SEQ // P) * D], BF16)
        ctx_sb = nc.alloc_sbuf_tensor(f"{tag}_ctx", [P, DT * SEQ], BF16)
        psb = nc.alloc_sbuf_tensor(f"{tag}_p", [P, SEQ], BF16)
        ptsb = nc.alloc_sbuf_tensor(f"{tag}_pt", [P, SEQ], BF16)
        maskc = nc.alloc_sbuf_tensor(f"{tag}_mask", [P, P], F32)
        identb = nc.alloc_sbuf_tensor(f"{tag}_id", [P, P], BF16)
        negmax = nc.alloc_sbuf_tensor(f"{tag}_nm", [P, 2], F32)
        denom = nc.alloc_sbuf_tensor(f"{tag}_dn", [P, 2], F32)
        recip = nc.alloc_sbuf_tensor(f"{tag}_rc", [P, 1], F32)
        wsb = nc.alloc_sbuf_tensor(f"{tag}_w", [P, DT * D], BF16)
        bsb = nc.alloc_sbuf_tensor(f"{tag}_b", [P, DT], F32)
        osb = nc.alloc_sbuf_tensor(f"{tag}_o", [P, 2 * DT * TC], F32)
        S_ps = nc.alloc_psum_tensor(f"{tag}_S", [P, SEQ], F32)
        PT_ps = nc.alloc_psum_tensor(f"{tag}_PT", [P, SEQ], BF16)
        cxA = nc.alloc_psum_tensor(f"{tag}_cxA", [P, P], F32)
        cxB = nc.alloc_psum_tensor(f"{tag}_cxB", [P, P], F32)
        st_reg = {"rc": 0, "last": {0: None, 1: None}}

        s.inc(g.dma_start(out=maskc[:, :], in_=ctx.inp("mask_causal", [P, P])), "ldc", 16)
        s.inc(g.dma_start(out=identb[:, :], in_=ctx.inp("identity", [P, P])), "ldc", 16)
        s.inc(g.dma_start(out=wsb[:, :].rearrange("p (kt n) -> p kt n", kt=DT),
                          in_=outT_d.rearrange("(kt p) n -> p kt n", p=P)), "ldc", 16)
        s.inc(g.dma_start(out=bsb[:, :], in_=outb_d[:, :]), "ldc", 16)
        LDC = s.c["ldc"]

        it = 0
        nst = 0
        for sq in range(NSEQ):
            ssl = slice(sq * SEQ, (sq + 1) * SEQ)
            if sq >= 1:
                s.wait(g, "st0", 16 * ((nst + 1) // 2))
                s.wait(g, "st1", 16 * (nst // 2))
            s.inc(g.dma_start(out=qsb[:, :].rearrange("p (kt t) -> p kt t", kt=DT),
                              in_=qk0.rearrange("(kt p) t -> p kt t", p=P)[:, 0:DT, ssl]),
                  f"ldq{sq % 2}", 16)
            s.inc(g.dma_start(out=ksb[:, :].rearrange("p (kt t) -> p kt t", kt=DT),
                              in_=qk0.rearrange("(kt p) t -> p kt t", p=P)[:, DT:2 * DT, ssl]),
                  f"ldq{sq % 2}", 16)
            s.inc(g.dma_start(out=vsb[:, :].rearrange("p (tt n) -> p tt n", tt=SEQ // P),
                              in_=v0t[ssl, :].rearrange("(tt p) n -> p tt n", p=P)),
                  f"ldq{sq % 2}", 16)
            nldq = 48 * (sq // 2 + 1)
            s.wait(v, f"ldq{sq % 2}", nldq)
            s.inc(v.tensor_scalar_mul(qsb[:, :], qsb[:, :], 0.125), "v_qs")

            s.wait(t, "s_op", s.c.get("s_op", 0))

            HALF = SEQ // 2
            def emit_S(iti, h, qt):
                hb = (h % 2) * 64
                qoff = (h // 2) * SEQ
                kr = (qt + 1) * P
                if kr <= HALF:
                    reg = st_reg["rc"] % 2
                    st_reg["rc"] += 1
                    off = reg * HALF
                    gate = st_reg["last"][reg]
                    regs = (reg,)
                else:
                    off = 0
                    gate = max(st_reg["last"][0], st_reg["last"][1])
                    regs = (0, 1)
                q_t = qsb[hb:hb + 64, qoff + qt * P: qoff + (qt + 1) * P]
                k_r = ksb[hb:hb + 64, qoff: qoff + kr]
                s.wait(t, f"ldq{sq % 2}", nldq)
                s.wait(t, "v_qs", sq + 1)
                s.wait(t, "ldc", LDC)
                if gate is not None:
                    s.wait(t, "s_e", gate + 1)
                for kk in range(qt + 1):
                    mm = t.matmul(S_ps[:, off + kk * P: off + (kk + 1) * P], q_t,
                                  k_r[:, kk * P:(kk + 1) * P],
                                  start=True, stop=True)
                s.inc(mm, "t_s")
                for r in regs:
                    st_reg["last"][r] = iti
                return off, gate, kr

            def emit_epi(iti, h, qt, off, gate, kr):
                hb = (h % 2) * 64
                qoff = (h // 2) * SEQ
                cxp = [cxA, cxB][iti % 2]
                s.wait(v, "t_s", iti + 1)
                if iti >= 2:
                    s.wait(v, "s_e", iti - 1)
                v.tensor_add(S_ps[:, off + qt * P: off + qt * P + P],
                             S_ps[:, off + qt * P: off + qt * P + P], maskc[:, :])
                v.drain()
                s.inc(v.reduce_max(negmax[:, iti % 2: iti % 2 + 1],
                                   S_ps[:, off: off + kr], axis=AX.X,
                                   negate=True), "v_m")
                s.wait(sc, "v_m", iti + 1)
                if gate is not None:
                    s.wait(sc, "t_pt", gate + 1)
                if iti >= 2:
                    s.wait(sc, "v_r", iti - 1)
                s.inc(sc.activation(psb[:, off: off + kr], S_ps[:, off: off + kr],
                                    AF.Exp, bias=negmax[:, iti % 2: iti % 2 + 1],
                                    accum_out=denom[:, iti % 2: iti % 2 + 1]), "s_e")
                s.wait(v, "s_e", iti + 1)
                v.reciprocal(recip[:, 0:1], denom[:, iti % 2: iti % 2 + 1])
                v.drain()
                s.inc(v.tensor_scalar_mul(psb[:, off: off + kr],
                                          psb[:, off: off + kr],
                                          recip[:, 0:1]), "v_r")
                s.wait(t, "v_r", iti + 1)
                if gate is not None:
                    s.wait(t, "s_pt", gate + 1)
                for kk in range(qt + 1):
                    tp = t.transpose(PT_ps[:, off + kk * P: off + (kk + 1) * P],
                                     psb[:, off + kk * P: off + (kk + 1) * P],
                                     identb[:, :])
                s.inc(tp, "t_pt")
                s.wait(sc, "t_pt", iti + 1)
                if gate is not None:
                    s.wait(sc, "t_av", gate + 1)
                s.inc(sc.copy(ptsb[:, off: off + kr], PT_ps[:, off: off + kr]),
                      "s_pt")
                s.wait(t, "s_pt", iti + 1)
                if iti >= 2:
                    s.wait(t, "s_cx", iti - 1)
                for kk in range(qt + 1):
                    mm = t.matmul(cxp[hb:hb + 64, :],
                                  vsb[:, kk * D + h * 64: kk * D + h * 64 + 64],
                                  ptsb[:, off + kk * P: off + (kk + 1) * P],
                                  start=(kk == 0), stop=(kk == qt),
                                  tile_position=(0, hb))
                s.inc(mm, "t_av")
                s.wait(sc, "t_av", iti + 1)
                s.inc(sc.copy(ctx_sb[hb:hb + 64, qoff + qt * P: qoff + (qt + 1) * P],
                              cxp[hb:hb + 64, :]), "s_cx")

            pend = None
            for h in range(NH):
                for qt in range(QT):
                    off, gate, kr = emit_S(it, h, qt)
                    if pend is not None:
                        emit_epi(*pend)
                    pend = (it, h, qt, off, gate, kr)
                    it += 1
            if pend is not None:
                emit_epi(*pend)
                pend = None

            for c4 in range(SEQ // TC):
                b = c4 % 2
                osb_b = osb[:, b * DT * TC:(b + 1) * DT * TC]
                csl = slice(c4 * TC, (c4 + 1) * TC)
                for j in range(DT):
                    op_n = s.c.get("s_op", 0)
                    ps_o = S_ps[:, (op_n % 2) * TC:(op_n % 2 + 1) * TC]
                    s.wait(t, "s_cx", it)
                    if op_n >= 2:
                        s.wait(t, "s_op", op_n - 1)
                    for kt in range(DT):
                        mm = t.matmul(ps_o, wsb[:, kt * D + j * P: kt * D + j * P + P],
                                      ctx_sb[:, kt * SEQ + c4 * TC: kt * SEQ + (c4 + 1) * TC],
                                      start=(kt == 0), stop=(kt == DT - 1))
                    s.inc(mm, "t_op")
                    s.wait(sc, "t_op", s.c["t_op"])
                    if nst >= 2 and j == 0:
                        s.wait(sc, f"st{nst % 2}", 16 * (nst // 2))
                    s.inc(sc.add(osb_b[:, j * TC:(j + 1) * TC], ps_o, bsb[:, j:j + 1]),
                          "s_op")
                s.wait(sy, "s_op", s.c["s_op"])
                s.inc(sy.dma_start(out=_fm3(s0, slice(None), slice(sq * SEQ + c4 * TC,
                                                                   sq * SEQ + (c4 + 1) * TC)),
                                   in_=osb_b.rearrange("p (nt t) -> p nt t", nt=DT)),
                      f"st{nst % 2}", 16)
                nst += 1
        s.wait(sy, "st0", 16 * ((nst + 1) // 2))
        s.wait(sy, "st1", 16 * (nst // 2))
        nc.all_engine_barrier()


def phase_router(ctx, tag, xn0f, wrT_d, gates_d):
    nc = ctx.nc
    g, v, sc, t, sy = nc.gpsimd, nc.vector, nc.scalar, nc.tensor, nc.sync
    NT_TOK = TOK // P
    with nc.cleanup_on_exit():
        s = Sems(nc, tag)
        xf = nc.alloc_sbuf_tensor(f"{tag}_xf", [P, 2 * DT * TC], F32)
        wr = nc.alloc_sbuf_tensor(f"{tag}_wr", [P, DT * 8], F32)
        idf = nc.alloc_sbuf_tensor(f"{tag}_id", [P, P], F32)
        probs = nc.alloc_sbuf_tensor(f"{tag}_pr", [P, 8], F32)
        gates_t = nc.alloc_sbuf_tensor(f"{tag}_gt", [P, 8], F32)
        nm8 = nc.alloc_sbuf_tensor(f"{tag}_nm", [P, 1], F32)
        dn8 = nc.alloc_sbuf_tensor(f"{tag}_dn", [P, 1], F32)
        m7 = nc.alloc_sbuf_tensor(f"{tag}_m7", [P, 1], F32)
        rs = nc.alloc_sbuf_tensor(f"{tag}_rs", [P, 1], F32)
        gsb = nc.alloc_sbuf_tensor(f"{tag}_g", [8, TOK], F32)
        pool_lg = PsumPool(nc, tag + "l", 2, [P, 8])
        pool_tr = PsumPool(nc, tag + "t", 2, [8, P])

        s.inc(g.dma_start(out=wr[:, :].rearrange("p (kt e) -> p kt e", kt=DT),
                          in_=wrT_d.rearrange("(kt p) e -> p kt e", p=P)), "ldc", 16)
        s.inc(g.dma_start(out=idf[:, :], in_=ctx.inp("identity", [P, P])), "ldc", 16)

        for tt in range(NT_TOK):
            c, b = tt // 4, (tt // 4) % 2
            if tt % 4 == 0:
                if c >= 2:
                    s.wait(sy, "t_lg", 4 * (c - 1))
                s.inc(sy.dma_start(
                    out=xf[:, b * DT * TC:(b + 1) * DT * TC].rearrange(
                        "p (kt t) -> p kt t", kt=DT),
                    in_=_fm3(xn0f, slice(None), slice(c * TC, (c + 1) * TC))),
                    f"mrd{b}", 16)
            xt_off = b * DT * TC + (tt % 4) * P
            lg, setg_lg = pool_lg.get(s, t)
            s.wait(t, f"mrd{b}", 16 * (c // 2 + 1))
            s.wait(t, "ldc", 32)
            for kt in range(DT):
                mm = t.matmul(lg[:, :], xf[:, xt_off + kt * TC: xt_off + kt * TC + P],
                              wr[:, kt * 8:(kt + 1) * 8],
                              start=(kt == 0), stop=(kt == DT - 1))
            s.inc(mm, "t_lg")
            s.wait(v, "t_lg", tt + 1)
            if tt >= 1:
                s.wait(v, "s_e8", tt)
            s.inc(v.reduce_max(nm8[:, 0:1], lg[:, :], axis=AX.X, negate=True), "v_m8")
            s.wait(sc, "v_m8", tt + 1)
            if tt >= 1:
                s.wait(sc, "v_g", tt)
            s.inc(sc.activation(probs[:, :], lg[:, :], AF.Exp, bias=nm8[:, 0:1],
                                accum_out=dn8[:, 0:1]), "s_e8")
            setg_lg("s_e8", s.c["s_e8"])
            s.wait(v, "s_e8", tt + 1)
            if tt >= 1:
                s.wait(v, "t_tr", tt)
            v.reciprocal(rs[:, 0:1], dn8[:, 0:1])
            v.drain()
            v.tensor_scalar_mul(probs[:, :], probs[:, :], rs[:, 0:1])
            v.drain()
            v.reduce_max(m7[:, 0:1], probs[:, 1:8], axis=AX.X)
            v.drain()
            v.tensor_scalar(gates_t[:, 1:8], probs[:, 1:8], m7[:, 0:1], m7[:, 0:1],
                            ALU.is_equal, ALU.mult)
            v.drain()
            v.tensor_add(rs[:, 0:1], probs[:, 0:1], m7[:, 0:1])
            v.drain()
            v.tensor_scalar_max(rs[:, 0:1], rs[:, 0:1], 1e-9)
            v.drain()
            v.reciprocal(rs[:, 0:1], rs[:, 0:1])
            v.drain()
            v.tensor_scalar_mul(gates_t[:, 1:8], gates_t[:, 1:8], rs[:, 0:1])
            v.drain()
            s.inc(v.tensor_scalar(gates_t[:, 0:1], probs[:, 0:1], rs[:, 0:1], None,
                                  ALU.mult), "v_g")
            trp, setg_tr = pool_tr.get(s, t)
            s.wait(t, "v_g", tt + 1)
            s.inc(t.transpose(trp[:, :], gates_t[:, :], idf[:, :]), "t_tr")
            s.wait(sc, "t_tr", tt + 1)
            s.inc(sc.copy(gsb[0:8, tt * P:(tt + 1) * P], trp[:, :]), "s_gt")
            setg_tr("s_gt", s.c["s_gt"])
        s.wait(g, "s_gt", NT_TOK)
        s.inc(g.dma_start(out=gates_d[:, :], in_=gsb[:, :]), "st", 16)
        s.wait(g, "st", 16)
        nc.all_engine_barrier()


def phase_moe(ctx, tag, xn0, w1T_d, w3T_d, w2T_d, gates_d, g0):
    nc = ctx.nc
    g, v, sc, t, sy = nc.gpsimd, nc.vector, nc.scalar, nc.tensor, nc.sync
    FC = 2048
    KT2 = FC // P
    NE = 8
    NROUND = NE * (4096 // FC)
    if MOE_MAX_ROUNDS is not None:
        NROUND = min(NROUND, MOE_MAX_ROUNDS)
    with nc.cleanup_on_exit():
        s = Sems(nc, tag)
        w1sb = nc.alloc_sbuf_tensor(f"{tag}_w1", [P, DT * FC], BF16)
        w3sb = nc.alloc_sbuf_tensor(f"{tag}_w3", [P, DT * FC], BF16)
        w2sb = nc.alloc_sbuf_tensor(f"{tag}_w2", [P, KT2 * D], BF16)
        xb = nc.alloc_sbuf_tensor(f"{tag}_x", [P, 2 * DT * TC], BF16)
        hsb = nc.alloc_sbuf_tensor(f"{tag}_h", [P, KT2 * TC], BF16)
        s1a = nc.alloc_sbuf_tensor(f"{tag}_s1a", [P, TC], BF16)
        s1b = nc.alloc_sbuf_tensor(f"{tag}_s1b", [P, TC], BF16)
        grow = nc.alloc_sbuf_tensor(f"{tag}_gr", [1, TOK], F32)
        onesr = nc.alloc_sbuf_tensor(f"{tag}_or", [1, P], F32)
        osb = nc.alloc_sbuf_tensor(f"{tag}_o", [P, 2 * DT * TC], F32)
        pp1 = PsumPool(nc, tag + "1", 2, [P, TC])
        pp3 = PsumPool(nc, tag + "3", 2, [P, TC])
        ppb = PsumPool(nc, tag + "b", 2, [P, TC])
        ppo = PsumPool(nc, tag + "o", 2, [P, TC])
        s.inc(g.memset(onesr[:, :], 1.0), "g")

        gidx = 0
        for r in range(NROUND):
            e, half = r // 2, r % 2
            fsl = slice(half * FC, (half + 1) * FC)
            if r >= 1:
                s.wait(g, "t_o", 8 * 16 * r)
            s.inc(g.dma_start(out=w1sb[:, :].rearrange("p (kt n) -> p kt n", kt=DT),
                              in_=w1T_d[e][:, fsl].rearrange("(kt p) n -> p kt n", p=P)),
                  "ldw", 16)
            s.inc(g.dma_start(out=w3sb[:, :].rearrange("p (kt n) -> p kt n", kt=DT),
                              in_=w3T_d[e][:, fsl].rearrange("(kt p) n -> p kt n", p=P)),
                  "ldw", 16)
            s.inc(g.dma_start(out=w2sb[:, :].rearrange("p (kt n) -> p kt n", kt=KT2),
                              in_=w2T_d[e][fsl, :].rearrange("(kt p) n -> p kt n", p=P)),
                  "ldw", 16)
            if r >= 1:
                s.wait(sy, "t_bc", 16 * r)
            s.inc(sy.dma_start(out=grow[:, :], in_=gates_d[e:e + 1, :]), "ldg", 16)
            NLDW = 48 * (r + 1)

            for c in range(NCH):
                b = c % 2
                tsl = slice(c * TC, (c + 1) * TC)
                xb_b = xb[:, b * DT * TC:(b + 1) * DT * TC]
                if gidx >= 2:
                    s.wait(g, "t_p", 16 * (gidx - 1))
                s.inc(g.dma_start(out=xb_b.rearrange("p (kt t) -> p kt t", kt=DT),
                                  in_=_fm3(xn0, slice(None), tsl)), f"xl{b}", 16)
                nxl = 16 * (gidx // 2 + 1)
                bc, setg_bc = ppb.get(s, t)
                s.wait(t, "ldg", 16 * (r + 1))
                s.wait(t, "g", 1)
                s.inc(t.matmul(bc[:, :], onesr[:, :], grow[0:1, tsl],
                               start=True, stop=True), "t_bc")
                for j in range(KT2):
                    p1, setg_1 = pp1.get(s, t)
                    s.wait(t, f"xl{b}", nxl)
                    s.wait(t, "ldw", NLDW)
                    for kt in range(DT):
                        t.matmul(p1[:, :], w1sb[:, kt * FC + j * P: kt * FC + j * P + P],
                                 xb_b[:, kt * TC:(kt + 1) * TC],
                                 start=(kt == 0), stop=(kt == DT - 1))
                    p3, setg_3 = pp3.get(s, t)
                    for kt in range(DT):
                        mm = t.matmul(p3[:, :], w3sb[:, kt * FC + j * P: kt * FC + j * P + P],
                                      xb_b[:, kt * TC:(kt + 1) * TC],
                                      start=(kt == 0), stop=(kt == DT - 1))
                    s.inc(mm, "t_p")
                    njh = s.c["t_p"]
                    s1t = [s1a, s1b][j % 2]
                    s.wait(sc, "t_p", njh)
                    if njh >= 2:
                        s.wait(sc, "v_h", njh - 1)
                    s.inc(sc.activation(s1t[:, :], p1[:, :], AF.Silu), "s_h")
                    setg_1("s_h", s.c["s_h"])
                    s.wait(v, "s_h", njh)
                    if j == 0 and gidx >= 1:
                        s.wait(v, "t_o", 8 * gidx)
                    hj = hsb[:, j * TC:(j + 1) * TC]
                    v.tensor_mul(hj, s1t[:, :], p3[:, :])
                    v.drain()
                    s.inc(v.tensor_mul(hj, hj, bc[:, :]), "v_h")
                    setg_3("v_h", s.c["v_h"])
                setg_bc("v_h", s.c["v_h"])
                for jn in range(DT):
                    po, setg_o = ppo.get(s, t)
                    s.wait(t, "v_h", 16 * (gidx + 1))
                    for kk in range(KT2):
                        mm = t.matmul(po[:, :], w2sb[:, kk * D + jn * P: kk * D + jn * P + P],
                                      hsb[:, kk * TC:(kk + 1) * TC],
                                      start=(kk == 0), stop=(kk == KT2 - 1))
                    s.inc(mm, "t_o")
                    s.wait(sc, "t_o", s.c["t_o"])
                    if gidx >= 2 and jn == 0:
                        s.wait(sc, f"acc{b}", 16 * (gidx // 2))
                    s.inc(sc.copy(osb[:, b * DT * TC + jn * TC: b * DT * TC + (jn + 1) * TC],
                                  po[:, :]), "s_o")
                    setg_o("s_o", s.c["s_o"])
                s.wait(g, "s_o", 8 * (gidx + 1))
                if c == 0 and r >= 1:
                    s.wait(g, "acc0", 16 * 8 * r)
                    s.wait(g, "acc1", 16 * 8 * r)
                if r == 0:
                    dma = g.dma_start(out=_fm3(g0, slice(None), tsl),
                                      in_=osb[:, b * DT * TC:(b + 1) * DT * TC]
                                      .rearrange("p (nt t) -> p nt t", nt=DT))
                else:
                    dma = g.dma_start(out=_fm3(g0, slice(None), tsl),
                                      in_=osb[:, b * DT * TC:(b + 1) * DT * TC]
                                      .rearrange("p (nt t) -> p nt t", nt=DT),
                                      accum_op=ALU.add)
                s.inc(dma, f"acc{b}", 16)
                gidx += 1
        s.wait(g, "acc0", 16 * 8 * NROUND)
        s.wait(g, "acc1", 16 * 8 * NROUND)
        nc.all_engine_barrier()


def phase_combine(ctx, tag, base_d, s_d, gpre_d, out_d):
    nc = ctx.nc
    g, v, sc, t, sy = nc.gpsimd, nc.vector, nc.scalar, nc.tensor, nc.sync
    W = DT * TC
    with nc.cleanup_on_exit():
        s = Sems(nc, tag)
        bsb = nc.alloc_sbuf_tensor(f"{tag}_b", [P, 2 * W], F32)
        ssb = nc.alloc_sbuf_tensor(f"{tag}_s", [P, 2 * W], F32)
        gsb = nc.alloc_sbuf_tensor(f"{tag}_g", [P, 2 * W], F32)
        for c in range(NCH):
            b = c % 2
            tsl = slice(c * TC, (c + 1) * TC)
            bb = bsb[:, b * W:(b + 1) * W]
            sb = ssb[:, b * W:(b + 1) * W]
            gb = gsb[:, b * W:(b + 1) * W]
            if c >= 2:
                s.wait(sy, "v_c", c - 1)
                s.wait(sy, f"st{b}", 16 * (c // 2))
                s.wait(g, "v_c", c - 1)
            s.inc(sy.dma_start(out=bb.rearrange("p (kt t) -> p kt t", kt=DT),
                               in_=_fm3(base_d, slice(None), tsl)), f"lb{b}", 16)
            s.inc(sy.dma_start(out=sb.rearrange("p (kt t) -> p kt t", kt=DT),
                               in_=_fm3(s_d, slice(None), tsl)), f"lb{b}", 16)
            s.inc(g.dma_start(out=gb.rearrange("p (kt t) -> p kt t", kt=DT),
                              in_=_fm3(gpre_d, slice(None), tsl)), f"lg{b}", 16)
            s.wait(sc, f"lg{b}", 16 * (c // 2 + 1))
            if c >= 2:
                s.wait(sc, "v_c", c - 1)
            s.inc(sc.activation(gb, gb, AF.Silu), "s_si")
            s.wait(v, "s_si", c + 1)
            s.wait(v, f"lb{b}", 32 * (c // 2 + 1))
            v.tensor_mul(sb, sb, gb)
            v.drain()
            v.tensor_add(bb, bb, sb)
            s.inc(v.drain(), "v_c")
            s.wait(g, "v_c", c + 1)
            s.inc(g.dma_start(out=_fm3(out_d, slice(None), tsl),
                              in_=bb.rearrange("p (kt t) -> p kt t", kt=DT)),
                  f"st{b}", 16)
        s.wait(g, "st0", 16 * (NCH // 2))
        s.wait(g, "st1", 16 * (NCH // 2))
        nc.all_engine_barrier()


def phase_attn1(ctx, tag, q1l, k1l, v1lt, xn1f, wol_d, wqg_d, wkg_d, wvg_d, wog_d, s1):
    nc = ctx.nc
    g, v, sc, t, sy = nc.gpsimd, nc.vector, nc.scalar, nc.tensor, nc.sync
    CH = 256
    NCHK = TOK // CH
    NH = 16
    with nc.cleanup_on_exit():
        s = Sems(nc, tag)
        qsb = nc.alloc_sbuf_tensor(f"{tag}_q", [P, 2 * DT * CH], BF16)
        ksb = nc.alloc_sbuf_tensor(f"{tag}_k", [P, 2 * DT * CH], BF16)
        vsb = nc.alloc_sbuf_tensor(f"{tag}_v", [P, 2 * (CH // P) * D], BF16)
        xfb = nc.alloc_sbuf_tensor(f"{tag}_xf", [P, 2 * DT * CH], F32)
        ctx_sb = nc.alloc_sbuf_tensor(f"{tag}_ctx", [P, DT * CH], BF16)
        psb = nc.alloc_sbuf_tensor(f"{tag}_p", [P, 2 * CH], BF16)
        ptsb = nc.alloc_sbuf_tensor(f"{tag}_pt", [P, 2 * CH], BF16)
        identb = nc.alloc_sbuf_tensor(f"{tag}_id", [P, P], BF16)
        negmax = nc.alloc_sbuf_tensor(f"{tag}_nm", [P, 2], F32)
        denom = nc.alloc_sbuf_tensor(f"{tag}_dn", [P, 2], F32)
        recip = nc.alloc_sbuf_tensor(f"{tag}_rc", [P, 1], F32)
        wsb = nc.alloc_sbuf_tensor(f"{tag}_wo", [P, DT * D], BF16)
        osb = nc.alloc_sbuf_tensor(f"{tag}_o", [P, 2 * DT * CH], F32)
        summ = nc.alloc_sbuf_tensor(f"{tag}_sm", [P, DT * NCHK], F32)
        summb = nc.alloc_sbuf_tensor(f"{tag}_smb", [P, DT * NCHK], BF16)
        wg = nc.alloc_sbuf_tensor(f"{tag}_wg", [P, 3 * DT * D], BF16)
        wvg = nc.alloc_sbuf_tensor(f"{tag}_wvg", [P, DT * D], BF16)
        qg = nc.alloc_sbuf_tensor(f"{tag}_qg", [P, DT * NCHK], BF16)
        kg = nc.alloc_sbuf_tensor(f"{tag}_kg", [P, DT * NCHK], BF16)
        vg = [nc.alloc_sbuf_tensor(f"{tag}_vg{s_}", [8, D], BF16) for s_ in range(NSEQ)]
        cg = nc.alloc_sbuf_tensor(f"{tag}_cg", [P, DT * NCHK], BF16)
        go = nc.alloc_sbuf_tensor(f"{tag}_go", [P, DT * NCHK], F32)
        pg = nc.alloc_sbuf_tensor(f"{tag}_pg", [8, 8], BF16)
        S_ps = nc.alloc_psum_tensor(f"{tag}_S", [P, 1024], F32)
        PT_ps = nc.alloc_psum_tensor(f"{tag}_PT", [P, 2048], BF16)
        cxA = nc.alloc_psum_tensor(f"{tag}_cxA", [P, P], F32)
        cxB = nc.alloc_psum_tensor(f"{tag}_cxB", [P, P], F32)
        op_ps = PsumPool(nc, tag + "o", 2, [P, CH])
        vg_ps = S_ps[0:8, 0:1024]

        s.inc(g.dma_start(out=identb[:, :], in_=ctx.inp("identity", [P, P])), "ldc", 16)
        s.inc(g.dma_start(out=wsb[:, :].rearrange("p (kt n) -> p kt n", kt=DT),
                          in_=wol_d.rearrange("(kt p) n -> p kt n", p=P)), "ldc", 16)
        for wi, wd in enumerate([wqg_d, wkg_d, wog_d]):
            s.inc(g.dma_start(out=wg[:, wi * DT * D:(wi + 1) * DT * D]
                              .rearrange("p (kt n) -> p kt n", kt=DT),
                              in_=wd.rearrange("(kt p) n -> p kt n", p=P)), "ldc", 16)
        s.inc(g.dma_start(out=wvg[:, :].rearrange("p (kt n) -> p kt n", kt=DT),
                          in_=wvg_d.rearrange("(kt p) n -> p kt n", p=P)), "ldc", 16)
        LDC = s.c["ldc"]

        it = 0
        nst = 0
        for ci in range(NCHK):
            cb = ci % 2
            csl = slice(ci * CH, (ci + 1) * CH)
            qb = qsb[:, cb * DT * CH:(cb + 1) * DT * CH]
            kb = ksb[:, cb * DT * CH:(cb + 1) * DT * CH]
            vb = vsb[:, cb * (CH // P) * D:(cb + 1) * (CH // P) * D]
            xb = xfb[:, cb * DT * CH:(cb + 1) * DT * CH]
            if ci >= 2:
                s.wait(g, "t_av", (ci - 1) * NH * 2)
                s.wait(g, "v_res", ci - 1)
            s.inc(g.dma_start(out=qb.rearrange("p (kt t) -> p kt t", kt=DT),
                              in_=_fm3(q1l, slice(None), csl)), f"lq{cb}", 16)
            s.inc(g.dma_start(out=kb.rearrange("p (kt t) -> p kt t", kt=DT),
                              in_=_fm3(k1l, slice(None), csl)), f"lq{cb}", 16)
            s.inc(g.dma_start(out=vb.rearrange("p (tt n) -> p tt n", tt=CH // P),
                              in_=v1lt[csl, :].rearrange("(tt p) n -> p tt n", p=P)),
                  f"lq{cb}", 16)
            s.inc(g.dma_start(out=xb.rearrange("p (kt t) -> p kt t", kt=DT),
                              in_=_fm3(xn1f, slice(None), csl)), f"lq{cb}", 16)
            nlq = 64 * (ci // 2 + 1)
            s.wait(v, f"lq{cb}", nlq)
            if ci >= 2:
                s.wait(v, "t_s", (ci - 1) * NH * 2)
            s.inc(v.tensor_scalar_mul(qb[:, :], qb[:, :], 0.125), "v_qs")

            def emit_S1(iti, h, qt):
                hb = (h % 2) * 64
                qoff = (h // 2) * CH
                off = (iti % 2) * 512
                q_t = qb[hb:hb + 64, qoff + qt * P: qoff + (qt + 1) * P]
                s.wait(t, "v_qs", ci + 1)
                s.wait(t, "ldc", LDC)
                if iti >= 2:
                    s.wait(t, "s_e", iti - 1)
                for kk in range(CH // P):
                    mm = t.matmul(S_ps[:, off + kk * P: off + (kk + 1) * P], q_t,
                                  kb[hb:hb + 64, qoff + kk * P: qoff + (kk + 1) * P],
                                  start=True, stop=True)
                s.inc(mm, "t_s")
                return off

            def emit_epi1(iti, h, qt, off):
                hb = (h % 2) * 64
                qoff = (h // 2) * CH
                po = (iti % 2) * CH
                pto = (iti % 2) * 1024
                nm = negmax[:, iti % 2: iti % 2 + 1]
                dn = denom[:, iti % 2: iti % 2 + 1]
                cxp = [cxA, cxB][iti % 2]
                s.wait(v, "t_s", iti + 1)
                if iti >= 2:
                    s.wait(v, "s_e", iti - 1)
                s.inc(v.reduce_max(nm, S_ps[:, off: off + CH], axis=AX.X,
                                   negate=True), "v_m")
                s.wait(sc, "v_m", iti + 1)
                if iti >= 2:
                    s.wait(sc, "t_pt", iti - 1)
                    s.wait(sc, "v_r", iti - 1)
                s.inc(sc.activation(psb[:, po: po + CH], S_ps[:, off: off + CH],
                                    AF.Exp, bias=nm, accum_out=dn), "s_e")
                s.wait(v, "s_e", iti + 1)
                v.reciprocal(recip[:, 0:1], dn)
                v.drain()
                s.inc(v.tensor_scalar_mul(psb[:, po: po + CH], psb[:, po: po + CH],
                                          recip[:, 0:1]), "v_r")
                s.wait(t, "v_r", iti + 1)
                if iti >= 2:
                    s.wait(t, "s_pt", iti - 1)
                for kk in range(CH // P):
                    tp = t.transpose(PT_ps[:, pto + kk * P: pto + (kk + 1) * P],
                                     psb[:, po + kk * P: po + (kk + 1) * P],
                                     identb[:, :])
                s.inc(tp, "t_pt")
                s.wait(sc, "t_pt", iti + 1)
                if iti >= 2:
                    s.wait(sc, "t_av", iti - 1)
                s.inc(sc.copy(ptsb[:, po: po + CH], PT_ps[:, pto: pto + CH]), "s_pt")
                s.wait(t, "s_pt", iti + 1)
                if iti >= 2:
                    s.wait(t, "s_cx", iti - 1)
                for kk in range(CH // P):
                    mm = t.matmul(cxp[hb:hb + 64, :],
                                  vb[:, kk * D + h * 64: kk * D + h * 64 + 64],
                                  ptsb[:, po + kk * P: po + (kk + 1) * P],
                                  start=(kk == 0), stop=(kk == CH // P - 1),
                                  tile_position=(0, hb))
                s.inc(mm, "t_av")
                s.wait(sc, "t_av", iti + 1)
                s.inc(sc.copy(ctx_sb[hb:hb + 64, qoff + qt * P: qoff + (qt + 1) * P],
                              cxp[hb:hb + 64, :]), "s_cx")

            pend1 = None
            for h in range(NH):
                for qt in range(CH // P):
                    off = emit_S1(it, h, qt)
                    if pend1 is not None:
                        emit_epi1(*pend1)
                    pend1 = (it, h, qt, off)
                    it += 1
            if pend1 is not None:
                emit_epi1(*pend1)
                pend1 = None

            ob = osb[:, cb * DT * CH:(cb + 1) * DT * CH]
            for j in range(DT):
                po, setg_o = op_ps.get(s, t)
                s.wait(t, "s_cx", it)
                for kt in range(DT):
                    mm = t.matmul(po[:, :], wsb[:, kt * D + j * P: kt * D + j * P + P],
                                  ctx_sb[:, kt * CH:(kt + 1) * CH],
                                  start=(kt == 0), stop=(kt == DT - 1))
                s.inc(mm, "t_op")
                s.wait(sc, "t_op", s.c["t_op"])
                if ci >= 2 and j == 0:
                    s.wait(sc, f"st{cb}", 16 * (ci // 2))
                s.inc(sc.copy(ob[:, j * CH:(j + 1) * CH], po[:, :]), "s_op")
                setg_o("s_op", s.c["s_op"])
            s.wait(v, "s_op", s.c["s_op"])
            for j in range(DT):
                v.tensor_add(ob[:, j * CH:(j + 1) * CH], ob[:, j * CH:(j + 1) * CH],
                             xb[:, j * CH:(j + 1) * CH])
            s.inc(v.drain(), "v_res")
            for j in range(DT):
                v.reduce_sum(summ[:, j * NCHK + ci: j * NCHK + ci + 1],
                             ob[:, j * CH:(j + 1) * CH], axis=AX.X)
            s.inc(v.drain(), "v_sum")
            s.wait(sy, "v_res", ci + 1)
            s.inc(sy.dma_start(out=_fm3(s1, slice(None), csl),
                               in_=ob.rearrange("p (kt t) -> p kt t", kt=DT)),
                  f"st{cb}", 16)
            nst += 1

        PT0_view = S_ps
        PT1_view = PT_ps
        s.wait(v, "v_sum", NCHK)
        v.tensor_scalar_mul(summ[:, :], summ[:, :], 1.0 / CH)
        v.drain()
        s.inc(v.tensor_copy(summb[:, :], summ[:, :]), "v_gb")
        for oi, (woff, dst) in enumerate([(0, qg), (DT * D, kg)]):
            for j in range(DT):
                po, setg_o = op_ps.get(s, t)
                s.wait(t, "v_gb", 1)
                for kt in range(DT):
                    mm = t.matmul(po[:, 0:NCHK],
                                  wg[:, woff + kt * D + j * P: woff + kt * D + j * P + P],
                                  summb[:, kt * NCHK:(kt + 1) * NCHK],
                                  start=(kt == 0), stop=(kt == DT - 1))
                s.inc(mm, "t_g")
                s.wait(v, "t_g", s.c["t_g"])
                s.inc(v.tensor_copy(dst[:, j * NCHK:(j + 1) * NCHK], po[:, 0:NCHK]),
                      "v_gq")
                setg_o("v_gq", s.c["v_gq"])
        s.wait(v, "v_gq", s.c["v_gq"])
        s.inc(v.tensor_scalar_mul(qg[:, :], qg[:, :], 0.125), "v_gq2")
        for sq in range(NSEQ):
            s.wait(t, "v_gb", 1)
            if sq >= 1:
                s.wait(t, "v_gq", 16 + sq)
            for half in range(2):
                for kt in range(DT):
                    mm = t.matmul(vg_ps[0:8, half * 512:(half + 1) * 512],
                                  summb[:, kt * NCHK + 8 * sq: kt * NCHK + 8 * sq + 8],
                                  wvg[:, kt * D + half * 512: kt * D + (half + 1) * 512],
                                  start=(kt == 0), stop=(kt == DT - 1))
            s.inc(mm, "t_g")
            s.wait(v, "t_g", s.c["t_g"])
            s.inc(v.tensor_copy(vg[sq][0:8, :], vg_ps[0:8, 0:D]), "v_gq")

        git = 0
        for sq in range(NSEQ):
            for h in range(NH):
                hb = (h % 2) * 64
                kt_h = h // 2
                coff = kt_h * NCHK + 8 * sq
                s.wait(t, "v_gq2", 1)
                s.wait(t, "v_gq", 16 + NSEQ)
                if git == 0:
                    s.wait(t, "s_e", it)
                    s.wait(t, "s_pt", it)
                    s.wait(t, "s_cx", it)
                else:
                    s.wait(t, "s_ge", git)
                mm = t.matmul(PT0_view[0:8, 0:8], qg[hb:hb + 64, coff:coff + 8],
                              kg[hb:hb + 64, coff:coff + 8], start=True, stop=True)
                s.inc(mm, "t_gs")
                s.wait(v, "t_gs", git + 1)
                s.inc(v.reduce_max(negmax[0:8, 0:1], PT0_view[0:8, 0:8], axis=AX.X,
                                   negate=True), "v_gm")
                s.wait(sc, "v_gm", git + 1)
                if git >= 1:
                    s.wait(sc, "t_gav", git)
                s.inc(sc.activation(pg[0:8, 0:8], PT0_view[0:8, 0:8], AF.Exp,
                                    bias=negmax[0:8, 0:1],
                                    accum_out=denom[0:8, 0:1]), "s_ge")
                s.wait(v, "s_ge", git + 1)
                v.reciprocal(recip[0:8, 0:1], denom[0:8, 0:1])
                v.drain()
                s.inc(v.tensor_scalar_mul(pg[0:8, 0:8], pg[0:8, 0:8],
                                          recip[0:8, 0:1]), "v_gr")
                s.wait(t, "v_gr", git + 1)
                if git >= 1:
                    s.wait(t, "s_gpt", git)
                s.inc(t.transpose(PT1_view[0:8, 0:8], pg[0:8, 0:8], identb[0:8, 0:8]),
                      "t_gpt")
                s.wait(sc, "t_gpt", git + 1)
                if git >= 1:
                    s.wait(sc, "t_gav", git)
                s.inc(sc.copy(pg[0:8, 0:8], PT1_view[0:8, 0:8]), "s_gpt")
                s.wait(t, "s_gpt", git + 1)
                if git >= 1:
                    s.wait(t, "s_gcx", git)
                s.inc(t.matmul(cxA[hb:hb + 64, 0:8], vg[sq][0:8, h * 64:(h + 1) * 64],
                               pg[0:8, 0:8], start=True, stop=True,
                               tile_position=(0, hb)), "t_gav")
                s.wait(sc, "t_gav", git + 1)
                s.inc(sc.copy(cg[hb:hb + 64, coff:coff + 8], cxA[hb:hb + 64, 0:8]),
                      "s_gcx")
                git += 1
        for j in range(DT):
            po, setg_o = op_ps.get(s, t)
            s.wait(t, "s_gcx", git)
            for kt in range(DT):
                mm = t.matmul(po[:, 0:NCHK],
                              wg[:, 2 * DT * D + kt * D + j * P: 2 * DT * D + kt * D + j * P + P],
                              cg[:, kt * NCHK:(kt + 1) * NCHK],
                              start=(kt == 0), stop=(kt == DT - 1))
            s.inc(mm, "t_go")
            s.wait(v, "t_go", s.c["t_go"])
            s.inc(v.tensor_add(go[:, j * NCHK:(j + 1) * NCHK], po[:, 0:NCHK],
                               summ[:, j * NCHK:(j + 1) * NCHK]), "v_go")
            setg_o("v_go", s.c["v_go"])

        for ci in range(NCHK):
            cb = ci % 2
            csl = slice(ci * CH, (ci + 1) * CH)
            ob = osb[:, cb * DT * CH:(cb + 1) * DT * CH]
            if ci >= 2:
                s.wait(sy, "v_up", ci - 1)
                s.wait(sy, f"su{cb}", 16 * (ci // 2))
            s.wait(sy, f"st{cb}", 16 * (NCHK // 2))
            s.inc(sy.dma_start(out=ob.rearrange("p (kt t) -> p kt t", kt=DT),
                               in_=_fm3(s1, slice(None), csl)), f"lu{cb}", 16)
            s.wait(v, f"lu{cb}", 16 * (ci // 2 + 1))
            s.wait(v, "v_go", DT)
            if ci >= 2:
                s.wait(v, f"su{cb}", 16 * (ci // 2))
            for j in range(DT):
                v.tensor_scalar_add(ob[:, j * CH:(j + 1) * CH],
                                    ob[:, j * CH:(j + 1) * CH],
                                    go[:, j * NCHK + ci: j * NCHK + ci + 1])
            s.inc(v.drain(), "v_up")
            s.wait(sy, "v_up", ci + 1)
            s.inc(sy.dma_start(out=_fm3(s1, slice(None), csl),
                               in_=ob.rearrange("p (kt t) -> p kt t", kt=DT)),
                  f"su{cb}", 16)
        s.wait(sy, "su0", 16 * (NCHK // 2))
        s.wait(sy, "su1", 16 * (NCHK // 2))
        nc.all_engine_barrier()


def phase_shiftqkv(ctx, tag, x_l, wqkv_d, qs_d, ks_d, qw, kw, vwt):
    nc = ctx.nc
    g, v, sc, t, sy = nc.gpsimd, nc.vector, nc.scalar, nc.tensor, nc.sync
    NQ = 512
    NT = NQ // P
    eff_level = 3 if SQKV_EPI_TILES < 10**9 else SQKV_LEVEL
    with nc.cleanup_on_exit():
        s = Sems(nc, tag)
        hb = nc.alloc_sbuf_tensor(f"{tag}_h", [P, 2 * DT * TC], BF16)
        wsb = nc.alloc_sbuf_tensor(f"{tag}_w", [P, DT * 1536], BF16)
        q2sb = nc.alloc_sbuf_tensor(f"{tag}_q2", [P, 2 * TC], BF16)
        qraw = nc.alloc_sbuf_tensor(f"{tag}_qr", [P, 2 * TC], BF16)
        qnsb = nc.alloc_sbuf_tensor(f"{tag}_qn", [P, 2 * TC], BF16)
        t1sb = nc.alloc_sbuf_tensor(f"{tag}_t1", [P, TC], BF16)
        t2sb = nc.alloc_sbuf_tensor(f"{tag}_t2", [P, TC], BF16)

        qscol = nc.alloc_sbuf_tensor(f"{tag}_qs", [P, 1], F32)
        kscol = nc.alloc_sbuf_tensor(f"{tag}_ks", [P, 1], F32)
        hs2 = nc.alloc_sbuf_tensor(f"{tag}_hs", [P, 2], BF16)
        Rb = nc.alloc_sbuf_tensor(f"{tag}_R", [P, P], BF16)
        ones64 = nc.alloc_sbuf_tensor(f"{tag}_o64", [1, 64], BF16)
        na = nc.alloc_sbuf_tensor(f"{tag}_na", [1, TC], F32)
        nb = nc.alloc_sbuf_tensor(f"{tag}_nb", [1, TC], F32)
        nab = nc.alloc_sbuf_tensor(f"{tag}_nab", [1, TC], BF16)
        nbb = nc.alloc_sbuf_tensor(f"{tag}_nbb", [1, TC], BF16)
        tabs = nc.alloc_sbuf_tensor(f"{tag}_tab", [P, 4 * SEQ], F32)
        qout = nc.alloc_sbuf_tensor(f"{tag}_qo", [P, 2 * NT * TC], BF16)
        kout = nc.alloc_sbuf_tensor(f"{tag}_ko", [P, 2 * NT * TC], BF16)
        vout = nc.alloc_sbuf_tensor(f"{tag}_vo", [P, 2 * (TC // P) * NQ], BF16)
        pq = PsumPool(nc, tag + "q", 2, [P, TC])
        ptok = PsumPool(nc, tag + "t", 2, [P, NQ])
        nps_a = nc.alloc_psum_tensor(f"{tag}_npa", [1, TC], F32)
        nps_b = nc.alloc_psum_tensor(f"{tag}_npb", [1, TC], F32)
        bc_ps = nc.alloc_psum_tensor(f"{tag}_bc", [P, TC], F32)
        rot_ps = nc.alloc_psum_tensor(f"{tag}_ro", [P, TC], F32)

        s.inc(g.dma_start(out=wsb[:, :].rearrange("p (kt n) -> p kt n", kt=DT),
                          in_=wqkv_d.rearrange("(kt p) n -> p kt n", p=P)), "ldc", 16)
        s.inc(g.dma_start(out=qscol[:, :], in_=qs_d), "ldc", 16)
        s.inc(g.dma_start(out=kscol[:, :], in_=ks_d), "ldc", 16)
        s.inc(g.dma_start(out=hs2[:, :], in_=ctx.inp("headsum2", [P, 2])), "ldc", 16)
        s.inc(g.dma_start(out=Rb[:, :], in_=ctx.inp("rothalf", [P, P])), "ldc", 16)
        s.inc(g.memset(ones64[:, :], 1.0), "ldc_g")
        for ti, tn in enumerate(["cs_q", "ss_q", "cs_k", "ss_k"]):
            s.inc(g.dma_start(out=tabs[:, ti * SEQ:(ti + 1) * SEQ],
                              in_=ctx.inp(tn, [P, SEQ])), "ldc", 16)
        LDC = s.c["ldc"]

        xv = x_l.rearrange("(kt p) t -> p kt t", p=P)
        zv = ctx.inp("zeros_col", [D, 1]).rearrange("(kt p) z -> p kt z", p=P)
        jq = 0
        ne = 0
        nst = 0
        for c in range(NCH):
            b = c % 2
            tsl = slice(c * TC, (c + 1) * TC)
            toff = (c % 4) * TC
            hb_b = hb[:, b * DT * TC:(b + 1) * DT * TC]
            if c >= 2:
                s.wait(g, "t_qkv", (c - 1) * (2 * NT + (0 if SQKV_NO_V else NT)))
            s.inc(g.dma_start(out=hb_b.rearrange("p (kt t) -> p kt t", kt=DT)[:, 0:4, :],
                              in_=xv[:, 0:4, tsl]), f"lh{b}", 16)
            if c % 4 == 0:
                s.inc(g.dma_start(
                    out=hb_b.rearrange("p (kt t) -> p kt t", kt=DT)[:, 4:8, 1:TC],
                    in_=xv[:, 4:8, c * TC:(c + 1) * TC - 1]), f"lh{b}", 16)
                s.inc(g.dma_start(
                    out=hb_b.rearrange("p (kt t) -> p kt t", kt=DT)[:, 4:8, 0:1],
                    in_=zv[:, 4:8, :], allow_slow_non_contiguous=True), f"lh{b}", 16)
            else:
                s.inc(g.dma_start(
                    out=hb_b.rearrange("p (kt t) -> p kt t", kt=DT)[:, 4:8, :],
                    in_=xv[:, 4:8, c * TC - 1:(c + 1) * TC - 1]), f"lh{b}", 16)
            nlh = s.c[f"lh{b}"]

            for part in range(2):
                woff = part * NQ
                scol = [qscol, kscol][part]
                cs_t = tabs[:, (2 * part) * SEQ + toff:(2 * part) * SEQ + toff + TC]
                ss_t = tabs[:, (2 * part + 1) * SEQ + toff:(2 * part + 1) * SEQ + toff + TC]
                outb = [qout, kout][part]
                outd = [qw, kw][part]
                for j in range(NT):
                    jb = jq % 2
                    ps, setg_ps = pq.get(s, t)
                    s.wait(t, f"lh{b}", nlh)
                    s.wait(t, "ldc", LDC)
                    for kt in range(DT):
                        mm = t.matmul(ps[:, :],
                                      wsb[:, kt * 1536 + woff + j * P: kt * 1536 + woff + j * P + P],
                                      hb_b[:, kt * TC:(kt + 1) * TC],
                                      start=(kt == 0), stop=(kt == DT - 1))
                    s.inc(mm, "t_qkv")
                    if SQKV_LEVEL >= 4 or jq >= SQKV_EPI_TILES:
                        s.wait(v, "t_qkv", s.c["t_qkv"])
                        s.inc(v.tensor_copy(outb[:, b * NT * TC + j * TC:
                                                 b * NT * TC + (j + 1) * TC], ps[:, :]),
                              "v_fin")
                        setg_ps("v_fin", s.c["v_fin"])
                        jq += 1
                        continue
                    jb = ne % 2
                    s.wait(v, "t_qkv", s.c["t_qkv"])
                    if ne >= 2:
                        s.wait(v, "v_qn" if eff_level in (0, 1, 2) else "v_fin2",
                               ne - 1)
                        s.wait(v, "t_nrm", ne - 1)
                    v.tensor_copy(qraw[:, jb * TC:(jb + 1) * TC], ps[:, :])
                    s.inc(v.drain(), "v_cp")
                    s.inc(v.tensor_mul(q2sb[:, jb * TC:(jb + 1) * TC],
                                       qraw[:, jb * TC:(jb + 1) * TC],
                                       qraw[:, jb * TC:(jb + 1) * TC]), "s_sq")
                    setg_ps(("v_cp", s.c["v_cp"]),)
                    s.wait(t, "s_sq", ne + 1)
                    if ne >= 1:
                        s.wait(t, "s_rt", ne)
                    t.matmul(nps_a[:, :], hs2[:, 0:1],
                             q2sb[:, jb * TC:(jb + 1) * TC], start=True, stop=True)
                    s.inc(t.matmul(nps_b[:, :], hs2[:, 1:2],
                                   q2sb[:, jb * TC:(jb + 1) * TC],
                                   start=True, stop=True), "t_nrm")
                    s.wait(sc, "t_nrm", ne + 1)
                    if ne >= 1:
                        s.wait(sc, "v_rn", ne)
                    sc.activation(na[:, :], nps_a[:, :], AF.Sqrt)
                    s.inc(sc.activation(nb[:, :], nps_b[:, :], AF.Sqrt), "s_rt")
                    s.wait(v, "s_rt", ne + 1)
                    if ne >= 1 and eff_level in (0, 1, 2):
                        s.wait(v, "t_bc", ne)
                    v.tensor_scalar_max(na[:, :], na[:, :], 1e-12)
                    v.tensor_scalar_max(nb[:, :], nb[:, :], 1e-12)
                    v.drain()
                    v.reciprocal(na[:, :], na[:, :])
                    v.reciprocal(nb[:, :], nb[:, :])
                    v.drain()
                    v.tensor_copy(nab[:, :], na[:, :])
                    s.inc(v.tensor_copy(nbb[:, :], nb[:, :]), "v_rn")
                    if SQKV_LEVEL == 3 or SQKV_EPI_TILES < 10**9:
                        s.wait(v, "v_cp", s.c["v_cp"])
                        vc = v.tensor_copy(outb[:, b * NT * TC + j * TC:
                                                b * NT * TC + (j + 1) * TC],
                                           qraw[:, jb * TC:(jb + 1) * TC])
                        s.inc(vc, "v_fin")
                        v.drain()
                        s.inc(v.drain() if False else v.nop(), "v_fin2") if False else None
                        s.inc(v.drain(), "v_fin2")
                        jq += 1
                        ne += 1
                        continue
                    s.wait(t, "v_rn", ne + 1)
                    s.wait(t, "ldc_g", 1)
                    if ne >= 1:
                        s.wait(t, "v_qn", ne)
                    t.matmul(bc_ps[0:64, :], ones64[:, :], nab[:, :],
                             start=True, stop=True)
                    s.inc(t.matmul(bc_ps[64:128, :], ones64[:, :], nbb[:, :],
                                   start=True, stop=True, tile_position=(0, 64)), "t_bc")
                    s.wait(v, "t_bc", ne + 1)
                    s.wait(v, "v_cp", s.c["v_cp"])
                    if ne >= 2:
                        s.wait(v, "t_rot", ne - 1)
                    s.inc(v.scalar_tensor_tensor(qnsb[:, jb * TC:(jb + 1) * TC],
                                                 qraw[:, jb * TC:(jb + 1) * TC],
                                                 scol[:, 0:1], bc_ps[:, :],
                                                 ALU.mult, ALU.mult), "v_qn")
                    if SQKV_LEVEL == 2:
                        s.wait(v, "v_qn", ne + 1)
                        s.inc(v.tensor_copy(outb[:, b * NT * TC + j * TC:
                                                 b * NT * TC + (j + 1) * TC],
                                            qnsb[:, jb * TC:(jb + 1) * TC]), "v_fin")
                        jq += 1
                        continue
                    s.wait(t, "v_qn", ne + 1)
                    if ne >= 1:
                        s.wait(t, "v_fin2", ne)
                    s.inc(t.matmul(rot_ps[:, :], Rb[:, :], qnsb[:, jb * TC:(jb + 1) * TC],
                                   start=True, stop=True), "t_rot")
                    if SQKV_LEVEL == 1:
                        s.wait(v, "t_rot", ne + 1)
                        s.inc(v.tensor_copy(outb[:, b * NT * TC + j * TC:
                                                 b * NT * TC + (j + 1) * TC],
                                            qnsb[:, jb * TC:(jb + 1) * TC]), "v_fin")
                        jq += 1
                        continue
                    s.wait(v, "t_rot", ne + 1)
                    if c >= 2 and part == 0 and j == 0:
                        s.wait(v, f"st{b}", 16 * 3 * (c // 2))
                    v.tensor_mul(t1sb[:, :], rot_ps[:, :], ss_t)
                    v.drain()
                    v.tensor_mul(t2sb[:, :], qnsb[:, jb * TC:(jb + 1) * TC], cs_t)
                    v.drain()
                    s.inc(v.tensor_add(outb[:, b * NT * TC + j * TC: b * NT * TC + (j + 1) * TC],
                                       t1sb[:, :], t2sb[:, :]), "v_fin")
                    s.inc(v.drain(), "v_fin2")
                    jq += 1
                    ne += 1
            for tt in (range(0) if SQKV_NO_V else range(TC // P)):
                ps, setg_ps = ptok.get(s, t)
                s.wait(t, f"lh{b}", nlh)
                for kt in range(DT):
                    mm = t.matmul(ps[:, :],
                                  hb_b[:, kt * TC + tt * P: kt * TC + (tt + 1) * P],
                                  wsb[:, kt * 1536 + 2 * NQ: kt * 1536 + 3 * NQ],
                                  start=(kt == 0), stop=(kt == DT - 1))
                s.inc(mm, "t_qkv")
                s.wait(v, "t_qkv", s.c["t_qkv"])
                if c >= 2 and tt == 0:
                    s.wait(v, f"st{b}", 16 * 3 * (c // 2))
                s.inc(v.tensor_copy(vout[:, b * (TC // P) * NQ + tt * NQ:
                                         b * (TC // P) * NQ + (tt + 1) * NQ],
                                    ps[:, :]), "v_ev")
                setg_ps("v_ev", s.c["v_ev"])
            if SQKV_NO_STORE:
                continue
            s.wait(sy, "v_fin", jq)
            if not SQKV_NO_V:
                s.wait(sy, "v_ev", s.c["v_ev"])
            s.inc(sy.dma_start(out=_fm3(qw, slice(None), tsl),
                               in_=qout[:, b * NT * TC:(b + 1) * NT * TC]
                               .rearrange("p (nt t) -> p nt t", nt=NT)), f"st{b}", 16)
            s.inc(sy.dma_start(out=_fm3(kw, slice(None), tsl),
                               in_=kout[:, b * NT * TC:(b + 1) * NT * TC]
                               .rearrange("p (nt t) -> p nt t", nt=NT)), f"st{b}", 16)
            if not SQKV_NO_V:
                s.inc(sy.dma_start(out=vwt[tsl, :].rearrange("(tt p) n -> p tt n", p=P),
                                   in_=vout[:, b * (TC // P) * NQ:(b + 1) * (TC // P) * NQ]
                                   .rearrange("p (tt n) -> p tt n", tt=TC // P)), f"st{b}", 16)
            nst += 3
        if not SQKV_NO_STORE:
            nper = 2 if SQKV_NO_V else 3
            s.wait(sy, "st0", 16 * nper * (NCH // 2))
            s.wait(sy, "st1", 16 * nper * (NCH // 2))
        nc.all_engine_barrier()


def phase_wattn(ctx, tag, qw, kw, vwt, ctxw):
    nc = ctx.nc
    g, v, sc, t, sy = nc.gpsimd, nc.vector, nc.scalar, nc.tensor, nc.sync
    NQ = 512
    NHW = 8
    WIN = 512
    NW = SEQ // WIN
    QT = WIN // P
    NTQ = NQ // P
    with nc.cleanup_on_exit():
        s = Sems(nc, tag)
        qsb = nc.alloc_sbuf_tensor(f"{tag}_q", [P, NTQ * SEQ], BF16)
        ksb = nc.alloc_sbuf_tensor(f"{tag}_k", [P, NTQ * SEQ], BF16)
        vsb = nc.alloc_sbuf_tensor(f"{tag}_v", [P, (SEQ // P) * NQ], BF16)
        ctx_sb = nc.alloc_sbuf_tensor(f"{tag}_c", [P, NTQ * SEQ], BF16)
        smsb = nc.alloc_sbuf_tensor(f"{tag}_sm", [P, 2 * 640], F32)
        psb = nc.alloc_sbuf_tensor(f"{tag}_p", [P, 2 * 1024], BF16)
        ptsb = nc.alloc_sbuf_tensor(f"{tag}_pt", [P, 2 * 1024], BF16)
        maskb = nc.alloc_sbuf_tensor(f"{tag}_mb", [P, 640], F32)
        maskc = nc.alloc_sbuf_tensor(f"{tag}_mc", [P, P], F32)
        identb = nc.alloc_sbuf_tensor(f"{tag}_id", [P, P], BF16)
        negmax = nc.alloc_sbuf_tensor(f"{tag}_nm", [P, 2], F32)
        denom = nc.alloc_sbuf_tensor(f"{tag}_dn", [P, 2], F32)
        recip = nc.alloc_sbuf_tensor(f"{tag}_rc", [P, 1], F32)
        S_ps = nc.alloc_psum_tensor(f"{tag}_S", [P, 2048], F32)
        PT_ps = nc.alloc_psum_tensor(f"{tag}_PT", [P, 2048], BF16)
        cxA = nc.alloc_psum_tensor(f"{tag}_cxA", [P, P], F32)
        cxB = nc.alloc_psum_tensor(f"{tag}_cxB", [P, P], F32)

        s.inc(g.dma_start(out=maskb[:, :], in_=ctx.inp("mask_band", [P, 640])), "ldc", 16)
        s.inc(g.dma_start(out=maskc[:, :], in_=ctx.inp("mask_causal", [P, P])), "ldc", 16)
        s.inc(g.dma_start(out=identb[:, :], in_=ctx.inp("identity", [P, P])), "ldc", 16)
        LDC = s.c["ldc"]

        it = 0
        nst = 0
        for sq in range(NSEQ):
            ssl = slice(sq * SEQ, (sq + 1) * SEQ)
            sb = sq % 2
            if sq >= 1:
                s.wait(g, "st", 16 * nst)
            s.inc(g.dma_start(out=qsb[:, :].rearrange("p (kt t) -> p kt t", kt=NTQ),
                              in_=_fm3(qw, slice(None), ssl)), f"lq{sb}", 16)
            s.inc(g.dma_start(out=ksb[:, :].rearrange("p (kt t) -> p kt t", kt=NTQ),
                              in_=_fm3(kw, slice(None), ssl)), f"lq{sb}", 16)
            s.inc(g.dma_start(out=vsb[:, :].rearrange("p (tt n) -> p tt n", tt=SEQ // P),
                              in_=vwt[ssl, :].rearrange("(tt p) n -> p tt n", p=P)),
                  f"lq{sb}", 16)
            nlq = 48 * (sq // 2 + 1)

            def emit_S(iti, w, h, qt):
                hb = (h % 2) * 64
                hoff = (h // 2) * SEQ
                q0 = qt * P
                if w == 0:
                    kr = q0 + P
                    kbase = hoff
                else:
                    kr = 640
                    kbase = hoff + (w - 1) * WIN + q0
                off = (iti % 2) * 1024
                q_t = qsb[hb:hb + 64, hoff + w * WIN + q0: hoff + w * WIN + q0 + P]
                s.wait(t, f"lq{sb}", nlq)
                s.wait(t, "ldc", LDC)
                if iti >= 2:
                    s.wait(t, "v_sm", iti - 1)
                for kk in range(kr // P):
                    mm = t.matmul(S_ps[:, off + kk * P: off + (kk + 1) * P], q_t,
                                  ksb[hb:hb + 64, kbase + kk * P: kbase + (kk + 1) * P],
                                  start=True, stop=True)
                s.inc(mm, "t_s")
                return off, kr, kbase

            def emit_epi(iti, w, h, qt, off, kr, kbase):
                hb = (h % 2) * 64
                hoff = (h // 2) * SEQ
                q0 = qt * P
                so = (iti % 2) * 640
                nm = negmax[:, iti % 2: iti % 2 + 1]
                dn = denom[:, iti % 2: iti % 2 + 1]
                cxp = [cxA, cxB][iti % 2]
                s.wait(v, "t_s", iti + 1)
                if iti >= 2:
                    s.wait(v, "s_e", iti - 1)
                if w == 0:
                    if qt > 0:
                        v.tensor_copy(smsb[:, so: so + q0], S_ps[:, off: off + q0])
                        v.drain()
                    v.tensor_add(smsb[:, so + q0: so + kr],
                                 S_ps[:, off + q0: off + kr], maskc[:, :])
                    v.drain()
                else:
                    v.tensor_add(smsb[:, so: so + kr], S_ps[:, off: off + kr],
                                 maskb[:, :])
                    v.drain()
                s.inc(v.reduce_max(nm, smsb[:, so: so + kr], axis=AX.X,
                                   negate=True), "v_sm")
                s.wait(sc, "v_sm", iti + 1)
                if iti >= 2:
                    s.wait(sc, "t_pt", iti - 1)
                    s.wait(sc, "v_r", iti - 1)
                s.inc(sc.activation(psb[:, off: off + kr], smsb[:, so: so + kr],
                                    AF.Exp, bias=nm, accum_out=dn), "s_e")
                s.wait(v, "s_e", iti + 1)
                v.reciprocal(recip[:, 0:1], dn)
                v.drain()
                s.inc(v.tensor_scalar_mul(psb[:, off: off + kr],
                                          psb[:, off: off + kr],
                                          recip[:, 0:1]), "v_r")
                s.wait(t, "v_r", iti + 1)
                if iti >= 2:
                    s.wait(t, "s_pt", iti - 1)
                for kk in range(kr // P):
                    tp = t.transpose(PT_ps[:, off + kk * P: off + (kk + 1) * P],
                                     psb[:, off + kk * P: off + (kk + 1) * P],
                                     identb[:, :])
                s.inc(tp, "t_pt")
                s.wait(sc, "t_pt", iti + 1)
                if iti >= 2:
                    s.wait(sc, "t_av", iti - 1)
                s.inc(sc.copy(ptsb[:, off: off + kr], PT_ps[:, off: off + kr]), "s_pt")
                s.wait(t, "s_pt", iti + 1)
                if iti >= 2:
                    s.wait(t, "s_cx", iti - 1)
                for kk in range(kr // P):
                    if w == 0:
                        vrow = kk * P
                    else:
                        vrow = (w - 1) * WIN + q0 + kk * P
                    vt_i = vrow // P
                    mm = t.matmul(cxp[hb:hb + 64, :],
                                  vsb[:, vt_i * NQ + h * 64: vt_i * NQ + h * 64 + 64],
                                  ptsb[:, off + kk * P: off + (kk + 1) * P],
                                  start=(kk == 0), stop=(kk == kr // P - 1),
                                  tile_position=(0, hb))
                s.inc(mm, "t_av")
                s.wait(sc, "t_av", iti + 1)
                s.inc(sc.copy(ctx_sb[hb:hb + 64,
                                     hoff + w * WIN + q0: hoff + w * WIN + q0 + P],
                              cxp[hb:hb + 64, :]), "s_cx")

            pend = None
            for w in range(NW):
                for h in range(NHW):
                    for qt in range(QT):
                        args = emit_S(it, w, h, qt)
                        if pend is not None:
                            emit_epi(*pend)
                        pend = (it, w, h, qt) + args
                        it += 1
            if pend is not None:
                emit_epi(*pend)
                pend = None
            s.wait(sy, "s_cx", it)
            s.inc(sy.dma_start(out=_fm3(ctxw, slice(None), ssl),
                               in_=ctx_sb[:, :].rearrange("p (nt t) -> p nt t", nt=NTQ)),
                  "st", 16)
            nst += 1
        s.wait(sy, "st", 16 * nst)
        nc.all_engine_barrier()


def phase_lnshift(ctx, tag, x_mid, lg_d, lb_d, hln, eps=1e-5):
    nc = ctx.nc
    g, v, sc, t, sy = nc.gpsimd, nc.vector, nc.scalar, nc.tensor, nc.sync
    with nc.cleanup_on_exit():
        s = Sems(nc, tag)
        xf = nc.alloc_sbuf_tensor(f"{tag}_xf", [P, 2 * DT * TC], F32)
        xb = nc.alloc_sbuf_tensor(f"{tag}_xb", [P, 2 * DT * TC], BF16)
        x2 = nc.alloc_sbuf_tensor(f"{tag}_x2", [P, DT * TC], BF16)
        tmp = nc.alloc_sbuf_tensor(f"{tag}_tmp", [P, 2 * DT * TC], F32)
        onesb = nc.alloc_sbuf_tensor(f"{tag}_ones", [P, 1], BF16)
        onesf_row = nc.alloc_sbuf_tensor(f"{tag}_onesr", [1, P], F32)
        lgc = nc.alloc_sbuf_tensor(f"{tag}_lg", [P, DT], F32)
        lbc = nc.alloc_sbuf_tensor(f"{tag}_lb", [P, DT], F32)
        ssum = nc.alloc_sbuf_tensor(f"{tag}_ssum", [1, TOK], F32)
        ssq = nc.alloc_sbuf_tensor(f"{tag}_ssq", [1, TOK], F32)
        xnout = nc.alloc_sbuf_tensor(f"{tag}_xn", [P, 2 * DT * TC], BF16)
        ps_sum = nc.alloc_psum_tensor(f"{tag}_pss", [1, TC], F32)
        ps_sq = nc.alloc_psum_tensor(f"{tag}_psq", [1, TC], F32)
        ps_a = nc.alloc_psum_tensor(f"{tag}_psa", [P, TC], F32)
        ps_b = nc.alloc_psum_tensor(f"{tag}_psb", [P, TC], F32)

        s.inc(g.memset(onesb[:, :], 1.0), "g")
        s.inc(g.memset(onesf_row[:, :], 1.0), "g")
        s.inc(g.dma_start(out=lgc[:, :], in_=lg_d), "ldc", 16)
        s.inc(g.dma_start(out=lbc[:, :], in_=lb_d), "ldc", 16)

        xv = x_mid.rearrange("(kt p) t -> p kt t", p=P)
        zv = ctx.inp("zeros_col", [D, 1]).rearrange("(kt p) z -> p kt z", p=P)

        def load_shifted(eng, dst_b, c, semkey):
            tsl = slice(c * TC, (c + 1) * TC)
            s.inc(eng.dma_start(out=dst_b.rearrange("p (kt t) -> p kt t", kt=DT)[:, 0:4, :],
                                in_=xv[:, 0:4, tsl]), semkey, 16)
            if c % 4 == 0:
                s.inc(eng.dma_start(
                    out=dst_b.rearrange("p (kt t) -> p kt t", kt=DT)[:, 4:8, 1:TC],
                    in_=xv[:, 4:8, c * TC:(c + 1) * TC - 1]), semkey, 16)
                s.inc(eng.dma_start(
                    out=dst_b.rearrange("p (kt t) -> p kt t", kt=DT)[:, 4:8, 0:1],
                    in_=zv[:, 4:8, :], allow_slow_non_contiguous=True), semkey, 16)
            else:
                s.inc(eng.dma_start(
                    out=dst_b.rearrange("p (kt t) -> p kt t", kt=DT)[:, 4:8, :],
                    in_=xv[:, 4:8, c * TC - 1:(c + 1) * TC - 1]), semkey, 16)

        for c in range(NCH):
            b = c % 2
            tsl = slice(c * TC, (c + 1) * TC)
            xf_b = xf[:, b * DT * TC:(b + 1) * DT * TC]
            xb_b = xb[:, b * DT * TC:(b + 1) * DT * TC]
            if c >= 2:
                s.wait(g, "t_stats", 2 * (c - 1))
                s.wait(sy, "s_sq", c - 1)
            load_shifted(g, xb_b, c, f"ld{b}")
            load_shifted(sy, xf_b, c, f"ldf{b}")
            nld = s.c[f"ld{b}"]
            nldf = s.c[f"ldf{b}"]
            s.wait(sc, f"ldf{b}", nldf)
            if c >= 1:
                s.wait(sc, "t_stats", 2 * c)
            s.inc(sc.activation(x2[:, :], xf_b[:, :], AF.Square), "s_sq")
            s.wait(t, f"ld{b}", nld)
            s.wait(t, "g", 2)
            s.wait(t, "s_sq", c + 1)
            if c >= 1:
                s.wait(t, "v_stats", c)
            for kt in range(DT):
                t.matmul(ps_sum[:, :], onesb[:, :], xb_b[:, kt * TC:(kt + 1) * TC],
                         start=(kt == 0), stop=(kt == DT - 1))
            for kt in range(DT):
                mm = t.matmul(ps_sq[:, :], onesb[:, :], x2[:, kt * TC:(kt + 1) * TC],
                              start=(kt == 0), stop=(kt == DT - 1))
            s.inc(mm, "t_stats", 2)
            s.wait(v, "t_stats", 2 * (c + 1))
            v.tensor_copy(ssum[:, tsl], ps_sum[:, :])
            s.inc(v.tensor_copy(ssq[:, tsl], ps_sq[:, :]), "v_stats")

        trow = tmp[0:1, 0:TOK]
        s.wait(v, "v_stats", NCH)
        v.tensor_scalar_mul(ssum[:, :], ssum[:, :], 1.0 / D)
        v.drain()
        v.tensor_scalar_mul(ssq[:, :], ssq[:, :], 1.0 / D)
        v.drain()
        v.tensor_mul(trow, ssum[:, :], ssum[:, :])
        v.drain()
        v.tensor_sub(ssq[:, :], ssq[:, :], trow)
        v.drain()
        s.inc(v.tensor_scalar_add(ssq[:, :], ssq[:, :], eps), "v1")
        s.wait(sc, "v1", 1)
        s.inc(sc.activation(ssq[:, :], ssq[:, :], AF.Sqrt), "s1")
        s.wait(v, "s1", 1)
        v.reciprocal(ssq[:, :], ssq[:, :])
        v.drain()
        s.inc(v.scalar_tensor_tensor(ssum[:, :], ssum[:, :], -1.0, ssq[:, :],
                                     ALU.mult, ALU.mult), "v2")
        arow, brow = ssq, ssum

        for c in range(NCH):
            b = c % 2
            tsl = slice(c * TC, (c + 1) * TC)
            xf_b = xf[:, b * DT * TC:(b + 1) * DT * TC]
            tmp_b = tmp[:, b * DT * TC:(b + 1) * DT * TC]
            xn_b = xnout[:, b * DT * TC:(b + 1) * DT * TC]
            s.wait(sy, "s_sq", NCH)
            if c >= 2:
                s.wait(sy, "v_app", c - 1)
            load_shifted(sy, xf_b, c, f"ldf{b}")
            s.wait(t, "v2", 1)
            if c >= 1:
                s.wait(t, "v_app", c)
            t.matmul(ps_a[:, :], onesf_row[:, :], arow[:, tsl], start=True, stop=True)
            s.inc(t.matmul(ps_b[:, :], onesf_row[:, :], brow[:, tsl], start=True,
                           stop=True), "t_ab")
            s.wait(v, f"ldf{b}", s.c[f"ldf{b}"])
            s.wait(v, "t_ab", c + 1)
            s.wait(v, "ldc", 32)
            if c >= 2:
                s.wait(v, f"st{b}", 16 * (c // 2))
            for kt in range(DT):
                v.tensor_mul(tmp_b[:, kt * TC:(kt + 1) * TC],
                             xf_b[:, kt * TC:(kt + 1) * TC], ps_a[:, :])
            v.drain()
            for kt in range(DT):
                v.tensor_add(tmp_b[:, kt * TC:(kt + 1) * TC],
                             tmp_b[:, kt * TC:(kt + 1) * TC], ps_b[:, :])
            v.drain()
            for kt in range(DT):
                v.tensor_scalar(xn_b[:, kt * TC:(kt + 1) * TC],
                                tmp_b[:, kt * TC:(kt + 1) * TC],
                                lgc[:, kt:kt + 1], lbc[:, kt:kt + 1],
                                ALU.mult, ALU.add)
            s.inc(v.drain(), "v_app")
            s.wait(g, "v_app", c + 1)
            s.inc(g.dma_start(out=_fm3(hln, slice(None), tsl),
                              in_=xn_b.rearrange("p (kt t) -> p kt t", kt=DT)),
                  f"st{b}", 16)
        s.wait(g, "st0", 16 * (NCH // 2))
        s.wait(g, "st1", 16 * (NCH // 2))
        nc.all_engine_barrier()


def phase_w1gate(ctx, tag, hln, w1A_d, w1G_d, b1_d, actw):
    nc = ctx.nc
    g, v, sc, t, sy = nc.gpsimd, nc.vector, nc.scalar, nc.tensor, nc.sync
    NHID = 2816
    NT = NHID // P
    with nc.cleanup_on_exit():
        s = Sems(nc, tag)
        wA = nc.alloc_sbuf_tensor(f"{tag}_wA", [P, DT * NHID], BF16)
        wG = nc.alloc_sbuf_tensor(f"{tag}_wG", [P, DT * NHID], BF16)
        bsb = nc.alloc_sbuf_tensor(f"{tag}_b", [P, 2 * NT], F32)
        ab = nc.alloc_sbuf_tensor(f"{tag}_act", [P, 2 * DT * TC], BF16)
        asb = nc.alloc_sbuf_tensor(f"{tag}_a", [P, 2 * TC], BF16)
        gsb = nc.alloc_sbuf_tensor(f"{tag}_g", [P, 2 * TC], BF16)
        osb = nc.alloc_sbuf_tensor(f"{tag}_o", [P, 2 * NT * TC], BF16)
        ppA = PsumPool(nc, tag + "A", 2, [P, TC])
        ppG = PsumPool(nc, tag + "G", 2, [P, TC])

        s.inc(g.dma_start(out=wA[:, :].rearrange("p (kt n) -> p kt n", kt=DT),
                          in_=w1A_d.rearrange("(kt p) n -> p kt n", p=P)), "ldw", 16)
        s.inc(g.dma_start(out=wG[:, :].rearrange("p (kt n) -> p kt n", kt=DT),
                          in_=w1G_d.rearrange("(kt p) n -> p kt n", p=P)), "ldw", 16)
        s.inc(g.dma_start(out=bsb[:, :], in_=b1_d), "ldw", 16)
        LDW = s.c["ldw"]

        for c in range(NCH):
            b = c % 2
            tsl = slice(c * TC, (c + 1) * TC)
            ab_b = ab[:, b * DT * TC:(b + 1) * DT * TC]
            if c >= 2:
                s.wait(g, "t_c", 2 * NT * (c - 1))
            s.inc(g.dma_start(out=ab_b.rearrange("p (kt t) -> p kt t", kt=DT),
                              in_=_fm3(hln, slice(None), tsl)), f"lda{b}", 16)
            nlda = 16 * (c // 2 + 1)
            for j in range(NT):
                jb = j % 2
                pA, setg_A = ppA.get(s, t)
                s.wait(t, f"lda{b}", nlda)
                s.wait(t, "ldw", LDW)
                for kt in range(DT):
                    mm = t.matmul(pA[:, :], wA[:, kt * NHID + j * P: kt * NHID + j * P + P],
                                  ab_b[:, kt * TC:(kt + 1) * TC],
                                  start=(kt == 0), stop=(kt == DT - 1))
                s.inc(mm, "t_c")
                pG, setg_G = ppG.get(s, t)
                for kt in range(DT):
                    mm = t.matmul(pG[:, :], wG[:, kt * NHID + j * P: kt * NHID + j * P + P],
                                  ab_b[:, kt * TC:(kt + 1) * TC],
                                  start=(kt == 0), stop=(kt == DT - 1))
                s.inc(mm, "t_c")
                nj = s.c["t_c"] // 2
                s.wait(sc, "t_c", s.c["t_c"])
                if nj >= 3:
                    s.wait(sc, "v_g", nj - 2)
                s.inc(sc.add(asb[:, jb * TC:(jb + 1) * TC], pA[:, :], bsb[:, j:j + 1]),
                      "s_a")
                setg_A("s_a", s.c["s_a"])
                s.inc(sc.activation(gsb[:, jb * TC:(jb + 1) * TC], pG[:, :], AF.Gelu,
                                    bias=bsb[:, NT + j:NT + j + 1]), "s_g")
                setg_G("s_g", s.c["s_g"])
                s.wait(v, "s_g", nj)
                if c >= 2 and j == 0:
                    s.wait(v, f"st{b}", 16 * (c // 2))
                s.inc(v.tensor_mul(osb[:, b * NT * TC + j * TC: b * NT * TC + (j + 1) * TC],
                                   asb[:, jb * TC:(jb + 1) * TC],
                                   gsb[:, jb * TC:(jb + 1) * TC]), "v_g")
            s.wait(sy, "v_g", s.c["v_g"])
            s.inc(sy.dma_start(out=_fm3(actw, slice(None), tsl),
                               in_=osb[:, b * NT * TC:(b + 1) * NT * TC]
                               .rearrange("p (nt t) -> p nt t", nt=NT)), f"st{b}", 16)
        s.wait(sy, "st0", 16 * (NCH // 2))
        s.wait(sy, "st1", 16 * (NCH // 2))
        nc.all_engine_barrier()


PHASES = ["ln0", "qkv0", "attn0", "router", "moe", "comb0", "qkv1", "attn1",
          "f0a", "f0b", "f0c", "f0d", "f0e", "f0f",
          "f1a", "f1b", "f1c", "f1d", "f1e", "f1f", "final"]


def build_program(upto=None, dbg=()):
    nc = bass.Bass("TRN2", target_bir_lowering=False)
    ctx = Ctx(nc)
    phases = PHASES if upto is None else PHASES[:PHASES.index(upto) + 1]

    x_fm = ctx.inp("x_fm", [D, TOK])
    xn0 = ctx.scr("xn0", [D, TOK], BF16)
    xn0f = ctx.scr("xn0f", [D, TOK], F32)

    if "ln0" in phases:
        phase_ln(ctx, "ln0", x_fm, xn0, xn_f32=xn0f)

    if "qkv0" in phases:
        qkvT = ctx.inp("qkvT", [D, 3072])
        qkvb = ctx.inp("qkvb", [P, 24])
        qkvb_row = ctx.inp("qkvb_row", [1, 3072])
        qk0 = ctx.scr("qk0", [2048, TOK], BF16)
        v0t = ctx.scr("v0t", [TOK, D], BF16)
        phase_gemm(ctx, "qkv0", xn0, D, [
            dict(kind="fm", w=qkvT[:, 0:2048], N=2048, bias=qkvb[:, 0:16],
                 outs=[(qk0, "bf16")]),
            dict(kind="tok", w=qkvT[:, 2048:3072], N=1024,
                 bias_row=qkvb_row[0:1, 2048:3072], out=v0t),
        ])

    if "attn0" in phases:
        outT_d = ctx.inp("outT", [D, D])
        outb_d = ctx.inp("outb", [P, DT])
        s0 = ctx.scr("s0", [D, TOK], F32)
        phase_attn0(ctx, "attn0", ctx.scratch["qk0"], ctx.scratch["v0t"],
                    outT_d, outb_d, s0)

    if "router" in phases:
        wrT_d = ctx.inp("wrT", [D, 8])
        gates_d = ctx.scr("gates_d", [8, TOK], F32)
        phase_router(ctx, "router", xn0f, wrT_d, gates_d)

    if "moe" in phases:
        w1T_d = ctx.inp("w1T", [8, D, 4096])
        w3T_d = ctx.inp("w3T", [8, D, 4096])
        w2T_d = ctx.inp("w2T", [8, 4096, D])
        g0 = ctx.scr("g0", [D, TOK], F32)
        phase_moe(ctx, "moe", xn0, w1T_d, w3T_d, w2T_d,
                  ctx.scratch["gates_d"], g0)

    if "comb0" in phases:
        x1 = ctx.scr("x1", [D, TOK], F32)
        xn1h = ctx.scr("xn1h", [D, TOK], BF16)
        xn1f = ctx.scr("xn1f", [D, TOK], F32)
        phase_combine(ctx, "comb0", x_fm, ctx.scratch["s0"], g0, x1)
        phase_ln(ctx, "ln1", x1, xn1h, xn_f32=xn1f)

    if "qkv1" in phases:
        q1l = ctx.scr("q1l", [D, TOK], BF16)
        k1l = ctx.scr("k1l", [D, TOK], BF16)
        v1lt = ctx.scr("v1lt", [TOK, D], BF16)
        phase_gemm(ctx, "qkv1", ctx.scratch["xn1h"], D, [
            dict(kind="fm", w=ctx.inp("wql", [D, D]), N=D, outs=[(q1l, "bf16")]),
            dict(kind="fm", w=ctx.inp("wkl", [D, D]), N=D, outs=[(k1l, "bf16")]),
            dict(kind="tok", w=ctx.inp("wvl", [D, D]), N=D, out=v1lt),
        ])

    if "attn1" in phases:
        s1 = ctx.scr("s1", [D, TOK], F32)
        phase_attn1(ctx, "attn1", ctx.scratch["q1l"], ctx.scratch["k1l"],
                    ctx.scratch["v1lt"], ctx.scratch["xn1f"],
                    ctx.inp("wol", [D, D]), ctx.inp("wqg", [D, D]),
                    ctx.inp("wkg", [D, D]), ctx.inp("wvg", [D, D]),
                    ctx.inp("wog", [D, D]), s1)

    if any(p.startswith("f0") or p.startswith("f1") for p in phases):
        wqkv_d = ctx.inp("wqkv1", [2, D, 1536])
        wo1_d = ctx.inp("wo1", [2, 512, D])
        w1p_d = ctx.inp("w1p", [2, D, 5632])
        b1p_d = ctx.inp("b1p", [2, P, 44])
        w2p_d = ctx.inp("w2p", [2, 2816, D])
        b2p_d = ctx.inp("b2p", [2, P, DT])
        qs_d = ctx.inp("qs", [2, P, 1])
        ks_d = ctx.inp("ks", [2, P, 1])
        lng_d = ctx.inp("lng", [2, P, DT])
        lnb_d = ctx.inp("lnb", [2, P, DT])
        x_l = ctx.scratch["xn1f"]
        for l in range(2):
            qw = ctx.scr(f"f{l}_qw", [512, TOK], BF16)
            kw = ctx.scr(f"f{l}_kw", [512, TOK], BF16)
            vwt = ctx.scr(f"f{l}_vwt", [TOK, 512], BF16)
            ctxw = ctx.scr(f"f{l}_ctx", [512, TOK], BF16)
            xi_mid = ctx.scr(f"f{l}_xm", [D, TOK], F32)
            hln = ctx.scr(f"f{l}_hln", [D, TOK], BF16)
            actw = ctx.scr(f"f{l}_act", [2816, TOK], BF16)
            xi_next = ctx.scr("g1" if l == 1 else f"f{l}_xn", [D, TOK], F32)
            if f"f{l}a" in phases:
                phase_shiftqkv(ctx, f"f{l}a", x_l, wqkv_d[l], qs_d[l], ks_d[l],
                               qw, kw, vwt)
            if f"f{l}b" in phases:
                phase_wattn(ctx, f"f{l}b", qw, kw, vwt, ctxw)
            if f"f{l}c" in phases:
                phase_gemm(ctx, f"f{l}c", ctxw, 512, [
                    dict(kind="fm", w=wo1_d[l], N=D, res=x_l,
                         outs=[(xi_mid, "f32")]),
                ])
            if f"f{l}d" in phases:
                phase_lnshift(ctx, f"f{l}d", xi_mid, lng_d[l], lnb_d[l], hln)
            if f"f{l}e" in phases:
                phase_w1gate(ctx, f"f{l}e", hln, w1p_d[l][:, 0:2816],
                             w1p_d[l][:, 2816:5632], b1p_d[l], actw)
            if f"f{l}f" in phases:
                phase_gemm(ctx, f"f{l}f", actw, 2816, [
                    dict(kind="fm", w=w2p_d[l], N=D, bias=b2p_d[l], res=xi_mid,
                         outs=[(xi_next, "f32")]),
                ])
            x_l = xi_next

    if "final" in phases:
        out_fm = ctx.out("out_fm", [D, TOK], F32)
        phase_combine(ctx, "final", ctx.scratch["x1"], ctx.scratch["s1"],
                      ctx.scratch["g1"], out_fm)

    for name in dbg:
        src = ctx.scratch[name]
        dst = ctx.out("dbg_" + name, list(src.shape), src.dtype)
        sem = nc.alloc_semaphore(f"dbgcp_{name}")
        nc.sync.dma_start(out=dst[:, :], in_=src[:, :]).then_inc(sem, 16)
        nc.sync.wait_ge(sem, 16)
    nc.all_engine_barrier()
    return nc, ctx


def prep_in_maps(inputs):
    shared = _prep_shared(inputs)
    x = np.asarray(inputs["L0_attn_x"], dtype=np.float32)
    in_maps = []
    for c in range(N_CORES):
        xs = x[4 * c:4 * c + 4]
        x_fm = np.ascontiguousarray(xs.transpose(2, 0, 1).reshape(D, TOK))
        m = dict(shared)
        m["x_fm"] = x_fm
        in_maps.append(m)
    return in_maps


def kernel(**inputs):
    nc, ctx = build_program()
    in_maps = prep_in_maps(inputs)
    used = set(ctx.din.keys())
    in_maps = [{k: v for k, v in m.items() if k in used} for m in in_maps]
    res = run_bass_kernel_spmd(nc, in_maps, core_ids=list(range(N_CORES)))
    outs = []
    for c in range(N_CORES):
        o = res.results[c]["out_fm"]
        outs.append(o.reshape(D, NSEQ, SEQ).transpose(1, 2, 0))
    return np.concatenate(outs, axis=0).astype(np.float32)
